# version 18
# baseline (speedup 1.0000x reference)
import contextlib
import numpy as np
import ml_dtypes
import concourse.bass as bass
import concourse.mybir as mybir
from concourse.bass_utils import run_bass_kernel_spmd

F32 = mybir.dt.float32
BF16 = mybir.dt.bfloat16
AF = mybir.ActivationFunctionType
ALU = mybir.AluOpType
AX = mybir.AxisListType.X
BF = ml_dtypes.bfloat16

S = 4096
DM = 1024
FH = 4096
TH = 2048
EPS = 1e-7
MAGIC = 12582912.0
KLN = float(1.0 + np.pi / 2.0)
C_SW1, C_SW2, C_ISW1, C_ISW2, C_SX, C_ISX, C_LOX, C_HIX = range(8)
C_S1T, C_SH, C_ISH, C_LOH, C_HIH, C_S2T, C_AL, C_BE = range(8, 16)


class Sem:
    def __init__(self, stack, nc, name):
        self.h = stack.enter_context(nc.semaphore(name))
        self.n = 0

    def inc(self, instr, amt=16):
        instr.then_inc(self.h, amt)
        self.n += amt
        return instr

    def wait(self, eng, n=None):
        eng.wait_ge(self.h, self.n if n is None else n)


def build_program(debug=()):
    nc = bass.Bass("TRN2", target_bir_lowering=False)
    st = contextlib.ExitStack()

    def dk(name):
        return "ExternalOutput" if name in debug else "Internal"

    xb = nc.dram_tensor("xb", [S, DM], F32, kind="ExternalInput").ap()
    xh = nc.dram_tensor("xh", [TH, DM], F32, kind="ExternalInput").ap()
    qkvw = nc.dram_tensor("qkvw", [3 * 512, DM], F32, kind="ExternalInput").ap()
    qkvb = nc.dram_tensor("qkvb", [128, 12], F32, kind="ExternalInput").ap()
    qkvbv = nc.dram_tensor("qkvbv", [1, 512], F32, kind="ExternalInput").ap()
    outw = nc.dram_tensor("outw", [DM, DM], F32, kind="ExternalInput").ap()
    outb = nc.dram_tensor("outb", [128, 8], F32, kind="ExternalInput").ap()
    w1 = nc.dram_tensor("w1", [FH, DM], F32, kind="ExternalInput").ap()
    b1 = nc.dram_tensor("b1", [128, 32], F32, kind="ExternalInput").ap()
    w2 = nc.dram_tensor("w2", [DM, FH], F32, kind="ExternalInput").ap()
    b2 = nc.dram_tensor("b2", [128, 8], F32, kind="ExternalInput").ap()
    idbf = nc.dram_tensor("idbf", [128, 128], BF16, kind="ExternalInput").ap()
    idf32 = nc.dram_tensor("idf32", [128, 128], F32, kind="ExternalInput").ap()
    ones2 = nc.dram_tensor("ones2", [128, 2], F32, kind="ExternalInput").ap()
    ones2t = nc.dram_tensor("ones2t", [2, 128], F32, kind="ExternalInput").ap()
    hsel = nc.dram_tensor("hsel", [1, 2], F32, kind="ExternalInput").ap()
    y = nc.dram_tensor("y", [TH, DM], F32, kind="ExternalOutput").ap()

    xnT_h = nc.dram_tensor("xnT_h", [DM, S], BF16, kind=dk("xnT_h")).ap()
    h_h = nc.dram_tensor("h_h", [FH, TH], F32, kind=dk("h_h")).ap()
    hq_h = nc.dram_tensor("hq_h", [FH, TH], BF16, kind=dk("hq_h")).ap()
    w2qT_h = nc.dram_tensor("w2qT_h", [8 * FH, 128], BF16, kind=dk("w2qT_h")).ap()
    am_in = nc.dram_tensor("am_in", [512, S], BF16).ap()
    am_out = nc.dram_tensor("am_out", [DM, S], BF16).ap()
    amdbg = (nc.dram_tensor("amdbg", [512, S], BF16, kind="ExternalOutput").ap()
             if "amdbg" in debug else None)
    qkdbg = (nc.dram_tensor("qkdbg", [256, S], BF16, kind="ExternalOutput").ap()
             if "qkdbg" in debug else None)
    stdbg = (nc.dram_tensor("stdbg", [512, S], BF16, kind="ExternalOutput").ap()
             if "qkdbg" in debug else None)
    csdbg = (nc.dram_tensor("csdbg", [128, 16], F32, kind="ExternalOutput").ap()
             if "qkdbg" in debug else None)
    vxdbg = (nc.dram_tensor("vxdbg", [128, 4 * 68], BF16, kind="ExternalOutput").ap()
             if "qkdbg" in debug else None)
    nmdbg = (nc.dram_tensor("nmdbg", [65, S], F32, kind="ExternalOutput").ap()
             if "qkdbg" in debug else None)
    cc1i = nc.dram_tensor("cc1i", [1, 8], F32).ap()
    cc1o = nc.dram_tensor("cc1o", [1, 8], F32).ap()
    cc2i = nc.dram_tensor("cc2i", [1, 8], F32).ap()
    cc2o = nc.dram_tensor("cc2o", [1, 8], F32).ap()
    scdbg = nc.dram_tensor("scdbg", [3, 16], F32, kind=dk("scdbg")).ap()

    def sbuf(name, shape, dtype):
        return st.enter_context(nc.sbuf_tensor(name, shape, dtype))

    ident_b = sbuf("ident_b", [128, 128], BF16)
    ident_f = sbuf("ident_f", [128, 128], F32)
    ones2_s = sbuf("ones2_s", [128, 2], F32)
    ones2t_s = sbuf("ones2t_s", [2, 128], F32)
    onesrow = sbuf("onesrow", [1, 128], F32)
    onesrow_b = sbuf("onesrow_b", [1, 128], BF16)
    onesr64 = sbuf("onesr64", [65, 64], F32)
    bias1 = sbuf("bias1", [128, 32], F32)
    bias2 = sbuf("bias2", [128, 8], F32)
    biaso = sbuf("biaso", [128, 8], F32)
    biasqkv = sbuf("biasqkv", [128, 12], F32)
    bv32 = sbuf("bv32", [1, 512], F32)
    bvrow = sbuf("bvrow", [1, 512], BF16)
    spp = sbuf("spp", [128, 16], F32)
    scal = sbuf("scal", [1, 16], F32)
    ccs = sbuf("ccs", [1, 8], F32)
    ccr = sbuf("ccr", [1, 8], F32)
    stats = sbuf("stats", [128, 80], F32)
    wstat = sbuf("wstat", [128, 40], F32)
    sc1a = sbuf("sc1a", [128, 1], F32)
    sc1b = sbuf("sc1b", [128, 1], F32)
    fsc = sbuf("fsc", [1, 2], F32)
    ctile = sbuf("ctile", [128, 3], F32)
    nc.const_aps.register(F32, 1.0, ctile[:, 0:1])
    nc.const_aps.register(F32, KLN, ctile[:, 1:2])
    nc.const_aps.register(F32, 13.862943611198906, ctile[:, 2:3])
    xin = [sbuf(f"xin{i}", [128, DM], F32) for i in range(2)]

    sems = {}
    for nm in ("ld", "sp", "vv", "ss", "tt", "gg", "c1", "c2", "c3", "yy", "pe"):
        sems[nm] = Sem(st, nc, nm)
    ld, sp, vv, ss, tt, gg = (sems[k] for k in ("ld", "sp", "vv", "ss", "tt", "gg"))
    c1, c2, c3, yy, pe = (sems[k] for k in ("c1", "c2", "c3", "yy", "pe"))

    blk = st.enter_context(nc.Block())

    def SY(f): blk.sync(f)
    def VE(f): blk.vector(f)
    def SC(f): blk.scalar(f)
    def TE(f): blk.tensor(f)
    def GP(f): blk.gpsimd(f)

    def _(g):
        g.memset(onesrow[:], 1.0)
        g.memset(onesrow_b[:], 1.0)
        g.memset(onesr64[:], 1.0)
        g.memset(ctile[:, 0:1], 1.0)
        g.memset(ctile[:, 1:2], KLN)
        g.memset(ctile[:, 2:3], 13.862943611198906)
        g.memset(fsc[:], 0.0)
        g.memset(scal[:], 0.0)
        gg.inc(g.memset(ccs[:], 0.0), 1)
    GP(_)

    def _(e):
        for dst, src in ((ident_b, idbf), (ident_f, idf32), (ones2_s, ones2),
                         (ones2t_s, ones2t), (bias1, b1), (bias2, b2),
                         (biaso, outb), (biasqkv, qkvb), (bv32, qkvbv)):
            ld.inc(e.dma_start(out=dst[:], in_=src[:]))
        ld.inc(e.dma_start(out=scal[0:1, C_AL:C_BE + 1], in_=hsel[:]))
    SY(_)

    def _(e):
        ld.wait(e)
        gg.wait(e)
        vv.inc(e.tensor_copy(out=bvrow[:], in_=bv32[:]), 1)
    VE(_)

    ph01 = contextlib.ExitStack()

    def sb01(name, shape, dtype):
        return ph01.enter_context(nc.sbuf_tensor(name, shape, dtype))

    def ps01(name, shape, dtype=F32):
        return ph01.enter_context(nc.psum_tensor(name, shape, dtype))

    xnf = sb01("xnf", [128, DM], F32)
    sqt = sb01("sqt", [128, DM], F32)
    xnb = sb01("xnb", [128, DM], BF16)
    stg = sb01("stg", [128, DM], BF16)
    xpose = ps01("xpose", [128, 1024], BF16)
    pmm1 = ps01("pmm1", [1, 128], F32)
    pmm2 = ps01("pmm2", [1, 128], F32)
    pbc = ps01("pbc", [128, 16], F32)

    xnT_w = xnT_h.rearrange("(a p) t -> p a t", p=128)

    ln_tiny_mk = [0]

    def emit_ln_stats_pair(tc0, src_ap, marks):
        def _(e):
            if len(marks) >= 1:
                e.wait_ge(vv.h, marks[-1])
            else:
                e.wait_ge(vv.h, vv.n)
            for tc in (tc0, tc0 + 1):
                ld.inc(e.dma_start(out=xin[tc % 2][:],
                                   in_=src_ap[tc * 128:(tc + 1) * 128, :]))
        SY(_)

        def _(e):
            ld.wait(e)
            e.wait_ge(vv.h, ln_tiny_mk[0])
            for i, tc in enumerate((tc0, tc0 + 1)):
                c0 = 72 * i
                e.activation(out=sqt[:], in_=xin[tc % 2][:], func=AF.Copy,
                             accum_out=stats[:, c0:c0 + 1])
                e.activation(out=sqt[:], in_=xin[tc % 2][:], func=AF.Square,
                             accum_out=stats[:, c0 + 1:c0 + 2])
            ss.inc(e.copy(out=fsc[0:1, 0:1], in_=fsc[0:1, 1:2]), 1)
        SC(_)
        stats_mk = ss.n

        def _(e, stats_mk=stats_mk):
            e.wait_ge(ss.h, stats_mk)

            def step(ins):
                vv.inc(ins, 1)
                vv.wait(e)
            for i in range(2):
                c0 = 72 * i
                e.tensor_scalar(out=stats[:, c0 + 2:c0 + 3],
                                in0=stats[:, c0:c0 + 1],
                                scalar1=1.0 / DM, scalar2=None, op0=ALU.mult)
                step(e.tensor_scalar(out=stats[:, c0 + 3:c0 + 4],
                                     in0=stats[:, c0 + 1:c0 + 2],
                                     scalar1=1.0 / DM, scalar2=1e-5,
                                     op0=ALU.mult, op1=ALU.add))
                step(e.scalar_tensor_tensor(out=stats[:, c0 + 4:c0 + 5],
                                            in0=stats[:, c0 + 2:c0 + 3],
                                            scalar=0.0,
                                            in1=stats[:, c0 + 2:c0 + 3],
                                            op0=ALU.add, op1=ALU.mult))
                step(e.tensor_sub(stats[:, c0 + 4:c0 + 5],
                                  stats[:, c0 + 3:c0 + 4],
                                  stats[:, c0 + 4:c0 + 5]))
            vv.inc(e.tensor_copy(out=fsc[0:1, 0:1], in_=fsc[0:1, 1:2]), 1)
        VE(_)
        ln_tiny_mk[0] = vv.n

        def _(e):
            vv.wait(e)
            e.activation(out=stats[:, 5:6], in_=stats[:, 4:5], func=AF.Ln)
            ss.inc(e.activation(out=stats[:, 77:78], in_=stats[:, 76:77],
                                func=AF.Ln), 1)
            ss.wait(e)
            e.activation(out=stats[:, 6:7], in_=stats[:, 5:6],
                         func=AF.Exp, scale=-0.5)
            e.activation(out=stats[:, 78:79], in_=stats[:, 77:78],
                         func=AF.Exp, scale=-0.5)
            ss.inc(e.copy(out=fsc[0:1, 0:1], in_=fsc[0:1, 1:2]), 1)
        SC(_)

    def emit_ln_chunk(tc, src_ap, do_spill, marks):
        c0 = 72 * (tc % 2)

        def _(e, c0=c0):
            ss.wait(e)
            xt = xin[tc % 2]
            e.tensor_scalar(out=xnf[:], in0=xt[:],
                            scalar1=stats[:, c0 + 2:c0 + 3],
                            scalar2=stats[:, c0 + 6:c0 + 7], op0=ALU.subtract,
                            op1=ALU.mult)
            if do_spill:
                e.tensor_reduce(out=stats[:, 8 + tc:9 + tc], in_=xnf[:],
                                axis=AX, op=ALU.max)
                e.tensor_reduce(out=stats[:, 40 + tc:41 + tc], in_=xnf[:],
                                axis=AX, op=ALU.min)
                e.wait_ge(sp.h, sp.n)
                e.wait_ge(tt.h, tt.n)
                e.tensor_copy(out=xnb[:], in_=xnf[:])
            vv.inc(e.tensor_copy(out=sqt[:, 0:1], in_=xnf[:, 0:1]), 1)
        VE(_)
        marks.append(vv.n)

        if do_spill:
            def _(e):
                vv.wait(e)
                last = None
                for dc in range(8):
                    last = e.transpose(xpose[:, dc * 128:(dc + 1) * 128],
                                       xnb[:, dc * 128:(dc + 1) * 128],
                                       ident_b[:])
                tt.inc(last, 1)
            TE(_)

            def _(e):
                tt.wait(e)
                vv.inc(e.tensor_copy(out=stg[:], in_=xpose[:]), 1)
            VE(_)
            marks[-1] = vv.n

            def _(e, tc=tc):
                vv.wait(e)
                sp.inc(e.dma_start(
                    out=xnT_w[:, :, tc * 128:(tc + 1) * 128],
                    in_=stg[:].rearrange("p (a q) -> p a q", q=128)))
            SY(_)

    ln_marks0 = []
    for tc0 in range(0, 32, 2):
        emit_ln_stats_pair(tc0, xb, ln_marks0)
        emit_ln_chunk(tc0, xb, do_spill=True, marks=ln_marks0)
        emit_ln_chunk(tc0 + 1, xb, do_spill=True, marks=ln_marks0)

    wld = [sb01(f"wld{i}", [128, FH], F32) for i in range(2)]
    w_marks = []

    def emit_w_absmax(gi, src_ap, cols):
        def _(g):
            if len(w_marks) >= 2:
                g.wait_ge(vv.h, w_marks[-2])
            gg.inc(g.dma_start(out=wld[gi % 2][0:128, 0:cols], in_=src_ap))
        GP(_)
        wld_mk = gg.n

        def _(e, wld_mk=wld_mk):
            e.wait_ge(gg.h, wld_mk)
            vv.inc(e.tensor_reduce(out=wstat[:, gi:gi + 1],
                                   in_=wld[gi % 2][0:128, 0:cols], axis=AX,
                                   op=ALU.max, apply_absolute_value=True), 1)
        VE(_)
        w_marks.append(vv.n)

    for ci in range(32):
        emit_w_absmax(ci, w1[ci * 128:(ci + 1) * 128, :], DM)
    for ci in range(8):
        emit_w_absmax(32 + ci, w2[ci * 128:(ci + 1) * 128, :], FH)

    def _(e):
        e.tensor_reduce(out=sc1a[:], in_=stats[:, 8:40], axis=AX, op=ALU.max)
        vv.inc(e.tensor_reduce(out=sc1b[:], in_=stats[:, 40:72], axis=AX,
                               op=ALU.min), 1)
    VE(_)

    def _(e):
        vv.wait(e)
        e.transpose(pmm1[:], sc1a[:], ident_f[:])
        tt.inc(e.transpose(pmm2[:], sc1b[:], ident_f[:]), 1)
    TE(_)

    def _(e):
        tt.wait(e)
        e.tensor_reduce(out=ccs[0:1, 1:2], in_=pmm1[:], axis=AX, op=ALU.max)
        vv.inc(e.tensor_reduce(out=ccs[0:1, 2:3], in_=pmm2[:], axis=AX,
                               op=ALU.min), 1)
        vv.wait(e)
        e.tensor_scalar(out=ccs[0:1, 0:1], in0=ccs[0:1, 2:3],
                        scalar1=-1.0, scalar2=None, op0=ALU.mult)
        vv.inc(e.tensor_copy(out=fsc[0:1, 0:1], in_=fsc[0:1, 1:2]), 1)
    VE(_)

    def _(g):
        vv.wait(g)
        gg.inc(g.dma_start(out=cc1i[:, :], in_=ccs[0:1, :]))
        gg.wait(g)
        c1.inc(g.collective_compute(
            "AllReduce", ALU.max, replica_groups=[list(range(8))],
            ins=[cc1i.opt()], outs=[cc1o.opt()]), 1)
        g.wait_ge(c1.h, 1)
        gg.inc(g.dma_start(out=ccr[0:1, :], in_=cc1o[:, :]))
    GP(_)

    def _(e):
        e.wait_ge(tt.h, tt.n)
        e.tensor_reduce(out=sc1a[:], in_=wstat[:, 0:32], axis=AX, op=ALU.max)
        vv.inc(e.tensor_reduce(out=sc1b[:], in_=wstat[:, 32:40], axis=AX,
                               op=ALU.max), 1)
    VE(_)

    def _(e):
        vv.wait(e)
        e.transpose(pmm1[:], sc1a[:], ident_f[:])
        tt.inc(e.transpose(pmm2[:], sc1b[:], ident_f[:]), 1)
    TE(_)

    def _(e):
        tt.wait(e)
        e.tensor_reduce(out=scal[0:1, C_SW1:C_SW1 + 1], in_=pmm1[:],
                        axis=AX, op=ALU.max)
        vv.inc(e.tensor_reduce(out=scal[0:1, C_SW2:C_SW2 + 1], in_=pmm2[:],
                               axis=AX, op=ALU.max), 1)
        vv.wait(e)
        e.tensor_scalar(out=scal[0:1, C_SW1:C_SW2 + 1],
                        in0=scal[0:1, C_SW1:C_SW2 + 1],
                        scalar1=1.0 / 127.0, scalar2=1e-8,
                        op0=ALU.mult, op1=ALU.max)
        vv.inc(e.tensor_copy(out=fsc[0:1, 0:1], in_=fsc[0:1, 1:2]), 1)
    VE(_)

    def _(e):
        vv.wait(e)
        ss.inc(e.activation(out=scal[0:1, C_ISW1:C_ISW2 + 1],
                            in_=scal[0:1, C_SW1:C_SW2 + 1], func=AF.Ln), 1)
        ss.wait(e)
        e.activation(out=scal[0:1, C_ISW1:C_ISW2 + 1],
                     in_=scal[0:1, C_ISW1:C_ISW2 + 1],
                     func=AF.Exp, scale=-1.0)
        ss.inc(e.copy(out=fsc[0:1, 0:1], in_=fsc[0:1, 1:2]), 1)
    SC(_)

    def bcast_spp(pbc_t):
        def _(e):
            ss.wait(e)
            vv.wait(e)
            ld.wait(e)
            tt.inc(e.matmul(pbc_t[:], onesrow[:], scal[:],
                            start=True, stop=True), 1)
        TE(_)

        def _(e):
            tt.wait(e)
            e.tensor_copy(out=spp[:], in_=pbc_t[:])
            vv.inc(e.tensor_copy(out=fsc[0:1, 0:1], in_=fsc[0:1, 1:2]), 1)
        VE(_)

    bcast_spp(pbc)

    wqb = sb01("wqb", [128, FH], BF16)
    w1qT = sb01("w1qT", [128, 8 * FH], BF16)
    w2st = sb01("w2st", [128, 4096], BF16)
    w1qT_w = w1qT[:].rearrange("p (a h) -> p a h", a=8)
    w2qT_w = w2qT_h.rearrange("(a p) q -> p a q", p=128)

    def emit_w_quant(gi, src_ap, cols, iswcol, is_w2, ci):
        def _(e):
            if len(w_marks) >= 2:
                e.wait_ge(vv.h, w_marks[-2])
            ld.inc(e.dma_start(out=wld[gi % 2][0:128, 0:cols], in_=src_ap))
        SY(_)

        def _(e):
            ld.wait(e)
            e.wait_ge(tt.h, tt.n)
            wt = wld[gi % 2][0:128, 0:cols]
            e.tensor_scalar(out=wt, in0=wt, scalar1=spp[:, iswcol:iswcol + 1],
                            scalar2=MAGIC, op0=ALU.mult, op1=ALU.add)
            e.tensor_scalar(out=wt, in0=wt, scalar1=MAGIC, scalar2=-127.0,
                            op0=ALU.subtract, op1=ALU.max)
            vv.inc(e.tensor_scalar(out=wqb[0:128, 0:cols], in0=wt,
                                   scalar1=127.0, scalar2=None, op0=ALU.min), 1)
        VE(_)
        w_marks.append(vv.n)

        for g0 in range(0, cols // 128, 8):
            def _(e, g0=g0):
                vv.wait(e)
                last = None
                for k in range(8):
                    last = e.transpose(xpose[:, k * 128:(k + 1) * 128],
                                       wqb[:, (g0 + k) * 128:(g0 + k + 1) * 128],
                                       ident_b[:])
                tt.inc(last, 1)
            TE(_)

            if not is_w2:
                def _(e, ci=ci):
                    tt.wait(e)
                    vv.inc(e.tensor_copy(
                        out=w1qT_w[:, :, ci * 128:(ci + 1) * 128],
                        in_=xpose[:].rearrange("p (a q) -> p a q", q=128)), 1)
                VE(_)
            else:
                def _(e, g0=g0):
                    tt.wait(e)
                    if g0 == 0:
                        e.wait_ge(sp.h, sp.n)
                    vv.inc(e.tensor_copy(out=w2st[:, g0 * 128:(g0 + 8) * 128],
                                         in_=xpose[:]), 1)
                VE(_)
        if is_w2:
            def _(e, ci=ci):
                vv.wait(e)
                sp.inc(e.dma_start(
                    out=w2qT_w[:, ci * 32:(ci + 1) * 32, :],
                    in_=w2st[:].rearrange("p (a q) -> p a q", q=128)))
            SY(_)

    for ci in range(32):
        emit_w_quant(32 + 8 + ci, w1[ci * 128:(ci + 1) * 128, :], DM,
                     C_ISW1, False, ci)
    for ci in range(8):
        emit_w_quant(32 + 8 + 32 + ci, w2[ci * 128:(ci + 1) * 128, :], FH,
                     C_ISW2, True, ci)

    def _(e):
        gg.wait(e)
        vv.inc(e.tensor_add(scal[0:1, C_SX:C_SX + 1], ccr[0:1, 0:1],
                            ccr[0:1, 1:2]), 1)
        vv.wait(e)
        vv.inc(e.tensor_scalar(out=scal[0:1, C_SX:C_SX + 1],
                               in0=scal[0:1, C_SX:C_SX + 1],
                               scalar1=1.0 / 255.0, scalar2=1e-8,
                               op0=ALU.mult, op1=ALU.max), 1)
    VE(_)

    def _(e):
        vv.wait(e)
        ss.inc(e.activation(out=scal[0:1, C_ISX:C_ISX + 1],
                            in_=scal[0:1, C_SX:C_SX + 1], func=AF.Ln), 1)
        ss.wait(e)
        e.activation(out=scal[0:1, C_ISX:C_ISX + 1],
                     in_=scal[0:1, C_ISX:C_ISX + 1],
                     func=AF.Exp, scale=-1.0)
        ss.inc(e.copy(out=fsc[0:1, 0:1], in_=fsc[0:1, 1:2]), 1)
    SC(_)

    def emit_zx(lo_col, hi_col, s_col, isx_col, stot_col, sw_col):
        def _(e):
            ss.wait(e)

            def step(ins):
                vv.inc(ins, 1)
                vv.wait(e)
            step(e.tensor_mul(scal[0:1, lo_col:lo_col + 1], ccr[0:1, 0:1],
                              scal[0:1, isx_col:isx_col + 1]))
            step(e.tensor_scalar(out=scal[0:1, lo_col:lo_col + 1],
                                 in0=scal[0:1, lo_col:lo_col + 1],
                                 scalar1=MAGIC, scalar2=MAGIC,
                                 op0=ALU.add, op1=ALU.subtract))
            step(e.tensor_scalar(out=scal[0:1, lo_col:lo_col + 1],
                                 in0=scal[0:1, lo_col:lo_col + 1],
                                 scalar1=-128.0, scalar2=-128.0,
                                 op0=ALU.add, op1=ALU.max))
            step(e.tensor_scalar(out=scal[0:1, lo_col:lo_col + 1],
                                 in0=scal[0:1, lo_col:lo_col + 1],
                                 scalar1=127.0, scalar2=None, op0=ALU.min))
            step(e.tensor_scalar(out=scal[0:1, hi_col:hi_col + 1],
                                 in0=scal[0:1, lo_col:lo_col + 1],
                                 scalar1=-1.0, scalar2=127.0,
                                 op0=ALU.mult, op1=ALU.add))
            step(e.tensor_scalar(out=scal[0:1, lo_col:lo_col + 1],
                                 in0=scal[0:1, lo_col:lo_col + 1],
                                 scalar1=-1.0, scalar2=-128.0,
                                 op0=ALU.mult, op1=ALU.add))
            e.tensor_mul(scal[0:1, stot_col:stot_col + 1],
                         scal[0:1, s_col:s_col + 1],
                         scal[0:1, sw_col:sw_col + 1])
            vv.inc(e.tensor_copy(out=fsc[0:1, 0:1], in_=fsc[0:1, 1:2]), 1)
        VE(_)

    emit_zx(C_LOX, C_HIX, C_SX, C_ISX, C_S1T, C_SW1)
    bcast_spp(pbc)

    def _(e):
        vv.wait(e)
        sp.inc(e.dma_start(out=scdbg[0:1, :], in_=scal[0:1, :]))
        sp.inc(e.dma_start(out=scdbg[1:2, 0:8], in_=ccs[0:1, :]))
        sp.inc(e.dma_start(out=scdbg[2:3, 0:8], in_=ccr[0:1, :]))
    SY(_)

    xqT = sb01("xqT", [128, 8 * TH], BF16)
    xqb = sb01("xqb", [128, DM], BF16)
    xqT_w = xqT[:].rearrange("p (a t) -> p a t", a=8)
    ln_marks1 = []
    for tc in range(16):
        if tc % 2 == 0:
            emit_ln_stats_pair(tc, xh, ln_marks1)
        emit_ln_chunk(tc, xh, do_spill=False, marks=ln_marks1)

        def _(e, tc=tc):
            e.wait_ge(tt.h, tt.n)
            e.tensor_scalar(out=xnf[:], in0=xnf[:],
                            scalar1=spp[:, C_ISX:C_ISX + 1], scalar2=MAGIC,
                            op0=ALU.mult, op1=ALU.add)
            e.tensor_scalar(out=xnf[:], in0=xnf[:], scalar1=MAGIC,
                            scalar2=spp[:, C_LOX:C_LOX + 1],
                            op0=ALU.subtract, op1=ALU.max)
            vv.inc(e.tensor_scalar(out=xqb[:], in0=xnf[:],
                                   scalar1=spp[:, C_HIX:C_HIX + 1],
                                   scalar2=None, op0=ALU.min), 1)
        VE(_)

        def _(e, tc=tc):
            vv.wait(e)
            last = None
            for dc in range(8):
                last = e.transpose(xpose[:, dc * 128:(dc + 1) * 128],
                                   xqb[:, dc * 128:(dc + 1) * 128], ident_b[:])
            tt.inc(last, 1)
        TE(_)

        def _(e, tc=tc):
            tt.wait(e)
            vv.inc(e.tensor_copy(
                out=xqT_w[:, :, tc * 128:(tc + 1) * 128],
                in_=xpose[:].rearrange("p (a q) -> p a q", q=128)), 1)
        VE(_)
        ln_marks1[-1] = vv.n

    hst = sb01("hst", [128, TH], F32)
    m1p = [ps01(f"m1p{i}", [128, 512], F32) for i in range(2)]
    mm1_ready = vv.n

    for hc in range(32):
        for tp in range(4):
            pp = m1p[(hc * 4 + tp) % 2]

            def _(e, hc=hc, tp=tp, pp=pp):
                e.wait_ge(vv.h, mm1_ready)
                e.wait_ge(ss.h, ss.n)
                last = None
                for dc in range(8):
                    last = e.matmul(
                        pp[:],
                        w1qT[:, dc * FH + hc * 128:dc * FH + (hc + 1) * 128],
                        xqT[:, dc * TH + tp * 512:dc * TH + (tp + 1) * 512],
                        start=(dc == 0), stop=(dc == 7))
                pe.inc(last, 1)
            TE(_)

            def _(e, hc=hc, tp=tp, pp=pp):
                pe.wait(e)
                if tp == 0:
                    e.wait_ge(sp.h, sp.n)
                    e.wait_ge(vv.h, vv.n)
                ss.inc(e.activation(out=hst[:, tp * 512:(tp + 1) * 512],
                                    in_=pp[:], func=AF.Gelu,
                                    scale=spp[:, C_S1T:C_S1T + 1],
                                    bias=bias1[:, hc:hc + 1]), 1)
            SC(_)

        def _(e, hc=hc):
            ss.wait(e)
            e.tensor_reduce(out=stats[:, 8 + hc:9 + hc], in_=hst[:], axis=AX,
                            op=ALU.max)
            vv.inc(e.tensor_reduce(out=stats[:, 40 + hc:41 + hc], in_=hst[:],
                                   axis=AX, op=ALU.min), 1)
        VE(_)

        def _(e, hc=hc):
            vv.wait(e)
            sp.inc(e.dma_start(out=h_h[hc * 128:(hc + 1) * 128, :], in_=hst[:]))
        SY(_)

    def _(e):
        e.tensor_reduce(out=sc1a[:], in_=stats[:, 8:40], axis=AX, op=ALU.max)
        vv.inc(e.tensor_reduce(out=sc1b[:], in_=stats[:, 40:72], axis=AX,
                               op=ALU.min), 1)
    VE(_)

    def _(e):
        vv.wait(e)
        e.transpose(pmm1[:], sc1a[:], ident_f[:])
        tt.inc(e.transpose(pmm2[:], sc1b[:], ident_f[:]), 1)
    TE(_)

    def _(e):
        tt.wait(e)
        e.tensor_reduce(out=ccs[0:1, 1:2], in_=pmm1[:], axis=AX, op=ALU.max)
        vv.inc(e.tensor_reduce(out=ccs[0:1, 2:3], in_=pmm2[:], axis=AX,
                               op=ALU.min), 1)
        vv.wait(e)
        e.tensor_scalar(out=ccs[0:1, 0:1], in0=ccs[0:1, 2:3],
                        scalar1=-1.0, scalar2=None, op0=ALU.mult)
        vv.inc(e.tensor_copy(out=fsc[0:1, 0:1], in_=fsc[0:1, 1:2]), 1)
    VE(_)

    def _(g):
        vv.wait(g)
        gg.inc(g.dma_start(out=cc2i[:, :], in_=ccs[0:1, :]))
        gg.wait(g)
        c2.inc(g.collective_compute(
            "AllReduce", ALU.max, replica_groups=[list(range(8))],
            ins=[cc2i.opt()], outs=[cc2o.opt()]), 1)
        g.wait_ge(c2.h, 1)
        gg.inc(g.dma_start(out=ccr[0:1, :], in_=cc2o[:, :]))
    GP(_)
    ccr2_mk = gg.n
    h_done_mk = sp.n

    ph01.close()

    ph2 = contextlib.ExitStack()

    def sb2(name, shape, dtype):
        return ph2.enter_context(nc.sbuf_tensor(name, shape, dtype))

    scoreT = sb2("scoreT", [128, 4 * S], BF16)
    qhat = sb2("qhat", [128, S], BF16)
    khat = sb2("khat", [128, S], BF16)
    vtok = sb2("vtok", [128, 32 * 128], BF16)
    vext = sb2("vext", [128, 4 * 68], BF16)
    numer = sb2("numer", [65, S], F32)
    colss = sb2("colss", [128, 8], F32)
    coltl = sb2("coltl", [128, 4], F32)
    rnq = sb2("rnq", [2, TH], F32)
    atst = [sb2(f"atst{i}", [64, 512], BF16) for i in range(2)]

    hstq = [sb2(f"hstq{i}", [128, 1024], F32) for i in range(2)]
    hqtq = sb2("hqtq", [128, 1024], BF16)
    hq_lmk = {}
    hq_qmk = {}
    hq_smk = {}
    hq_state = [0, 0]

    def emit_hq_load(j):
        hc, t2 = j // 2, j % 2

        def _(e, j=j, hc=hc, t2=t2):
            e.wait_ge(sp.h, h_done_mk)
            if j >= 2:
                e.wait_ge(vv.h, hq_qmk[j - 2])
            ld.inc(e.dma_start(out=hstq[j % 2][:],
                               in_=h_h[hc * 128:(hc + 1) * 128,
                                       t2 * 1024:(t2 + 1) * 1024]))
            hq_lmk[j] = ld.n
        SY(_)

    def emit_hq_quant(j):
        def _(e, j=j):
            e.wait_ge(ld.h, hq_lmk[j])
            e.wait_ge(sp.h, hq_smk.get(j - 1, 0))
            hs = hstq[j % 2]
            e.tensor_scalar(out=hs[:], in0=hs[:],
                            scalar1=spp[:, C_ISH:C_ISH + 1], scalar2=MAGIC,
                            op0=ALU.mult, op1=ALU.add)
            e.tensor_scalar(out=hs[:], in0=hs[:], scalar1=MAGIC,
                            scalar2=spp[:, C_LOH:C_LOH + 1],
                            op0=ALU.subtract, op1=ALU.max)
            vv.inc(e.tensor_scalar(out=hqtq[:], in0=hs[:],
                                   scalar1=spp[:, C_HIH:C_HIH + 1],
                                   scalar2=None, op0=ALU.min), 1)
            hq_qmk[j] = vv.n
        VE(_)

    def emit_hq_spill(j):
        hc, t2 = j // 2, j % 2

        def _(e, j=j, hc=hc, t2=t2):
            e.wait_ge(vv.h, hq_qmk[j])
            sp.inc(e.dma_start(out=hq_h[hc * 128:(hc + 1) * 128,
                                        t2 * 1024:(t2 + 1) * 1024],
                               in_=hqtq[:]))
            hq_smk[j] = sp.n
        SY(_)

    def emit_hq_step():
        j, g = hq_state
        if g % 3 == 0 and j < 64:
            emit_hq_load(j)
        elif g % 3 == 2 and j < 64:
            emit_hq_quant(j)
            emit_hq_spill(j)
            hq_state[0] = j + 1
        hq_state[1] = g + 1

    for p in range(4):
        pstart_vv, pstart_ss = vv.n, ss.n
        prj = contextlib.ExitStack()
        xnp = prj.enter_context(nc.sbuf_tensor(f"xnp{p}", [128, 8 * TH], BF16))
        wqkT = prj.enter_context(nc.sbuf_tensor(f"wqkT{p}", [128, 3 * 1024], BF16))
        qf = prj.enter_context(nc.sbuf_tensor(f"qf{p}", [128, TH], F32))
        kf = prj.enter_context(nc.sbuf_tensor(f"kf{p}", [128, TH], F32))
        sqp = prj.enter_context(nc.sbuf_tensor(f"sqp{p}", [128, 512], F32))
        wl32 = prj.enter_context(nc.sbuf_tensor(f"wl32{p}", [128, DM], F32))
        wlb = prj.enter_context(nc.sbuf_tensor(f"wlb{p}", [128, DM], BF16))
        xp2 = prj.enter_context(nc.psum_tensor(f"xp2{p}", [128, 1024], BF16))
        qp = [prj.enter_context(nc.psum_tensor(f"qp{p}{i}", [128, 512], F32))
              for i in range(2)]
        npp = prj.enter_context(nc.psum_tensor(f"npp{p}", [2, 512], F32))
        bcp = prj.enter_context(nc.psum_tensor(f"bcp{p}", [128, 512], F32))
        vp = [prj.enter_context(nc.psum_tensor(f"vp{p}{i}", [128, 512], F32))
              for i in range(2)]

        if p == 1:
            pbch = prj.enter_context(nc.psum_tensor("pbch", [128, 16], F32))

            def _(e):
                e.wait_ge(gg.h, ccr2_mk)
                vv.inc(e.tensor_add(scal[0:1, C_SH:C_SH + 1], ccr[0:1, 0:1],
                                    ccr[0:1, 1:2]), 1)
                vv.wait(e)
                vv.inc(e.tensor_scalar(out=scal[0:1, C_SH:C_SH + 1],
                                       in0=scal[0:1, C_SH:C_SH + 1],
                                       scalar1=1.0 / 255.0, scalar2=1e-8,
                                       op0=ALU.mult, op1=ALU.max), 1)
            VE(_)

            def _(e):
                vv.wait(e)
                ss.inc(e.activation(out=scal[0:1, C_ISH:C_ISH + 1],
                                    in_=scal[0:1, C_SH:C_SH + 1],
                                    func=AF.Ln), 1)
                ss.wait(e)
                e.activation(out=scal[0:1, C_ISH:C_ISH + 1],
                             in_=scal[0:1, C_ISH:C_ISH + 1],
                             func=AF.Exp, scale=-1.0)
                ss.inc(e.copy(out=fsc[0:1, 0:1], in_=fsc[0:1, 1:2]), 1)
            SC(_)

            emit_zx(C_LOH, C_HIH, C_SH, C_ISH, C_S2T, C_SW2)
            bcast_spp(pbch)

        for sec in range(3):
            def _(e, sec=sec, p=p):
                e.wait_ge(vv.h, vv.n)
                e.wait_ge(ss.h, ss.n)
                e.wait_ge(gg.h, gg.n)
                e.wait_ge(pe.h, pe.n)
                ld.inc(e.dma_start(
                    out=wl32[:],
                    in_=qkvw[sec * 512 + p * 128: sec * 512 + (p + 1) * 128, :]))
            SY(_)

            def _(e, sec=sec):
                ld.wait(e)
                e.wait_ge(tt.h, tt.n)
                vv.inc(e.tensor_copy(out=wlb[:], in_=wl32[:]), 1)
            VE(_)

            def _(e, sec=sec):
                vv.wait(e)
                last = None
                for dc in range(8):
                    last = e.transpose(xp2[:, dc * 128:(dc + 1) * 128],
                                       wlb[:, dc * 128:(dc + 1) * 128],
                                       ident_b[:])
                tt.inc(last, 1)
            TE(_)

            def _(e, sec=sec):
                tt.wait(e)
                vv.inc(e.tensor_copy(out=wqkT[:, sec * 1024:(sec + 1) * 1024],
                                     in_=xp2[:]), 1)
            VE(_)

        wq_mk = vv.n
        qp_mk = [pstart_vv, pstart_vv]
        vp_mk = [pstart_vv, pstart_vv]
        xnp_gmk = [0, 0]

        def emit_norm1(srcf):
            for tp in range(4):
                def _(e, tp=tp, srcf=srcf):
                    e.wait_ge(tt.h, tt.n)
                    qs = srcf[:, tp * 512:(tp + 1) * 512]
                    vv.inc(e.scalar_tensor_tensor(out=sqp[:], in0=qs,
                                                  scalar=0.0, in1=qs,
                                                  op0=ALU.add,
                                                  op1=ALU.mult), 1)
                VE(_)

                def _(e, tp=tp):
                    vv.wait(e)
                    tt.inc(e.matmul(npp[:], ones2_s[:], sqp[:],
                                    start=True, stop=True), 1)
                TE(_)

                def _(e, tp=tp):
                    tt.wait(e)
                    vv.inc(e.tensor_copy(
                        out=rnq[:, tp * 512:(tp + 1) * 512],
                        in_=npp[:]), 1)
                VE(_)

        def emit_norm2(srcf, dstt, t0c):
            def _(e):
                vv.inc(e.tensor_scalar(
                    out=rnq[:], in0=rnq[:],
                    scalar1=1e-24, scalar2=None, op0=ALU.max), 1)
            VE(_)

            def _(e):
                vv.wait(e)
                ss.inc(e.activation(out=rnq[:], in_=rnq[:], func=AF.Ln), 1)
                ss.wait(e)
                e.activation(out=rnq[:], in_=rnq[:], func=AF.Exp,
                             scale=-0.5)
                ss.inc(e.copy(out=fsc[0:1, 0:1], in_=fsc[0:1, 1:2]), 1)
            SC(_)

            for tp in range(4):
                def _(e, tp=tp):
                    ss.wait(e)
                    e.wait_ge(vv.h, vv.n)
                    tt.inc(e.matmul(
                        bcp[:], ones2t_s[:],
                        rnq[:, tp * 512:(tp + 1) * 512],
                        start=True, stop=True), 1)
                TE(_)

                def _(e, tp=tp, dstt=dstt, srcf=srcf, t0c=t0c):
                    tt.wait(e)
                    vv.inc(e.tensor_mul(
                        dstt[:, t0c + tp * 512:t0c + (tp + 1) * 512],
                        srcf[:, tp * 512:(tp + 1) * 512], bcp[:]), 1)
                VE(_)

        for th in range(2):
            t0c = th * TH
            for dc in range(4):
                def _(e, dc=dc, th=th):
                    sp.wait(e)
                    if dc == 0 and th == 1:
                        e.wait_ge(pe.h, pe.n)
                    ld.inc(e.dma_start(
                        out=xnp[:, dc * TH:(dc + 1) * TH],
                        in_=xnT_h[dc * 128:(dc + 1) * 128,
                                  th * TH:(th + 1) * TH]))
                SY(_)
            xnp_lmk = ld.n

            def _(g, th=th, pen=pe.n, pvv=pstart_vv, pss=pstart_ss):
                g.wait_ge(vv.h, pvv)
                g.wait_ge(ss.h, pss)
                g.wait_ge(pe.h, pen)
                for dc in range(4, 8):
                    gg.inc(g.dma_start(
                        out=xnp[:, dc * TH:(dc + 1) * TH],
                        in_=xnT_h[dc * 128:(dc + 1) * 128,
                                  th * TH:(th + 1) * TH]))
            GP(_)
            xnp_gmk[th] = gg.n

            for sec, dstf in ((0, qf), (1, kf)):
                for tp in range(4):
                    pp = qp[tp % 2]

                    def _(e, sec=sec, tp=tp, pp=pp, th=th,
                          pmk=qp_mk[tp % 2], xl=xnp_lmk):
                        e.wait_ge(ld.h, xl)
                        e.wait_ge(gg.h, xnp_gmk[th])
                        e.wait_ge(vv.h, max(pmk, wq_mk))
                        last = None
                        for dc in range(8):
                            last = e.matmul(
                                pp[:],
                                wqkT[:, sec * 1024 + dc * 128:
                                     sec * 1024 + (dc + 1) * 128],
                                xnp[:, dc * TH + tp * 512:dc * TH + (tp + 1) * 512],
                                start=(dc == 0), stop=(dc == 7))
                        pe.inc(last, 1)
                    TE(_)

                    def _(e, sec=sec, tp=tp, pp=pp, p=p, dstf=dstf):
                        pe.wait(e)
                        vv.inc(e.tensor_scalar(
                            out=dstf[:, tp * 512:(tp + 1) * 512], in0=pp[:],
                            scalar1=biasqkv[:, sec * 4 + p:sec * 4 + p + 1],
                            scalar2=None, op0=ALU.add), 1)
                    VE(_)
                    qp_mk[tp % 2] = vv.n

            emit_norm1(qf)

            for g4 in range(4):
                pp = vp[g4 % 2]

                def _(e, g4=g4, pp=pp, p=p, th=th,
                      pmk=vp_mk[g4 % 2], xl=xnp_lmk):
                    e.wait_ge(vv.h, pmk)
                    e.wait_ge(ld.h, xl)
                    e.wait_ge(gg.h, xnp_gmk[th])
                    last = None
                    for k in range(4):
                        tcl = g4 * 4 + k
                        for dc in range(8):
                            e.matmul(pp[:, k * 128:(k + 1) * 128],
                                     xnp[:, dc * TH + tcl * 128:
                                         dc * TH + (tcl + 1) * 128],
                                     wqkT[:, 2 * 1024 + dc * 128:
                                          2 * 1024 + (dc + 1) * 128],
                                     start=(dc == 0), stop=False)
                        last = e.matmul(pp[:, k * 128:(k + 1) * 128],
                                        onesrow_b[:],
                                        bvrow[0:1, p * 128:(p + 1) * 128],
                                        start=False, stop=True)
                    pe.inc(last, 1)
                TE(_)

                def _(e, g4=g4, pp=pp, th=th):
                    pe.wait(e)
                    vv.inc(e.tensor_copy(
                        out=vtok[:, th * 2048 + g4 * 512:
                                 th * 2048 + (g4 + 1) * 512],
                        in_=pp[:]), 1)
                VE(_)
                vp_mk[g4 % 2] = vv.n

            emit_norm2(qf, qhat, t0c)
            emit_norm1(kf)
            emit_norm2(kf, khat, t0c)

        prj.close()

        if qkdbg is not None and p == 0:
            def _(e):
                vv.wait(e)
                sp.inc(e.dma_start(out=qkdbg[0:128, :], in_=qhat[:]))
                sp.inc(e.dma_start(out=qkdbg[128:256, :], in_=khat[:]))
            SY(_)

        stp = contextlib.ExitStack()
        chA = [stp.enter_context(nc.sbuf_tensor(f"chA{p}{s}", [128, 4096], F32))
               for s in range(2)]
        chB = [stp.enter_context(nc.sbuf_tensor(f"chB{p}{s}", [128, 4096], F32))
               for s in range(2)]
        chC = [stp.enter_context(nc.sbuf_tensor(f"chC{p}{s}", [128, 4096], F32))
               for s in range(2)]
        cosp = [stp.enter_context(nc.psum_tensor(f"cosp{p}{s}", [128, 1024], F32))
                for s in range(2)]
        pvp = stp.enter_context(nc.psum_tensor(f"pvp{p}", [65, 2048], F32))

        clampmk = {}
        mmk = {}
        act1mk = {}
        xmk = {}
        act2mk = {}
        vextmk = {}
        evacmk = {}
        pvmk = {}
        lastpv = pe.n
        pvp_free = vv.n
        lastcolread = vv.n
        lastbcast = tt.n
        lastdenom = ss.n
        proj_ready = vv.n

        NCH = 64

        def emit_cos(n):
            stx_, j4_ = (n % 32) // 4, n % 4
            q0_ = 64 * (n // 32)
            s = n % 2
            for r in range(4):
                def _(e, n=n, r=r, stx=stx_, j4=j4_, q0=q0_):
                    prev = (n, r - 2) if r >= 2 else (n - 1, r + 2)
                    e.wait_ge(vv.h, max(clampmk.get(prev, 0), proj_ready))
                    last = None
                    for h2 in range(2):
                        i0_ = r * 1024 + h2 * 512
                        last = e.matmul(
                            cosp[r % 2][:, h2 * 512:(h2 + 1) * 512],
                            khat[q0:q0 + 64,
                                 stx * 512 + j4 * 128:stx * 512 + (j4 + 1) * 128],
                            qhat[q0:q0 + 64, i0_:i0_ + 512],
                            start=True, stop=True)
                    pe.inc(last, 1)
                    pvmk[("cos", n, r)] = pe.n
                TE(_)

                def _(e, n=n, r=r, s=s):
                    e.wait_ge(pe.h, pvmk[("cos", n, r)])
                    e.wait_ge(gg.h, xmk.get(n - 2, 0))
                    vv.inc(e.tensor_scalar(
                        out=chA[s][:, r * 1024:(r + 1) * 1024],
                        in0=cosp[r % 2][:],
                        scalar1=-(1.0 - EPS), scalar2=(1.0 - EPS),
                        op0=ALU.max, op1=ALU.min), 1)
                    clampmk[(n, r)] = vv.n
                VE(_)

        def emit_m(n):
            s = n % 2

            def _(g, n=n, s=s):
                g.wait_ge(vv.h, clampmk[(n, 3)])
                gg.inc(g.tensor_tensor(out=chB[s][:], in0=chA[s][:],
                                       in1=chA[s][:], op=ALU.mult), 1)
                mmk[n] = gg.n
            GP(_)

        def emit_act1(n):
            s = n % 2

            def _(e, n=n, s=s):
                e.wait_ge(gg.h, mmk[n])
                e.activation(out=chB[s][:], in_=chB[s][:], func=AF.Ln,
                             scale=-1.0, bias=1.0)
                ss.inc(e.activation(out=chB[s][:], in_=chB[s][:],
                                    func=AF.Sigmoid, scale=-0.5), 1)
                act1mk[n] = ss.n
            SC(_)

        def emit_x(n):
            s = n % 2

            def _(g, n=n, s=s):
                g.wait_ge(ss.h, max(act1mk[n], act2mk.get(n - 2, 0)))
                gg.inc(g.tensor_tensor(out=chC[s][:], in0=chA[s][:],
                                       in1=chB[s][:], op=ALU.mult), 1)
                xmk[n] = gg.n
            GP(_)

        def emit_act2(n):
            s = n % 2
            j4_ = n % 4

            def _(e, n=n, s=s, j4=j4_, lpv=None):
                e.wait_ge(gg.h, xmk[n])
                e.wait_ge(vv.h, lastcolread)
                e.wait_ge(pe.h, lastpv)
                e.activation(out=chC[s][:], in_=chC[s][:], func=AF.Arctan)
                e.activation(out=chC[s][:], in_=chC[s][:], func=AF.Ln,
                             scale=-2.0, bias=KLN)
                e.activation(out=scoreT[:, j4 * S:(j4 + 1) * S],
                             in_=chC[s][:], func=AF.Exp, scale=-65.0,
                             accum_out=colss[:, j4:j4 + 1])
                ss.inc(e.copy(out=fsc[0:1, 0:1], in_=fsc[0:1, 1:2]), 1)
                act2mk[n] = ss.n
            SC(_)

        def emit_tail(st_g):
            nonlocal_vars = None
            stx_ = st_g % 8
            q0_ = 64 * (st_g // 8)

            def _(e, st_g=st_g):
                e.wait_ge(ss.h, act2mk[st_g * 4 + 3])
                vv.inc(e.tensor_scalar(out=colss[:, 4:8], in0=colss[:, 0:4],
                                       scalar1=1e-25, scalar2=None,
                                       op0=ALU.max), 1)
            VE(_)
            ret_lastcolread = vv.n

            def _(e, st_g=st_g):
                vv.wait(e)
                ss.inc(e.activation(out=colss[:, 4:8], in_=colss[:, 4:8],
                                    func=AF.Ln, scale=float(2.0 ** 40)), 1)
                ss.wait(e)
                e.activation(out=coltl[:], in_=colss[:, 4:8],
                             func=AF.Exp, scale=-0.5,
                             bias=13.862943611198906)
                ss.inc(e.copy(out=fsc[0:1, 0:1], in_=fsc[0:1, 1:2]), 1)
            SC(_)
            coltl_mk = ss.n

            def _(e, stx=stx_, q0=q0_, coltl_mk=coltl_mk):
                e.wait_ge(ss.h, coltl_mk)
                e.wait_ge(pe.h, lastpv)
                for j4 in range(4):
                    e.tensor_copy(out=vext[:, j4 * 68 + 64:j4 * 68 + 65],
                                  in_=coltl[:, j4:j4 + 1])
                for j4 in range(4):
                    tc = stx * 4 + j4
                    e.tensor_scalar(
                        out=vext[:, j4 * 68:j4 * 68 + 64],
                        in0=vtok[:, tc * 128 + q0:tc * 128 + q0 + 64],
                        scalar1=coltl[:, j4:j4 + 1], scalar2=None,
                        op0=ALU.mult)
                vv.inc(e.tensor_copy(out=fsc[0:1, 0:1], in_=fsc[0:1, 1:2]), 1)
            VE(_)
            vextmk[st_g] = vv.n
            return ret_lastcolread

        def emit_pv(st_g):
            nonlocal pvp_free
            stx_ = st_g % 8
            for ipc in range(2):
                def _(e, ipc=ipc, st_g=st_g, pf=pvp_free):
                    e.wait_ge(vv.h, max(vextmk[st_g], pf))
                    e.wait_ge(ss.h, act2mk[st_g * 4 + 3])
                    last = None
                    for j4 in range(4):
                        for n5 in range(4):
                            c0 = j4 * S + ipc * 2048 + n5 * 512
                            last = e.matmul(
                                pvp[:, n5 * 512:(n5 + 1) * 512],
                                vext[:, j4 * 68:j4 * 68 + 65],
                                scoreT[:, c0:c0 + 512],
                                start=(j4 == 0), stop=(j4 == 3))
                    pe.inc(last, 1)
                    pvmk[(st_g, ipc)] = pe.n
                TE(_)

                def _(e, ipc=ipc, st_g=st_g, stx=stx_, lb=lastbcast, ld_=lastdenom):
                    e.wait_ge(pe.h, pvmk[(st_g, ipc)])
                    if stx == 0:
                        e.wait_ge(tt.h, lb)
                        e.wait_ge(ss.h, ld_)
                    dst = numer[:, ipc * 2048:(ipc + 1) * 2048]
                    if stx == 0:
                        ins = e.tensor_copy(out=dst, in_=pvp[:])
                    else:
                        ins = e.tensor_add(dst, dst, pvp[:])
                    vv.inc(ins, 1)
                    evacmk[(st_g, ipc)] = vv.n
                VE(_)
                pvp_free = vv.n

        def emit_epilogue(hh, p):
            nonlocal pvp_free, lastdenom, lastbcast

            def _(e, hh=hh):
                e.wait_ge(vv.h, max(evacmk[(hh * 8 + 7, 0)],
                                    evacmk[(hh * 8 + 7, 1)]))
                ss.inc(e.activation(out=numer[64:65, :], in_=numer[64:65, :],
                                    func=AF.Ln), 1)
                ss.wait(e)
                e.activation(out=numer[64:65, :], in_=numer[64:65, :],
                             func=AF.Exp, scale=-1.0)
                ss.inc(e.copy(out=fsc[0:1, 0:1], in_=fsc[0:1, 1:2]), 1)
            SC(_)
            lastdenom = ss.n

            for ip in range(8):
                def _(e, ip=ip, pf=pvp_free, hh=hh):
                    e.wait_ge(ss.h, lastdenom)
                    e.wait_ge(vv.h, max(pf, evacmk[(hh * 8 + 7, 0)],
                                        evacmk[(hh * 8 + 7, 1)]))
                    tt.inc(e.matmul(pvp[0:64, 0:512], onesr64[64:65, :],
                                    numer[64:65, ip * 512:(ip + 1) * 512],
                                    start=True, stop=True), 1)
                TE(_)
                bc_mk = tt.n

                def _(e, ip=ip, bc_mk=bc_mk):
                    e.wait_ge(tt.h, bc_mk)
                    if ip >= 2:
                        e.wait_ge(sp.h, sp.n)
                    e.tensor_mul(atst[ip % 2][:],
                                 numer[0:64, ip * 512:(ip + 1) * 512],
                                 pvp[0:64, 0:512])
                    vv.inc(e.tensor_copy(out=fsc[0:1, 0:1],
                                         in_=fsc[0:1, 1:2]), 1)
                VE(_)
                pvp_free = vv.n

                def _(e, ip=ip, p=p, hh=hh):
                    vv.wait(e)
                    r0 = (p * 2 + hh) * 64
                    sp.inc(e.dma_start(
                        out=am_in[r0:r0 + 64, ip * 512:(ip + 1) * 512],
                        in_=atst[ip % 2][:]))
                    if amdbg is not None:
                        sp.inc(e.dma_start(
                            out=amdbg[r0:r0 + 64, ip * 512:(ip + 1) * 512],
                            in_=atst[ip % 2][:]))
                SY(_)
            lastbcast = tt.n

        emit_cos(0); emit_m(0)
        emit_cos(1); emit_m(1)
        emit_act1(0)
        for n in range(NCH):
            if n + 1 < NCH:
                emit_act1(n + 1)
            emit_x(n)
            emit_act2(n)
            if n + 2 < NCH:
                emit_cos(n + 2)
                emit_m(n + 2)
            if n % 4 == 3:
                st_g = n // 4
                lastcolread = emit_tail(st_g)
                emit_pv(st_g)
                lastpv = pe.n
                if st_g % 8 == 7:
                    emit_epilogue(st_g // 8, p)
            if p >= 1:
                emit_hq_step()

        stp.close()

    ph2.close()

    def _(g):
        sp.wait(g)
        c3.inc(g.collective_compute(
            "AllGather", ALU.bypass,
            replica_groups=[[0, 1], [2, 3], [4, 5], [6, 7]],
            ins=[am_in.opt()], outs=[am_out.opt()]), 1)
        g.wait_ge(c3.h, 1)
        gg.inc(g.engine_nop(), 1)
    GP(_)
    ag_mk = gg.n

    ph46 = contextlib.ExitStack()

    def sb46(name, shape, dtype):
        return ph46.enter_context(nc.sbuf_tensor(name, shape, dtype))

    def ps46(name, shape, dtype=F32):
        return ph46.enter_context(nc.psum_tensor(name, shape, dtype))

    outacc = sb46("outacc", [128, 8 * TH], F32)
    outwT = sb46("outwT", [128, 8 * 1024], BF16)
    biassum = sb46("biassum", [128, 8], F32)
    fps = ps46("fps", [128, 1024], F32)
    m2p = [ps46(f"m2p{i}", [128, 512]) for i in range(2)]
    outwT_w = outwT[:].rearrange("p (a o) -> p a o", a=8)

    def _(e):
        e.wait_ge(sp.h, sp.n)
        vv.inc(e.tensor_add(biassum[:], biaso[:], bias2[:]), 1)
        vv.wait(e)
        vv.inc(e.tensor_copy(out=fsc[0:1, 0:1], in_=fsc[0:1, 1:2]), 1)
    VE(_)

    ph4a = contextlib.ExitStack()
    wl4 = ph4a.enter_context(nc.sbuf_tensor("wl4", [128, DM], F32))
    wlb4 = ph4a.enter_context(nc.sbuf_tensor("wlb4", [128, DM], BF16))
    xp4 = ph4a.enter_context(nc.psum_tensor("xp4", [128, 1024], BF16))

    for oc in range(8):
        def _(e, oc=oc):
            e.wait_ge(vv.h, vv.n)
            if oc == 0:
                e.wait_ge(sp.h, sp.n)
            ld.inc(e.dma_start(out=wl4[:], in_=outw[oc * 128:(oc + 1) * 128, :]))
        SY(_)

        def _(e, oc=oc):
            ld.wait(e)
            e.wait_ge(tt.h, tt.n)
            vv.inc(e.tensor_copy(out=wlb4[:], in_=wl4[:]), 1)
        VE(_)

        def _(e, oc=oc):
            vv.wait(e)
            last = None
            for dc in range(8):
                last = e.transpose(xp4[:, dc * 128:(dc + 1) * 128],
                                   wlb4[:, dc * 128:(dc + 1) * 128], ident_b[:])
            tt.inc(last, 1)
        TE(_)

        def _(e, oc=oc):
            tt.wait(e)
            vv.inc(e.tensor_copy(
                out=outwT_w[:, :, oc * 128:(oc + 1) * 128],
                in_=xp4[:].rearrange("p (a q) -> p a q", q=128)), 1)
        VE(_)

    ph4a.close()

    w2s_r = w2qT_h.rearrange("(a p) q -> p a q", p=128)

    ph5b = contextlib.ExitStack()
    w2all = ph5b.enter_context(nc.sbuf_tensor("w2all", [128, 256 * 128], BF16))
    hqc = ph5b.enter_context(nc.sbuf_tensor("hqc", [128, 32 * 512], BF16))
    w2all_w = w2all[:].rearrange("p (a q) -> p a q", q=128)

    def _(e):
        e.wait_ge(vv.h, vv.n)
        e.wait_ge(pe.h, pe.n)
        sp.wait(e)
        for g in range(4):
            ld.inc(e.dma_start(out=w2all_w[:, g * 64:(g + 1) * 64, :],
                               in_=w2s_r[:, g * 64:(g + 1) * 64, :]))
    SY(_)

    hm1 = {}
    hm2 = {}
    for tp in range(4):
        for hf in range(2):
            def _(e, tp=tp, hf=hf):
                if tp == 0:
                    sp.wait(e)
                else:
                    mk = hm1 if hf == 0 else hm2
                    e.wait_ge(pe.h, mk[(tp - 1, 7)])
                for hc in range(hf * 16, hf * 16 + 16):
                    ld.inc(e.dma_start(
                        out=hqc[:, hc * 512:(hc + 1) * 512],
                        in_=hq_h[hc * 128:(hc + 1) * 128,
                                 tp * 512:(tp + 1) * 512]))
            SY(_)
            if hf == 0:
                hq_ld1 = ld.n
            else:
                hq_ld2 = ld.n

        for oc in range(8):
            pp = m2p[oc % 2]

            def _(e, tp=tp, oc=oc, pp=pp, l1=hq_ld1, l2=hq_ld2):
                e.wait_ge(ld.h, l1)
                e.wait_ge(vv.h, vv.n)
                last = None
                for hc in range(16):
                    last = e.matmul(
                        pp[:], w2all[:, (oc * 32 + hc) * 128:
                                     (oc * 32 + hc + 1) * 128],
                        hqc[:, hc * 512:(hc + 1) * 512],
                        start=(hc == 0), stop=False)
                pe.inc(last, 1)
                hm1[(tp, oc)] = pe.n
                e.wait_ge(ld.h, l2)
                for hc in range(16, 32):
                    last = e.matmul(
                        pp[:], w2all[:, (oc * 32 + hc) * 128:
                                     (oc * 32 + hc + 1) * 128],
                        hqc[:, hc * 512:(hc + 1) * 512],
                        start=False, stop=(hc == 31))
                pe.inc(last, 1)
                hm2[(tp, oc)] = pe.n
            TE(_)

            def _(e, tp=tp, oc=oc, pp=pp):
                pe.wait(e)
                dst = outacc[:, oc * TH + tp * 512:oc * TH + (tp + 1) * 512]
                vv.inc(e.tensor_scalar(
                    out=dst, in0=pp[:], scalar1=spp[:, C_S2T:C_S2T + 1],
                    scalar2=biassum[:, oc:oc + 1],
                    op0=ALU.mult, op1=ALU.add), 1)
            VE(_)

    ph5b.close()

    ph4b = contextlib.ExitStack()
    rhsc = ph4b.enter_context(nc.sbuf_tensor("rhsc", [128, 8 * TH], BF16))
    gthr = [ph4b.enter_context(nc.sbuf_tensor(f"gthr{i}", [128, S], BF16))
            for i in range(2)]
    xin6 = [ph4b.enter_context(nc.sbuf_tensor(f"xin6{i}", [128, DM], F32))
            for i in range(2)]
    ybuf = [ph4b.enter_context(nc.sbuf_tensor(f"ybuf{i}", [128, DM], F32))
            for i in range(2)]

    for hc in range(8):
        def _(e, hc=hc):
            e.wait_ge(gg.h, ag_mk)
            if hc >= 2:
                e.wait_ge(vv.h, vv.n)
            else:
                e.wait_ge(pe.h, pe.n)
            ld.inc(e.dma_start(out=gthr[hc % 2][:],
                               in_=am_out[hc * 128:(hc + 1) * 128, :]))
        SY(_)

        def _(e, hc=hc):
            ld.wait(e)
            gt = gthr[hc % 2]
            e.tensor_scalar(out=gt[:, TH:S], in0=gt[:, TH:S],
                            scalar1=spp[:, C_BE:C_BE + 1], scalar2=None,
                            op0=ALU.mult)
            vv.inc(e.scalar_tensor_tensor(
                out=rhsc[:, hc * TH:(hc + 1) * TH], in0=gt[:, 0:TH],
                scalar=spp[:, C_AL:C_AL + 1], in1=gt[:, TH:S],
                op0=ALU.mult, op1=ALU.add), 1)
        VE(_)

    for tp in range(4):
        for oc in range(8):
            pp = m2p[oc % 2]

            def _(e, oc=oc, tp=tp, pp=pp):
                e.wait_ge(vv.h, vv.n)
                last = None
                for hc in range(8):
                    last = e.matmul(
                        pp[:],
                        outwT[:, hc * 1024 + oc * 128:hc * 1024 + (oc + 1) * 128],
                        rhsc[:, hc * TH + tp * 512:hc * TH + (tp + 1) * 512],
                        start=(hc == 0), stop=(hc == 7))
                pe.inc(last, 1)
            TE(_)

            def _(e, oc=oc, tp=tp, pp=pp):
                pe.wait(e)
                dst = outacc[:, oc * TH + tp * 512:oc * TH + (tp + 1) * 512]
                vv.inc(e.scalar_tensor_tensor(
                    out=dst, in0=pp[:], scalar=1.0,
                    in1=dst, op0=ALU.mult, op1=ALU.add), 1)
            VE(_)

        for tc in range(tp * 4, tp * 4 + 4):
            def _(e, tc=tc):
                e.wait_ge(vv.h, vv.n)
                ld.inc(e.dma_start(out=xin6[tc % 2][:],
                                   in_=xh[tc * 128:(tc + 1) * 128, :]))
            SY(_)

            def _(e, tc=tc):
                vv.wait(e)
                last = None
                for oc in range(8):
                    last = e.transpose(
                        fps[:, oc * 128:(oc + 1) * 128],
                        outacc[:, oc * TH + tc * 128:oc * TH + (tc + 1) * 128],
                        ident_f[:])
                tt.inc(last, 1)
            TE(_)

            def _(e, tc=tc):
                tt.wait(e)
                ld.wait(e)
                e.wait_ge(yy.h, yy.n)
                vv.inc(e.scalar_tensor_tensor(out=ybuf[tc % 2][:], in0=fps[:],
                                              scalar=1.0, in1=xin6[tc % 2][:],
                                              op0=ALU.mult, op1=ALU.add), 1)
            VE(_)

            def _(e, tc=tc):
                vv.wait(e)
                yy.inc(e.dma_start(out=y[tc * 128:(tc + 1) * 128, :],
                                   in_=ybuf[tc % 2][:]))
            SY(_)

    ph4b.close()
    ph46.close()

    def _(g):
        for s in sems.values():
            g.wait_ge(s.h, s.n)
    GP(_)

    st.close()
    return nc


def kernel(L0_attn_x, L0_attn_qkv_w, L0_attn_qkv_b, L0_attn_out_w, L0_attn_out_b,
           L0_ffn_W1, L0_ffn_b1, L0_ffn_W2, L0_ffn_b2):
    x = np.ascontiguousarray(np.asarray(L0_attn_x, dtype=np.float32))
    qkv_w = np.ascontiguousarray(np.asarray(L0_attn_qkv_w, dtype=np.float32))
    qkv_b = np.asarray(L0_attn_qkv_b, dtype=np.float32)
    out_w = np.ascontiguousarray(np.asarray(L0_attn_out_w, dtype=np.float32))
    out_b = np.asarray(L0_attn_out_b, dtype=np.float32)
    W1 = np.ascontiguousarray(np.asarray(L0_ffn_W1, dtype=np.float32))
    b1 = np.asarray(L0_ffn_b1, dtype=np.float32)
    W2 = np.ascontiguousarray(np.asarray(L0_ffn_W2, dtype=np.float32))
    b2 = np.asarray(L0_ffn_b2, dtype=np.float32)

    nc = build_program()
    eye_b = np.eye(128, dtype=BF)
    eye_f = np.eye(128, dtype=np.float32)
    o2 = np.zeros((128, 2), np.float32)
    o2[0:64, 0] = 1.0
    o2[64:128, 1] = 1.0
    b1h = np.ascontiguousarray(b1.reshape(32, 128).T)
    b2h = np.ascontiguousarray(b2.reshape(8, 128).T)
    obh = np.ascontiguousarray(out_b.reshape(8, 128).T)
    in_maps = []
    for c in range(8):
        b, half = c // 2, c % 2
        qw = np.ascontiguousarray(np.concatenate(
            [qkv_w[s * DM + half * 512: s * DM + half * 512 + 512, :]
             for s in range(3)], axis=0))
        qb = np.stack([qkv_b[s * DM + half * 512 + p * 128:
                             s * DM + half * 512 + (p + 1) * 128]
                       for s in range(3) for p in range(4)], axis=1)
        qbv = qkv_b[2 * DM + half * 512: 2 * DM + half * 512 + 512].reshape(1, 512)
        hs = np.array([[1.0, 0.0]] if half == 0 else [[0.0, 1.0]], np.float32)
        in_maps.append({
            "xb": np.ascontiguousarray(x[b].reshape(S, DM)),
            "xh": np.ascontiguousarray(
                x[b, half * TH:(half + 1) * TH].reshape(TH, DM)),
            "qkvw": qw,
            "qkvb": np.ascontiguousarray(qb.astype(np.float32)),
            "qkvbv": np.ascontiguousarray(qbv.astype(np.float32)),
            "outw": out_w,
            "outb": obh,
            "w1": W1, "b1": b1h, "w2": W2, "b2": b2h,
            "idbf": eye_b, "idf32": eye_f,
            "ones2": o2, "ones2t": np.ascontiguousarray(o2.T),
            "hsel": hs,
        })
    res = run_bass_kernel_spmd(nc, in_maps, core_ids=list(range(8)))
    out = np.empty((4, S, DM), dtype=np.float32)
    for c in range(8):
        b, half = c // 2, c % 2
        out[b, half * TH:(half + 1) * TH] = res.results[c]["y"]
    return out


# version 20
# speedup vs baseline: 1.0143x; 1.0143x over previous
import contextlib
import numpy as np
import ml_dtypes
import concourse.bass as bass
import concourse.mybir as mybir
from concourse.bass_utils import run_bass_kernel_spmd

F32 = mybir.dt.float32
BF16 = mybir.dt.bfloat16
AF = mybir.ActivationFunctionType
ALU = mybir.AluOpType
AX = mybir.AxisListType.X
BF = ml_dtypes.bfloat16

S = 4096
DM = 1024
FH = 4096
TH = 2048
EPS = 1e-7
MAGIC = 12582912.0
KLN = float(1.0 + np.pi / 2.0)
C_SW1, C_SW2, C_ISW1, C_ISW2, C_SX, C_ISX, C_LOX, C_HIX = range(8)
C_S1T, C_SH, C_ISH, C_LOH, C_HIH, C_S2T, C_AL, C_BE = range(8, 16)


class Sem:
    def __init__(self, stack, nc, name):
        self.h = stack.enter_context(nc.semaphore(name))
        self.n = 0

    def inc(self, instr, amt=16):
        instr.then_inc(self.h, amt)
        self.n += amt
        return instr

    def wait(self, eng, n=None):
        eng.wait_ge(self.h, self.n if n is None else n)


def build_program(debug=()):
    nc = bass.Bass("TRN2", target_bir_lowering=False)
    st = contextlib.ExitStack()

    def dk(name):
        return "ExternalOutput" if name in debug else "Internal"

    xb = nc.dram_tensor("xb", [S, DM], F32, kind="ExternalInput").ap()
    xh = nc.dram_tensor("xh", [TH, DM], F32, kind="ExternalInput").ap()
    qkvw = nc.dram_tensor("qkvw", [3 * 512, DM], F32, kind="ExternalInput").ap()
    qkvb = nc.dram_tensor("qkvb", [128, 12], F32, kind="ExternalInput").ap()
    qkvbv = nc.dram_tensor("qkvbv", [1, 512], F32, kind="ExternalInput").ap()
    outw = nc.dram_tensor("outw", [DM, DM], F32, kind="ExternalInput").ap()
    outb = nc.dram_tensor("outb", [128, 8], F32, kind="ExternalInput").ap()
    w1 = nc.dram_tensor("w1", [FH, DM], F32, kind="ExternalInput").ap()
    b1 = nc.dram_tensor("b1", [128, 32], F32, kind="ExternalInput").ap()
    w2 = nc.dram_tensor("w2", [DM, FH], F32, kind="ExternalInput").ap()
    b2 = nc.dram_tensor("b2", [128, 8], F32, kind="ExternalInput").ap()
    idbf = nc.dram_tensor("idbf", [128, 128], BF16, kind="ExternalInput").ap()
    idf32 = nc.dram_tensor("idf32", [128, 128], F32, kind="ExternalInput").ap()
    ones2 = nc.dram_tensor("ones2", [128, 2], F32, kind="ExternalInput").ap()
    ones2t = nc.dram_tensor("ones2t", [2, 128], F32, kind="ExternalInput").ap()
    hsel = nc.dram_tensor("hsel", [1, 2], F32, kind="ExternalInput").ap()
    y = nc.dram_tensor("y", [TH, DM], F32, kind="ExternalOutput").ap()

    xnT_h = nc.dram_tensor("xnT_h", [DM, S], BF16, kind=dk("xnT_h")).ap()
    h_h = nc.dram_tensor("h_h", [FH, TH], F32, kind=dk("h_h")).ap()
    hq_h = nc.dram_tensor("hq_h", [FH, TH], BF16, kind=dk("hq_h")).ap()
    w2qT_h = nc.dram_tensor("w2qT_h", [8 * FH, 128], BF16, kind=dk("w2qT_h")).ap()
    am_in = nc.dram_tensor("am_in", [512, S], BF16).ap()
    am_out = nc.dram_tensor("am_out", [DM, S], BF16).ap()
    amdbg = (nc.dram_tensor("amdbg", [512, S], BF16, kind="ExternalOutput").ap()
             if "amdbg" in debug else None)
    qkdbg = (nc.dram_tensor("qkdbg", [256, S], BF16, kind="ExternalOutput").ap()
             if "qkdbg" in debug else None)
    stdbg = (nc.dram_tensor("stdbg", [512, S], BF16, kind="ExternalOutput").ap()
             if "qkdbg" in debug else None)
    csdbg = (nc.dram_tensor("csdbg", [128, 16], F32, kind="ExternalOutput").ap()
             if "qkdbg" in debug else None)
    vxdbg = (nc.dram_tensor("vxdbg", [128, 4 * 68], BF16, kind="ExternalOutput").ap()
             if "qkdbg" in debug else None)
    nmdbg = (nc.dram_tensor("nmdbg", [65, S], F32, kind="ExternalOutput").ap()
             if "qkdbg" in debug else None)
    cc1i = nc.dram_tensor("cc1i", [1, 8], F32).ap()
    cc1o = nc.dram_tensor("cc1o", [1, 8], F32).ap()
    cc2i = nc.dram_tensor("cc2i", [1, 8], F32).ap()
    cc2o = nc.dram_tensor("cc2o", [1, 8], F32).ap()
    scdbg = nc.dram_tensor("scdbg", [3, 16], F32, kind=dk("scdbg")).ap()

    def sbuf(name, shape, dtype):
        return st.enter_context(nc.sbuf_tensor(name, shape, dtype))

    ident_b = sbuf("ident_b", [128, 128], BF16)
    ident_f = sbuf("ident_f", [128, 128], F32)
    ones2_s = sbuf("ones2_s", [128, 2], F32)
    ones2t_s = sbuf("ones2t_s", [2, 128], F32)
    onesrow = sbuf("onesrow", [1, 128], F32)
    onesrow_b = sbuf("onesrow_b", [1, 128], BF16)
    onesr64 = sbuf("onesr64", [65, 64], F32)
    bias1 = sbuf("bias1", [128, 32], F32)
    bias2 = sbuf("bias2", [128, 8], F32)
    biaso = sbuf("biaso", [128, 8], F32)
    biasqkv = sbuf("biasqkv", [128, 12], F32)
    bv32 = sbuf("bv32", [1, 512], F32)
    bvrow = sbuf("bvrow", [1, 512], BF16)
    spp = sbuf("spp", [128, 16], F32)
    scal = sbuf("scal", [1, 16], F32)
    ccs = sbuf("ccs", [1, 8], F32)
    ccr = sbuf("ccr", [1, 8], F32)
    stats = sbuf("stats", [128, 80], F32)
    wstat = sbuf("wstat", [128, 40], F32)
    sc1a = sbuf("sc1a", [128, 1], F32)
    sc1b = sbuf("sc1b", [128, 1], F32)
    fsc = sbuf("fsc", [1, 2], F32)
    ctile = sbuf("ctile", [128, 3], F32)
    nc.const_aps.register(F32, 1.0, ctile[:, 0:1])
    nc.const_aps.register(F32, KLN, ctile[:, 1:2])
    nc.const_aps.register(F32, 13.862943611198906, ctile[:, 2:3])

    sems = {}
    for nm in ("ld", "sp", "vv", "ss", "tt", "gg", "c1", "c2", "c3", "yy", "pe"):
        sems[nm] = Sem(st, nc, nm)
    ld, sp, vv, ss, tt, gg = (sems[k] for k in ("ld", "sp", "vv", "ss", "tt", "gg"))
    c1, c2, c3, yy, pe = (sems[k] for k in ("c1", "c2", "c3", "yy", "pe"))

    blk = st.enter_context(nc.Block())

    def SY(f): blk.sync(f)
    def VE(f): blk.vector(f)
    def SC(f): blk.scalar(f)
    def TE(f): blk.tensor(f)
    def GP(f): blk.gpsimd(f)

    def _(g):
        g.memset(onesrow[:], 1.0)
        g.memset(onesrow_b[:], 1.0)
        g.memset(onesr64[:], 1.0)
        g.memset(ctile[:, 0:1], 1.0)
        g.memset(ctile[:, 1:2], KLN)
        g.memset(ctile[:, 2:3], 13.862943611198906)
        g.memset(fsc[:], 0.0)
        g.memset(scal[:], 0.0)
        gg.inc(g.memset(ccs[:], 0.0), 1)
    GP(_)

    def _(e):
        for dst, src in ((ident_b, idbf), (ident_f, idf32), (ones2_s, ones2),
                         (ones2t_s, ones2t), (bias1, b1), (bias2, b2),
                         (biaso, outb), (biasqkv, qkvb), (bv32, qkvbv)):
            ld.inc(e.dma_start(out=dst[:], in_=src[:]))
        ld.inc(e.dma_start(out=scal[0:1, C_AL:C_BE + 1], in_=hsel[:]))
    SY(_)

    def _(e):
        ld.wait(e)
        gg.wait(e)
        vv.inc(e.tensor_copy(out=bvrow[:], in_=bv32[:]), 1)
    VE(_)

    ph01 = contextlib.ExitStack()

    def sb01(name, shape, dtype):
        return ph01.enter_context(nc.sbuf_tensor(name, shape, dtype))

    def ps01(name, shape, dtype=F32):
        return ph01.enter_context(nc.psum_tensor(name, shape, dtype))

    xin = [sb01(f"xin{i}", [128, DM], F32) for i in range(4)]
    xnf = sb01("xnf", [128, DM], F32)
    sqt = sb01("sqt", [128, DM], F32)
    xnb = sb01("xnb", [128, DM], BF16)
    stg = sb01("stg", [128, DM], BF16)
    xpose = ps01("xpose", [128, 1024], BF16)
    pmm1 = ps01("pmm1", [1, 128], F32)
    pmm2 = ps01("pmm2", [1, 128], F32)
    pbc = ps01("pbc", [128, 16], F32)

    xnT_w = xnT_h.rearrange("(a p) t -> p a t", p=128)

    ln_tiny_mk = [0]

    def emit_ln_stats_pair(tc0, src_ap, marks):
        def _(e):
            if len(marks) >= 3:
                e.wait_ge(vv.h, marks[-3])
            elif len(marks) == 0:
                e.wait_ge(vv.h, vv.n)
            for tc in (tc0, tc0 + 1):
                ld.inc(e.dma_start(out=xin[tc % 4][:],
                                   in_=src_ap[tc * 128:(tc + 1) * 128, :]))
        SY(_)

        def _(e):
            ld.wait(e)
            e.wait_ge(vv.h, ln_tiny_mk[0])
            for i, tc in enumerate((tc0, tc0 + 1)):
                c0 = 72 * i
                e.activation(out=sqt[:], in_=xin[tc % 4][:], func=AF.Copy,
                             accum_out=stats[:, c0:c0 + 1])
                e.activation(out=sqt[:], in_=xin[tc % 4][:], func=AF.Square,
                             accum_out=stats[:, c0 + 1:c0 + 2])
            ss.inc(e.copy(out=fsc[0:1, 0:1], in_=fsc[0:1, 1:2]), 1)
        SC(_)
        stats_mk = ss.n

        def _(e, stats_mk=stats_mk):
            e.wait_ge(ss.h, stats_mk)

            def step(ins):
                vv.inc(ins, 1)
                vv.wait(e)
            for i in range(2):
                c0 = 72 * i
                e.tensor_scalar(out=stats[:, c0 + 2:c0 + 3],
                                in0=stats[:, c0:c0 + 1],
                                scalar1=1.0 / DM, scalar2=None, op0=ALU.mult)
                step(e.tensor_scalar(out=stats[:, c0 + 3:c0 + 4],
                                     in0=stats[:, c0 + 1:c0 + 2],
                                     scalar1=1.0 / DM, scalar2=1e-5,
                                     op0=ALU.mult, op1=ALU.add))
                step(e.scalar_tensor_tensor(out=stats[:, c0 + 4:c0 + 5],
                                            in0=stats[:, c0 + 2:c0 + 3],
                                            scalar=0.0,
                                            in1=stats[:, c0 + 2:c0 + 3],
                                            op0=ALU.add, op1=ALU.mult))
                step(e.tensor_sub(stats[:, c0 + 4:c0 + 5],
                                  stats[:, c0 + 3:c0 + 4],
                                  stats[:, c0 + 4:c0 + 5]))
            vv.inc(e.tensor_copy(out=fsc[0:1, 0:1], in_=fsc[0:1, 1:2]), 1)
        VE(_)
        ln_tiny_mk[0] = vv.n

        def _(e):
            vv.wait(e)
            e.activation(out=stats[:, 5:6], in_=stats[:, 4:5], func=AF.Ln)
            ss.inc(e.activation(out=stats[:, 77:78], in_=stats[:, 76:77],
                                func=AF.Ln), 1)
            ss.wait(e)
            e.activation(out=stats[:, 6:7], in_=stats[:, 5:6],
                         func=AF.Exp, scale=-0.5)
            e.activation(out=stats[:, 78:79], in_=stats[:, 77:78],
                         func=AF.Exp, scale=-0.5)
            ss.inc(e.copy(out=fsc[0:1, 0:1], in_=fsc[0:1, 1:2]), 1)
        SC(_)

    def emit_ln_chunk(tc, src_ap, do_spill, marks):
        c0 = 72 * (tc % 2)

        def _(e, c0=c0):
            ss.wait(e)
            xt = xin[tc % 4]
            e.tensor_scalar(out=xnf[:], in0=xt[:],
                            scalar1=stats[:, c0 + 2:c0 + 3],
                            scalar2=stats[:, c0 + 6:c0 + 7], op0=ALU.subtract,
                            op1=ALU.mult)
            if do_spill:
                e.tensor_reduce(out=stats[:, 8 + tc:9 + tc], in_=xnf[:],
                                axis=AX, op=ALU.max)
                e.tensor_reduce(out=stats[:, 40 + tc:41 + tc], in_=xnf[:],
                                axis=AX, op=ALU.min)
                e.wait_ge(sp.h, sp.n)
                e.wait_ge(tt.h, tt.n)
                e.tensor_copy(out=xnb[:], in_=xnf[:])
            vv.inc(e.tensor_copy(out=sqt[:, 0:1], in_=xnf[:, 0:1]), 1)
        VE(_)
        marks.append(vv.n)

        if do_spill:
            def _(e):
                vv.wait(e)
                last = None
                for dc in range(8):
                    last = e.transpose(xpose[:, dc * 128:(dc + 1) * 128],
                                       xnb[:, dc * 128:(dc + 1) * 128],
                                       ident_b[:])
                tt.inc(last, 1)
            TE(_)

            def _(e):
                tt.wait(e)
                vv.inc(e.tensor_copy(out=stg[:], in_=xpose[:]), 1)
            VE(_)
            marks[-1] = vv.n

            def _(e, tc=tc):
                vv.wait(e)
                sp.inc(e.dma_start(
                    out=xnT_w[:, :, tc * 128:(tc + 1) * 128],
                    in_=stg[:].rearrange("p (a q) -> p a q", q=128)))
            SY(_)

    ln_marks0 = []
    for tc0 in range(0, 32, 2):
        emit_ln_stats_pair(tc0, xb, ln_marks0)
        emit_ln_chunk(tc0, xb, do_spill=True, marks=ln_marks0)
        emit_ln_chunk(tc0 + 1, xb, do_spill=True, marks=ln_marks0)

    wld = [sb01(f"wld{i}", [128, FH], F32) for i in range(2)]
    w_marks = []

    def emit_w_absmax(gi, src_ap, cols):
        def _(g):
            if len(w_marks) >= 2:
                g.wait_ge(vv.h, w_marks[-2])
            gg.inc(g.dma_start(out=wld[gi % 2][0:128, 0:cols], in_=src_ap))
        GP(_)
        wld_mk = gg.n

        def _(e, wld_mk=wld_mk):
            e.wait_ge(gg.h, wld_mk)
            vv.inc(e.tensor_reduce(out=wstat[:, gi:gi + 1],
                                   in_=wld[gi % 2][0:128, 0:cols], axis=AX,
                                   op=ALU.max, apply_absolute_value=True), 1)
        VE(_)
        w_marks.append(vv.n)

    for ci in range(32):
        emit_w_absmax(ci, w1[ci * 128:(ci + 1) * 128, :], DM)
    for ci in range(8):
        emit_w_absmax(32 + ci, w2[ci * 128:(ci + 1) * 128, :], FH)

    def _(e):
        e.tensor_reduce(out=sc1a[:], in_=stats[:, 8:40], axis=AX, op=ALU.max)
        vv.inc(e.tensor_reduce(out=sc1b[:], in_=stats[:, 40:72], axis=AX,
                               op=ALU.min), 1)
    VE(_)

    def _(e):
        vv.wait(e)
        e.transpose(pmm1[:], sc1a[:], ident_f[:])
        tt.inc(e.transpose(pmm2[:], sc1b[:], ident_f[:]), 1)
    TE(_)

    def _(e):
        tt.wait(e)
        e.tensor_reduce(out=ccs[0:1, 1:2], in_=pmm1[:], axis=AX, op=ALU.max)
        vv.inc(e.tensor_reduce(out=ccs[0:1, 2:3], in_=pmm2[:], axis=AX,
                               op=ALU.min), 1)
        vv.wait(e)
        e.tensor_scalar(out=ccs[0:1, 0:1], in0=ccs[0:1, 2:3],
                        scalar1=-1.0, scalar2=None, op0=ALU.mult)
        vv.inc(e.tensor_copy(out=fsc[0:1, 0:1], in_=fsc[0:1, 1:2]), 1)
    VE(_)

    def _(g):
        vv.wait(g)
        gg.inc(g.dma_start(out=cc1i[:, :], in_=ccs[0:1, :]))
        gg.wait(g)
        c1.inc(g.collective_compute(
            "AllReduce", ALU.max, replica_groups=[list(range(8))],
            ins=[cc1i.opt()], outs=[cc1o.opt()]), 1)
        g.wait_ge(c1.h, 1)
        gg.inc(g.dma_start(out=ccr[0:1, :], in_=cc1o[:, :]))
    GP(_)

    def _(e):
        e.wait_ge(tt.h, tt.n)
        e.tensor_reduce(out=sc1a[:], in_=wstat[:, 0:32], axis=AX, op=ALU.max)
        vv.inc(e.tensor_reduce(out=sc1b[:], in_=wstat[:, 32:40], axis=AX,
                               op=ALU.max), 1)
    VE(_)

    def _(e):
        vv.wait(e)
        e.transpose(pmm1[:], sc1a[:], ident_f[:])
        tt.inc(e.transpose(pmm2[:], sc1b[:], ident_f[:]), 1)
    TE(_)

    def _(e):
        tt.wait(e)
        e.tensor_reduce(out=scal[0:1, C_SW1:C_SW1 + 1], in_=pmm1[:],
                        axis=AX, op=ALU.max)
        vv.inc(e.tensor_reduce(out=scal[0:1, C_SW2:C_SW2 + 1], in_=pmm2[:],
                               axis=AX, op=ALU.max), 1)
        vv.wait(e)
        e.tensor_scalar(out=scal[0:1, C_SW1:C_SW2 + 1],
                        in0=scal[0:1, C_SW1:C_SW2 + 1],
                        scalar1=1.0 / 127.0, scalar2=1e-8,
                        op0=ALU.mult, op1=ALU.max)
        vv.inc(e.tensor_copy(out=fsc[0:1, 0:1], in_=fsc[0:1, 1:2]), 1)
    VE(_)

    def _(e):
        vv.wait(e)
        ss.inc(e.activation(out=scal[0:1, C_ISW1:C_ISW2 + 1],
                            in_=scal[0:1, C_SW1:C_SW2 + 1], func=AF.Ln), 1)
        ss.wait(e)
        e.activation(out=scal[0:1, C_ISW1:C_ISW2 + 1],
                     in_=scal[0:1, C_ISW1:C_ISW2 + 1],
                     func=AF.Exp, scale=-1.0)
        ss.inc(e.copy(out=fsc[0:1, 0:1], in_=fsc[0:1, 1:2]), 1)
    SC(_)

    def bcast_spp(pbc_t):
        def _(e):
            ss.wait(e)
            vv.wait(e)
            ld.wait(e)
            tt.inc(e.matmul(pbc_t[:], onesrow[:], scal[:],
                            start=True, stop=True), 1)
        TE(_)

        def _(e):
            tt.wait(e)
            e.tensor_copy(out=spp[:], in_=pbc_t[:])
            vv.inc(e.tensor_copy(out=fsc[0:1, 0:1], in_=fsc[0:1, 1:2]), 1)
        VE(_)

    bcast_spp(pbc)

    wqb = sb01("wqb", [128, FH], BF16)
    w1qT = sb01("w1qT", [128, 8 * FH], BF16)
    w2st = sb01("w2st", [128, 4096], BF16)
    w1qT_w = w1qT[:].rearrange("p (a h) -> p a h", a=8)
    w2qT_w = w2qT_h.rearrange("(a p) q -> p a q", p=128)

    def emit_w_quant(gi, src_ap, cols, iswcol, is_w2, ci):
        def _(e):
            if len(w_marks) >= 2:
                e.wait_ge(vv.h, w_marks[-2])
            ld.inc(e.dma_start(out=wld[gi % 2][0:128, 0:cols], in_=src_ap))
        SY(_)

        def _(e):
            ld.wait(e)
            e.wait_ge(tt.h, tt.n)
            wt = wld[gi % 2][0:128, 0:cols]
            e.tensor_scalar(out=wt, in0=wt, scalar1=spp[:, iswcol:iswcol + 1],
                            scalar2=MAGIC, op0=ALU.mult, op1=ALU.add)
            e.tensor_scalar(out=wt, in0=wt, scalar1=MAGIC, scalar2=-127.0,
                            op0=ALU.subtract, op1=ALU.max)
            vv.inc(e.tensor_scalar(out=wqb[0:128, 0:cols], in0=wt,
                                   scalar1=127.0, scalar2=None, op0=ALU.min), 1)
        VE(_)
        w_marks.append(vv.n)

        for g0 in range(0, cols // 128, 8):
            def _(e, g0=g0):
                vv.wait(e)
                last = None
                for k in range(8):
                    last = e.transpose(xpose[:, k * 128:(k + 1) * 128],
                                       wqb[:, (g0 + k) * 128:(g0 + k + 1) * 128],
                                       ident_b[:])
                tt.inc(last, 1)
            TE(_)

            if not is_w2:
                def _(e, ci=ci):
                    tt.wait(e)
                    vv.inc(e.tensor_copy(
                        out=w1qT_w[:, :, ci * 128:(ci + 1) * 128],
                        in_=xpose[:].rearrange("p (a q) -> p a q", q=128)), 1)
                VE(_)
            else:
                def _(e, g0=g0):
                    tt.wait(e)
                    if g0 == 0:
                        e.wait_ge(sp.h, sp.n)
                    vv.inc(e.tensor_copy(out=w2st[:, g0 * 128:(g0 + 8) * 128],
                                         in_=xpose[:]), 1)
                VE(_)
        if is_w2:
            def _(e, ci=ci):
                vv.wait(e)
                sp.inc(e.dma_start(
                    out=w2qT_w[:, ci * 32:(ci + 1) * 32, :],
                    in_=w2st[:].rearrange("p (a q) -> p a q", q=128)))
            SY(_)

    for ci in range(32):
        emit_w_quant(32 + 8 + ci, w1[ci * 128:(ci + 1) * 128, :], DM,
                     C_ISW1, False, ci)
    for ci in range(8):
        emit_w_quant(32 + 8 + 32 + ci, w2[ci * 128:(ci + 1) * 128, :], FH,
                     C_ISW2, True, ci)

    def _(e):
        gg.wait(e)
        vv.inc(e.tensor_add(scal[0:1, C_SX:C_SX + 1], ccr[0:1, 0:1],
                            ccr[0:1, 1:2]), 1)
        vv.wait(e)
        vv.inc(e.tensor_scalar(out=scal[0:1, C_SX:C_SX + 1],
                               in0=scal[0:1, C_SX:C_SX + 1],
                               scalar1=1.0 / 255.0, scalar2=1e-8,
                               op0=ALU.mult, op1=ALU.max), 1)
    VE(_)

    def _(e):
        vv.wait(e)
        ss.inc(e.activation(out=scal[0:1, C_ISX:C_ISX + 1],
                            in_=scal[0:1, C_SX:C_SX + 1], func=AF.Ln), 1)
        ss.wait(e)
        e.activation(out=scal[0:1, C_ISX:C_ISX + 1],
                     in_=scal[0:1, C_ISX:C_ISX + 1],
                     func=AF.Exp, scale=-1.0)
        ss.inc(e.copy(out=fsc[0:1, 0:1], in_=fsc[0:1, 1:2]), 1)
    SC(_)

    def emit_zx(lo_col, hi_col, s_col, isx_col, stot_col, sw_col):
        def _(e):
            ss.wait(e)

            def step(ins):
                vv.inc(ins, 1)
                vv.wait(e)
            step(e.tensor_mul(scal[0:1, lo_col:lo_col + 1], ccr[0:1, 0:1],
                              scal[0:1, isx_col:isx_col + 1]))
            step(e.tensor_scalar(out=scal[0:1, lo_col:lo_col + 1],
                                 in0=scal[0:1, lo_col:lo_col + 1],
                                 scalar1=MAGIC, scalar2=MAGIC,
                                 op0=ALU.add, op1=ALU.subtract))
            step(e.tensor_scalar(out=scal[0:1, lo_col:lo_col + 1],
                                 in0=scal[0:1, lo_col:lo_col + 1],
                                 scalar1=-128.0, scalar2=-128.0,
                                 op0=ALU.add, op1=ALU.max))
            step(e.tensor_scalar(out=scal[0:1, lo_col:lo_col + 1],
                                 in0=scal[0:1, lo_col:lo_col + 1],
                                 scalar1=127.0, scalar2=None, op0=ALU.min))
            step(e.tensor_scalar(out=scal[0:1, hi_col:hi_col + 1],
                                 in0=scal[0:1, lo_col:lo_col + 1],
                                 scalar1=-1.0, scalar2=127.0,
                                 op0=ALU.mult, op1=ALU.add))
            step(e.tensor_scalar(out=scal[0:1, lo_col:lo_col + 1],
                                 in0=scal[0:1, lo_col:lo_col + 1],
                                 scalar1=-1.0, scalar2=-128.0,
                                 op0=ALU.mult, op1=ALU.add))
            e.tensor_mul(scal[0:1, stot_col:stot_col + 1],
                         scal[0:1, s_col:s_col + 1],
                         scal[0:1, sw_col:sw_col + 1])
            vv.inc(e.tensor_copy(out=fsc[0:1, 0:1], in_=fsc[0:1, 1:2]), 1)
        VE(_)

    emit_zx(C_LOX, C_HIX, C_SX, C_ISX, C_S1T, C_SW1)
    bcast_spp(pbc)

    def _(e):
        vv.wait(e)
        sp.inc(e.dma_start(out=scdbg[0:1, :], in_=scal[0:1, :]))
        sp.inc(e.dma_start(out=scdbg[1:2, 0:8], in_=ccs[0:1, :]))
        sp.inc(e.dma_start(out=scdbg[2:3, 0:8], in_=ccr[0:1, :]))
    SY(_)

    xqT = sb01("xqT", [128, 8 * TH], BF16)
    xqb = sb01("xqb", [128, DM], BF16)
    xqT_w = xqT[:].rearrange("p (a t) -> p a t", a=8)
    ln_marks1 = []
    for tc in range(16):
        if tc % 2 == 0:
            emit_ln_stats_pair(tc, xh, ln_marks1)
        emit_ln_chunk(tc, xh, do_spill=False, marks=ln_marks1)

        def _(e, tc=tc):
            e.wait_ge(tt.h, tt.n)
            e.tensor_scalar(out=xnf[:], in0=xnf[:],
                            scalar1=spp[:, C_ISX:C_ISX + 1], scalar2=MAGIC,
                            op0=ALU.mult, op1=ALU.add)
            e.tensor_scalar(out=xnf[:], in0=xnf[:], scalar1=MAGIC,
                            scalar2=spp[:, C_LOX:C_LOX + 1],
                            op0=ALU.subtract, op1=ALU.max)
            vv.inc(e.tensor_scalar(out=xqb[:], in0=xnf[:],
                                   scalar1=spp[:, C_HIX:C_HIX + 1],
                                   scalar2=None, op0=ALU.min), 1)
        VE(_)

        def _(e, tc=tc):
            vv.wait(e)
            last = None
            for dc in range(8):
                last = e.transpose(xpose[:, dc * 128:(dc + 1) * 128],
                                   xqb[:, dc * 128:(dc + 1) * 128], ident_b[:])
            tt.inc(last, 1)
        TE(_)

        def _(e, tc=tc):
            tt.wait(e)
            vv.inc(e.tensor_copy(
                out=xqT_w[:, :, tc * 128:(tc + 1) * 128],
                in_=xpose[:].rearrange("p (a q) -> p a q", q=128)), 1)
        VE(_)
        ln_marks1[-1] = vv.n

    hst = sb01("hst", [128, TH], F32)
    m1p = [ps01(f"m1p{i}", [128, 512], F32) for i in range(2)]
    mm1_ready = vv.n

    for hc in range(32):
        for tp in range(4):
            pp = m1p[(hc * 4 + tp) % 2]

            def _(e, hc=hc, tp=tp, pp=pp):
                e.wait_ge(vv.h, mm1_ready)
                e.wait_ge(ss.h, ss.n)
                last = None
                for dc in range(8):
                    last = e.matmul(
                        pp[:],
                        w1qT[:, dc * FH + hc * 128:dc * FH + (hc + 1) * 128],
                        xqT[:, dc * TH + tp * 512:dc * TH + (tp + 1) * 512],
                        start=(dc == 0), stop=(dc == 7))
                pe.inc(last, 1)
            TE(_)

            def _(e, hc=hc, tp=tp, pp=pp):
                pe.wait(e)
                if tp == 0:
                    e.wait_ge(sp.h, sp.n)
                    e.wait_ge(vv.h, vv.n)
                ss.inc(e.activation(out=hst[:, tp * 512:(tp + 1) * 512],
                                    in_=pp[:], func=AF.Gelu,
                                    scale=spp[:, C_S1T:C_S1T + 1],
                                    bias=bias1[:, hc:hc + 1]), 1)
            SC(_)

        def _(e, hc=hc):
            ss.wait(e)
            e.tensor_reduce(out=stats[:, 8 + hc:9 + hc], in_=hst[:], axis=AX,
                            op=ALU.max)
            vv.inc(e.tensor_reduce(out=stats[:, 40 + hc:41 + hc], in_=hst[:],
                                   axis=AX, op=ALU.min), 1)
        VE(_)

        def _(e, hc=hc):
            vv.wait(e)
            sp.inc(e.dma_start(out=h_h[hc * 128:(hc + 1) * 128, :], in_=hst[:]))
        SY(_)

    def _(e):
        e.tensor_reduce(out=sc1a[:], in_=stats[:, 8:40], axis=AX, op=ALU.max)
        vv.inc(e.tensor_reduce(out=sc1b[:], in_=stats[:, 40:72], axis=AX,
                               op=ALU.min), 1)
    VE(_)

    def _(e):
        vv.wait(e)
        e.transpose(pmm1[:], sc1a[:], ident_f[:])
        tt.inc(e.transpose(pmm2[:], sc1b[:], ident_f[:]), 1)
    TE(_)

    def _(e):
        tt.wait(e)
        e.tensor_reduce(out=ccs[0:1, 1:2], in_=pmm1[:], axis=AX, op=ALU.max)
        vv.inc(e.tensor_reduce(out=ccs[0:1, 2:3], in_=pmm2[:], axis=AX,
                               op=ALU.min), 1)
        vv.wait(e)
        e.tensor_scalar(out=ccs[0:1, 0:1], in0=ccs[0:1, 2:3],
                        scalar1=-1.0, scalar2=None, op0=ALU.mult)
        vv.inc(e.tensor_copy(out=fsc[0:1, 0:1], in_=fsc[0:1, 1:2]), 1)
    VE(_)

    def _(g):
        vv.wait(g)
        gg.inc(g.dma_start(out=cc2i[:, :], in_=ccs[0:1, :]))
        gg.wait(g)
        c2.inc(g.collective_compute(
            "AllReduce", ALU.max, replica_groups=[list(range(8))],
            ins=[cc2i.opt()], outs=[cc2o.opt()]), 1)
        g.wait_ge(c2.h, 1)
        gg.inc(g.dma_start(out=ccr[0:1, :], in_=cc2o[:, :]))
    GP(_)
    ccr2_mk = gg.n
    h_done_mk = sp.n

    ph01.close()

    ph2 = contextlib.ExitStack()

    def sb2(name, shape, dtype):
        return ph2.enter_context(nc.sbuf_tensor(name, shape, dtype))

    scoreT = sb2("scoreT", [128, 4 * S], BF16)
    qhat = sb2("qhat", [128, S], BF16)
    khat = sb2("khat", [128, S], BF16)
    vtok = sb2("vtok", [128, 32 * 128], BF16)
    vext = sb2("vext", [128, 4 * 68], BF16)
    numer = sb2("numer", [65, S], F32)
    colss = sb2("colss", [128, 8], F32)
    coltl = sb2("coltl", [128, 4], F32)
    rnq = sb2("rnq", [2, TH], F32)
    atst = [sb2(f"atst{i}", [64, 512], BF16) for i in range(2)]

    hstq = [sb2(f"hstq{i}", [128, 1024], F32) for i in range(2)]
    hqtq = sb2("hqtq", [128, 1024], BF16)
    hq_lmk = {}
    hq_qmk = {}
    hq_smk = {}
    hq_state = [0, 0]

    def emit_hq_load(j):
        hc, t2 = j // 2, j % 2

        def _(e, j=j, hc=hc, t2=t2):
            e.wait_ge(sp.h, h_done_mk)
            if j >= 2:
                e.wait_ge(vv.h, hq_qmk[j - 2])
            ld.inc(e.dma_start(out=hstq[j % 2][:],
                               in_=h_h[hc * 128:(hc + 1) * 128,
                                       t2 * 1024:(t2 + 1) * 1024]))
            hq_lmk[j] = ld.n
        SY(_)

    def emit_hq_quant(j):
        def _(e, j=j):
            e.wait_ge(ld.h, hq_lmk[j])
            e.wait_ge(sp.h, hq_smk.get(j - 1, 0))
            hs = hstq[j % 2]
            e.tensor_scalar(out=hs[:], in0=hs[:],
                            scalar1=spp[:, C_ISH:C_ISH + 1], scalar2=MAGIC,
                            op0=ALU.mult, op1=ALU.add)
            e.tensor_scalar(out=hs[:], in0=hs[:], scalar1=MAGIC,
                            scalar2=spp[:, C_LOH:C_LOH + 1],
                            op0=ALU.subtract, op1=ALU.max)
            vv.inc(e.tensor_scalar(out=hqtq[:], in0=hs[:],
                                   scalar1=spp[:, C_HIH:C_HIH + 1],
                                   scalar2=None, op0=ALU.min), 1)
            hq_qmk[j] = vv.n
        VE(_)

    def emit_hq_spill(j):
        hc, t2 = j // 2, j % 2

        def _(e, j=j, hc=hc, t2=t2):
            e.wait_ge(vv.h, hq_qmk[j])
            sp.inc(e.dma_start(out=hq_h[hc * 128:(hc + 1) * 128,
                                        t2 * 1024:(t2 + 1) * 1024],
                               in_=hqtq[:]))
            hq_smk[j] = sp.n
        SY(_)

    def emit_hq_step():
        j, g = hq_state
        if g % 3 == 0 and j < 64:
            emit_hq_load(j)
        elif g % 3 == 2 and j < 64:
            emit_hq_quant(j)
            emit_hq_spill(j)
            hq_state[0] = j + 1
        hq_state[1] = g + 1

    for p in range(4):
        pstart_vv, pstart_ss = vv.n, ss.n
        prj = contextlib.ExitStack()
        xnp = prj.enter_context(nc.sbuf_tensor(f"xnp{p}", [128, 8 * TH], BF16))
        wqkT = prj.enter_context(nc.sbuf_tensor(f"wqkT{p}", [128, 3 * 1024], BF16))
        qf = prj.enter_context(nc.sbuf_tensor(f"qf{p}", [128, TH], F32))
        kf = prj.enter_context(nc.sbuf_tensor(f"kf{p}", [128, TH], F32))
        sqp = prj.enter_context(nc.sbuf_tensor(f"sqp{p}", [128, 512], F32))
        wl32 = prj.enter_context(nc.sbuf_tensor(f"wl32{p}", [128, DM], F32))
        wlb = prj.enter_context(nc.sbuf_tensor(f"wlb{p}", [128, DM], BF16))
        xp2 = prj.enter_context(nc.psum_tensor(f"xp2{p}", [128, 1024], BF16))
        qp = [prj.enter_context(nc.psum_tensor(f"qp{p}{i}", [128, 512], F32))
              for i in range(2)]
        npp = prj.enter_context(nc.psum_tensor(f"npp{p}", [2, 512], F32))
        bcp = prj.enter_context(nc.psum_tensor(f"bcp{p}", [128, 512], F32))
        vp = [prj.enter_context(nc.psum_tensor(f"vp{p}{i}", [128, 512], F32))
              for i in range(2)]

        if p == 1:
            pbch = prj.enter_context(nc.psum_tensor("pbch", [128, 16], F32))

            def _(e):
                e.wait_ge(gg.h, ccr2_mk)
                vv.inc(e.tensor_add(scal[0:1, C_SH:C_SH + 1], ccr[0:1, 0:1],
                                    ccr[0:1, 1:2]), 1)
                vv.wait(e)
                vv.inc(e.tensor_scalar(out=scal[0:1, C_SH:C_SH + 1],
                                       in0=scal[0:1, C_SH:C_SH + 1],
                                       scalar1=1.0 / 255.0, scalar2=1e-8,
                                       op0=ALU.mult, op1=ALU.max), 1)
            VE(_)

            def _(e):
                vv.wait(e)
                ss.inc(e.activation(out=scal[0:1, C_ISH:C_ISH + 1],
                                    in_=scal[0:1, C_SH:C_SH + 1],
                                    func=AF.Ln), 1)
                ss.wait(e)
                e.activation(out=scal[0:1, C_ISH:C_ISH + 1],
                             in_=scal[0:1, C_ISH:C_ISH + 1],
                             func=AF.Exp, scale=-1.0)
                ss.inc(e.copy(out=fsc[0:1, 0:1], in_=fsc[0:1, 1:2]), 1)
            SC(_)

            emit_zx(C_LOH, C_HIH, C_SH, C_ISH, C_S2T, C_SW2)
            bcast_spp(pbch)

        for sec in range(3):
            def _(e, sec=sec, p=p):
                e.wait_ge(vv.h, vv.n)
                e.wait_ge(ss.h, ss.n)
                e.wait_ge(gg.h, gg.n)
                e.wait_ge(pe.h, pe.n)
                ld.inc(e.dma_start(
                    out=wl32[:],
                    in_=qkvw[sec * 512 + p * 128: sec * 512 + (p + 1) * 128, :]))
            SY(_)

            def _(e, sec=sec):
                ld.wait(e)
                e.wait_ge(tt.h, tt.n)
                vv.inc(e.tensor_copy(out=wlb[:], in_=wl32[:]), 1)
            VE(_)

            def _(e, sec=sec):
                vv.wait(e)
                last = None
                for dc in range(8):
                    last = e.transpose(xp2[:, dc * 128:(dc + 1) * 128],
                                       wlb[:, dc * 128:(dc + 1) * 128],
                                       ident_b[:])
                tt.inc(last, 1)
            TE(_)

            def _(e, sec=sec):
                tt.wait(e)
                vv.inc(e.tensor_copy(out=wqkT[:, sec * 1024:(sec + 1) * 1024],
                                     in_=xp2[:]), 1)
            VE(_)

        wq_mk = vv.n
        qp_mk = [pstart_vv, pstart_vv]
        vp_mk = [pstart_vv, pstart_vv]
        xnp_gmk = [0, 0]

        def emit_norm1(srcf):
            for tp in range(4):
                def _(e, tp=tp, srcf=srcf):
                    e.wait_ge(tt.h, tt.n)
                    qs = srcf[:, tp * 512:(tp + 1) * 512]
                    vv.inc(e.scalar_tensor_tensor(out=sqp[:], in0=qs,
                                                  scalar=0.0, in1=qs,
                                                  op0=ALU.add,
                                                  op1=ALU.mult), 1)
                VE(_)

                def _(e, tp=tp):
                    vv.wait(e)
                    tt.inc(e.matmul(npp[:], ones2_s[:], sqp[:],
                                    start=True, stop=True), 1)
                TE(_)

                def _(e, tp=tp):
                    tt.wait(e)
                    vv.inc(e.tensor_copy(
                        out=rnq[:, tp * 512:(tp + 1) * 512],
                        in_=npp[:]), 1)
                VE(_)

        def emit_norm2(srcf, dstt, t0c):
            def _(e):
                vv.inc(e.tensor_scalar(
                    out=rnq[:], in0=rnq[:],
                    scalar1=1e-24, scalar2=None, op0=ALU.max), 1)
            VE(_)

            def _(e):
                vv.wait(e)
                ss.inc(e.activation(out=rnq[:], in_=rnq[:], func=AF.Ln), 1)
                ss.wait(e)
                e.activation(out=rnq[:], in_=rnq[:], func=AF.Exp,
                             scale=-0.5)
                ss.inc(e.copy(out=fsc[0:1, 0:1], in_=fsc[0:1, 1:2]), 1)
            SC(_)

            for tp in range(4):
                def _(e, tp=tp):
                    ss.wait(e)
                    e.wait_ge(vv.h, vv.n)
                    tt.inc(e.matmul(
                        bcp[:], ones2t_s[:],
                        rnq[:, tp * 512:(tp + 1) * 512],
                        start=True, stop=True), 1)
                TE(_)

                def _(e, tp=tp, dstt=dstt, srcf=srcf, t0c=t0c):
                    tt.wait(e)
                    vv.inc(e.tensor_mul(
                        dstt[:, t0c + tp * 512:t0c + (tp + 1) * 512],
                        srcf[:, tp * 512:(tp + 1) * 512], bcp[:]), 1)
                VE(_)

        for th in range(2):
            t0c = th * TH
            for dc in range(4):
                def _(e, dc=dc, th=th):
                    sp.wait(e)
                    if dc == 0 and th == 1:
                        e.wait_ge(pe.h, pe.n)
                    ld.inc(e.dma_start(
                        out=xnp[:, dc * TH:(dc + 1) * TH],
                        in_=xnT_h[dc * 128:(dc + 1) * 128,
                                  th * TH:(th + 1) * TH]))
                SY(_)
            xnp_lmk = ld.n

            def _(g, th=th, pen=pe.n, pvv=pstart_vv, pss=pstart_ss):
                g.wait_ge(vv.h, pvv)
                g.wait_ge(ss.h, pss)
                g.wait_ge(pe.h, pen)
                for dc in range(4, 8):
                    gg.inc(g.dma_start(
                        out=xnp[:, dc * TH:(dc + 1) * TH],
                        in_=xnT_h[dc * 128:(dc + 1) * 128,
                                  th * TH:(th + 1) * TH]))
            GP(_)
            xnp_gmk[th] = gg.n

            for sec, dstf in ((0, qf), (1, kf)):
                for tp in range(4):
                    pp = qp[tp % 2]

                    def _(e, sec=sec, tp=tp, pp=pp, th=th,
                          pmk=qp_mk[tp % 2], xl=xnp_lmk):
                        e.wait_ge(ld.h, xl)
                        e.wait_ge(gg.h, xnp_gmk[th])
                        e.wait_ge(vv.h, max(pmk, wq_mk))
                        last = None
                        for dc in range(8):
                            last = e.matmul(
                                pp[:],
                                wqkT[:, sec * 1024 + dc * 128:
                                     sec * 1024 + (dc + 1) * 128],
                                xnp[:, dc * TH + tp * 512:dc * TH + (tp + 1) * 512],
                                start=(dc == 0), stop=(dc == 7))
                        pe.inc(last, 1)
                    TE(_)

                    def _(e, sec=sec, tp=tp, pp=pp, p=p, dstf=dstf):
                        pe.wait(e)
                        vv.inc(e.tensor_scalar(
                            out=dstf[:, tp * 512:(tp + 1) * 512], in0=pp[:],
                            scalar1=biasqkv[:, sec * 4 + p:sec * 4 + p + 1],
                            scalar2=None, op0=ALU.add), 1)
                    VE(_)
                    qp_mk[tp % 2] = vv.n

            emit_norm1(qf)

            for g4 in range(4):
                pp = vp[g4 % 2]

                def _(e, g4=g4, pp=pp, p=p, th=th,
                      pmk=vp_mk[g4 % 2], xl=xnp_lmk):
                    e.wait_ge(vv.h, pmk)
                    e.wait_ge(ld.h, xl)
                    e.wait_ge(gg.h, xnp_gmk[th])
                    last = None
                    for k in range(4):
                        tcl = g4 * 4 + k
                        for dc in range(8):
                            e.matmul(pp[:, k * 128:(k + 1) * 128],
                                     xnp[:, dc * TH + tcl * 128:
                                         dc * TH + (tcl + 1) * 128],
                                     wqkT[:, 2 * 1024 + dc * 128:
                                          2 * 1024 + (dc + 1) * 128],
                                     start=(dc == 0), stop=False)
                        last = e.matmul(pp[:, k * 128:(k + 1) * 128],
                                        onesrow_b[:],
                                        bvrow[0:1, p * 128:(p + 1) * 128],
                                        start=False, stop=True)
                    pe.inc(last, 1)
                TE(_)

                def _(e, g4=g4, pp=pp, th=th):
                    pe.wait(e)
                    vv.inc(e.tensor_copy(
                        out=vtok[:, th * 2048 + g4 * 512:
                                 th * 2048 + (g4 + 1) * 512],
                        in_=pp[:]), 1)
                VE(_)
                vp_mk[g4 % 2] = vv.n

            emit_norm2(qf, qhat, t0c)
            emit_norm1(kf)
            emit_norm2(kf, khat, t0c)

        prj.close()

        if qkdbg is not None and p == 0:
            def _(e):
                vv.wait(e)
                sp.inc(e.dma_start(out=qkdbg[0:128, :], in_=qhat[:]))
                sp.inc(e.dma_start(out=qkdbg[128:256, :], in_=khat[:]))
            SY(_)

        stp = contextlib.ExitStack()
        chA = [stp.enter_context(nc.sbuf_tensor(f"chA{p}{s}", [128, 4096], F32))
               for s in range(2)]
        chB = [stp.enter_context(nc.sbuf_tensor(f"chB{p}{s}", [128, 4096], F32))
               for s in range(2)]
        chC = [stp.enter_context(nc.sbuf_tensor(f"chC{p}{s}", [128, 4096], F32))
               for s in range(2)]
        cosp = [stp.enter_context(nc.psum_tensor(f"cosp{p}{s}", [128, 1024], F32))
                for s in range(2)]
        pvp = stp.enter_context(nc.psum_tensor(f"pvp{p}", [65, 2048], F32))

        clampmk = {}
        mmk = {}
        act1mk = {}
        xmk = {}
        act2mk = {}
        vextmk = {}
        evacmk = {}
        pvmk = {}
        lastpv = pe.n
        pvp_free = vv.n
        lastcolread = vv.n
        lastbcast = tt.n
        lastdenom = ss.n
        proj_ready = vv.n

        NCH = 64

        def emit_cos(n):
            stx_, j4_ = (n % 32) // 4, n % 4
            q0_ = 64 * (n // 32)
            s = n % 2
            for r in range(4):
                def _(e, n=n, r=r, stx=stx_, j4=j4_, q0=q0_):
                    prev = (n, r - 2) if r >= 2 else (n - 1, r + 2)
                    e.wait_ge(vv.h, max(clampmk.get(prev, 0), proj_ready))
                    last = None
                    for h2 in range(2):
                        i0_ = r * 1024 + h2 * 512
                        last = e.matmul(
                            cosp[r % 2][:, h2 * 512:(h2 + 1) * 512],
                            khat[q0:q0 + 64,
                                 stx * 512 + j4 * 128:stx * 512 + (j4 + 1) * 128],
                            qhat[q0:q0 + 64, i0_:i0_ + 512],
                            start=True, stop=True)
                    pe.inc(last, 1)
                    pvmk[("cos", n, r)] = pe.n
                TE(_)

                def _(e, n=n, r=r, s=s):
                    e.wait_ge(pe.h, pvmk[("cos", n, r)])
                    e.wait_ge(gg.h, xmk.get(n - 2, 0))
                    vv.inc(e.tensor_scalar(
                        out=chA[s][:, r * 1024:(r + 1) * 1024],
                        in0=cosp[r % 2][:],
                        scalar1=-(1.0 - EPS), scalar2=(1.0 - EPS),
                        op0=ALU.max, op1=ALU.min), 1)
                    clampmk[(n, r)] = vv.n
                VE(_)

        def emit_m(n):
            s = n % 2

            def _(g, n=n, s=s):
                g.wait_ge(vv.h, clampmk[(n, 3)])
                gg.inc(g.tensor_tensor(out=chB[s][:], in0=chA[s][:],
                                       in1=chA[s][:], op=ALU.mult), 1)
                mmk[n] = gg.n
            GP(_)

        def emit_act1(n):
            s = n % 2

            def _(e, n=n, s=s):
                e.wait_ge(gg.h, mmk[n])
                e.activation(out=chB[s][:], in_=chB[s][:], func=AF.Ln,
                             scale=-1.0, bias=1.0)
                ss.inc(e.activation(out=chB[s][:], in_=chB[s][:],
                                    func=AF.Sigmoid, scale=-0.5), 1)
                act1mk[n] = ss.n
            SC(_)

        def emit_x(n):
            s = n % 2

            def _(g, n=n, s=s):
                g.wait_ge(ss.h, max(act1mk[n], act2mk.get(n - 2, 0)))
                gg.inc(g.tensor_tensor(out=chC[s][:], in0=chA[s][:],
                                       in1=chB[s][:], op=ALU.mult), 1)
                xmk[n] = gg.n
            GP(_)

        def emit_act2(n):
            s = n % 2
            j4_ = n % 4

            def _(e, n=n, s=s, j4=j4_, lpv=None):
                e.wait_ge(gg.h, xmk[n])
                e.wait_ge(vv.h, lastcolread)
                e.wait_ge(pe.h, lastpv)
                e.activation(out=chC[s][:], in_=chC[s][:], func=AF.Arctan)
                e.activation(out=chC[s][:], in_=chC[s][:], func=AF.Ln,
                             scale=-2.0, bias=KLN)
                e.activation(out=scoreT[:, j4 * S:(j4 + 1) * S],
                             in_=chC[s][:], func=AF.Exp, scale=-65.0,
                             accum_out=colss[:, j4:j4 + 1])
                ss.inc(e.copy(out=fsc[0:1, 0:1], in_=fsc[0:1, 1:2]), 1)
                act2mk[n] = ss.n
            SC(_)

        def emit_tail(st_g):
            nonlocal_vars = None
            stx_ = st_g % 8
            q0_ = 64 * (st_g // 8)

            def _(e, st_g=st_g):
                e.wait_ge(ss.h, act2mk[st_g * 4 + 3])
                vv.inc(e.tensor_scalar(out=colss[:, 4:8], in0=colss[:, 0:4],
                                       scalar1=1e-25, scalar2=None,
                                       op0=ALU.max), 1)
            VE(_)
            ret_lastcolread = vv.n

            def _(e, st_g=st_g):
                vv.wait(e)
                ss.inc(e.activation(out=colss[:, 4:8], in_=colss[:, 4:8],
                                    func=AF.Ln, scale=float(2.0 ** 40)), 1)
                ss.wait(e)
                e.activation(out=coltl[:], in_=colss[:, 4:8],
                             func=AF.Exp, scale=-0.5,
                             bias=13.862943611198906)
                ss.inc(e.copy(out=fsc[0:1, 0:1], in_=fsc[0:1, 1:2]), 1)
            SC(_)
            coltl_mk = ss.n

            def _(e, stx=stx_, q0=q0_, coltl_mk=coltl_mk):
                e.wait_ge(ss.h, coltl_mk)
                e.wait_ge(pe.h, lastpv)
                for j4 in range(4):
                    e.tensor_copy(out=vext[:, j4 * 68 + 64:j4 * 68 + 65],
                                  in_=coltl[:, j4:j4 + 1])
                for j4 in range(4):
                    tc = stx * 4 + j4
                    e.tensor_scalar(
                        out=vext[:, j4 * 68:j4 * 68 + 64],
                        in0=vtok[:, tc * 128 + q0:tc * 128 + q0 + 64],
                        scalar1=coltl[:, j4:j4 + 1], scalar2=None,
                        op0=ALU.mult)
                vv.inc(e.tensor_copy(out=fsc[0:1, 0:1], in_=fsc[0:1, 1:2]), 1)
            VE(_)
            vextmk[st_g] = vv.n
            return ret_lastcolread

        def emit_pv(st_g):
            nonlocal pvp_free
            stx_ = st_g % 8
            for ipc in range(2):
                def _(e, ipc=ipc, st_g=st_g, pf=pvp_free):
                    e.wait_ge(vv.h, max(vextmk[st_g], pf))
                    e.wait_ge(ss.h, act2mk[st_g * 4 + 3])
                    last = None
                    for j4 in range(4):
                        for n5 in range(4):
                            c0 = j4 * S + ipc * 2048 + n5 * 512
                            last = e.matmul(
                                pvp[:, n5 * 512:(n5 + 1) * 512],
                                vext[:, j4 * 68:j4 * 68 + 65],
                                scoreT[:, c0:c0 + 512],
                                start=(j4 == 0), stop=(j4 == 3))
                    pe.inc(last, 1)
                    pvmk[(st_g, ipc)] = pe.n
                TE(_)

                def _(e, ipc=ipc, st_g=st_g, stx=stx_, lb=lastbcast, ld_=lastdenom):
                    e.wait_ge(pe.h, pvmk[(st_g, ipc)])
                    if stx == 0:
                        e.wait_ge(tt.h, lb)
                        e.wait_ge(ss.h, ld_)
                    dst = numer[:, ipc * 2048:(ipc + 1) * 2048]
                    if stx == 0:
                        ins = e.tensor_copy(out=dst, in_=pvp[:])
                    else:
                        ins = e.tensor_add(dst, dst, pvp[:])
                    vv.inc(ins, 1)
                    evacmk[(st_g, ipc)] = vv.n
                VE(_)
                pvp_free = vv.n

        def emit_epilogue(hh, p):
            nonlocal pvp_free, lastdenom, lastbcast

            def _(e, hh=hh):
                e.wait_ge(vv.h, max(evacmk[(hh * 8 + 7, 0)],
                                    evacmk[(hh * 8 + 7, 1)]))
                ss.inc(e.activation(out=numer[64:65, :], in_=numer[64:65, :],
                                    func=AF.Ln), 1)
                ss.wait(e)
                e.activation(out=numer[64:65, :], in_=numer[64:65, :],
                             func=AF.Exp, scale=-1.0)
                ss.inc(e.copy(out=fsc[0:1, 0:1], in_=fsc[0:1, 1:2]), 1)
            SC(_)
            lastdenom = ss.n

            for ip in range(8):
                def _(e, ip=ip, pf=pvp_free, hh=hh):
                    e.wait_ge(ss.h, lastdenom)
                    e.wait_ge(vv.h, max(pf, evacmk[(hh * 8 + 7, 0)],
                                        evacmk[(hh * 8 + 7, 1)]))
                    tt.inc(e.matmul(pvp[0:64, 0:512], onesr64[64:65, :],
                                    numer[64:65, ip * 512:(ip + 1) * 512],
                                    start=True, stop=True), 1)
                TE(_)
                bc_mk = tt.n

                def _(e, ip=ip, bc_mk=bc_mk):
                    e.wait_ge(tt.h, bc_mk)
                    if ip >= 2:
                        e.wait_ge(sp.h, sp.n)
                    e.tensor_mul(atst[ip % 2][:],
                                 numer[0:64, ip * 512:(ip + 1) * 512],
                                 pvp[0:64, 0:512])
                    vv.inc(e.tensor_copy(out=fsc[0:1, 0:1],
                                         in_=fsc[0:1, 1:2]), 1)
                VE(_)
                pvp_free = vv.n

                def _(e, ip=ip, p=p, hh=hh):
                    vv.wait(e)
                    r0 = (p * 2 + hh) * 64
                    sp.inc(e.dma_start(
                        out=am_in[r0:r0 + 64, ip * 512:(ip + 1) * 512],
                        in_=atst[ip % 2][:]))
                    if amdbg is not None:
                        sp.inc(e.dma_start(
                            out=amdbg[r0:r0 + 64, ip * 512:(ip + 1) * 512],
                            in_=atst[ip % 2][:]))
                SY(_)
            lastbcast = tt.n

        emit_cos(0); emit_m(0)
        emit_cos(1); emit_m(1)
        emit_act1(0)
        for n in range(NCH):
            if n + 1 < NCH:
                emit_act1(n + 1)
            emit_x(n)
            emit_act2(n)
            if n + 2 < NCH:
                emit_cos(n + 2)
                emit_m(n + 2)
            if n % 4 == 3:
                st_g = n // 4
                lastcolread = emit_tail(st_g)
                emit_pv(st_g)
                lastpv = pe.n
                if st_g % 8 == 7:
                    emit_epilogue(st_g // 8, p)
            if p >= 1:
                emit_hq_step()

        stp.close()

    ph2.close()

    def _(g):
        sp.wait(g)
        c3.inc(g.collective_compute(
            "AllGather", ALU.bypass,
            replica_groups=[[0, 1], [2, 3], [4, 5], [6, 7]],
            ins=[am_in.opt()], outs=[am_out.opt()]), 1)
        g.wait_ge(c3.h, 1)
        gg.inc(g.engine_nop(), 1)
    GP(_)
    ag_mk = gg.n

    ph46 = contextlib.ExitStack()

    def sb46(name, shape, dtype):
        return ph46.enter_context(nc.sbuf_tensor(name, shape, dtype))

    def ps46(name, shape, dtype=F32):
        return ph46.enter_context(nc.psum_tensor(name, shape, dtype))

    outacc = sb46("outacc", [128, 8 * TH], F32)
    outwT = sb46("outwT", [128, 8 * 1024], BF16)
    biassum = sb46("biassum", [128, 8], F32)
    fps = ps46("fps", [128, 1024], F32)
    m2p = [ps46(f"m2p{i}", [128, 512]) for i in range(2)]
    outwT_w = outwT[:].rearrange("p (a o) -> p a o", a=8)

    def _(e):
        e.wait_ge(sp.h, sp.n)
        vv.inc(e.tensor_add(biassum[:], biaso[:], bias2[:]), 1)
        vv.wait(e)
        vv.inc(e.tensor_copy(out=fsc[0:1, 0:1], in_=fsc[0:1, 1:2]), 1)
    VE(_)

    ph4a = contextlib.ExitStack()
    wl4 = ph4a.enter_context(nc.sbuf_tensor("wl4", [128, DM], F32))
    wlb4 = ph4a.enter_context(nc.sbuf_tensor("wlb4", [128, DM], BF16))
    xp4 = ph4a.enter_context(nc.psum_tensor("xp4", [128, 1024], BF16))

    for oc in range(8):
        def _(e, oc=oc):
            e.wait_ge(vv.h, vv.n)
            if oc == 0:
                e.wait_ge(sp.h, sp.n)
            ld.inc(e.dma_start(out=wl4[:], in_=outw[oc * 128:(oc + 1) * 128, :]))
        SY(_)

        def _(e, oc=oc):
            ld.wait(e)
            e.wait_ge(tt.h, tt.n)
            vv.inc(e.tensor_copy(out=wlb4[:], in_=wl4[:]), 1)
        VE(_)

        def _(e, oc=oc):
            vv.wait(e)
            last = None
            for dc in range(8):
                last = e.transpose(xp4[:, dc * 128:(dc + 1) * 128],
                                   wlb4[:, dc * 128:(dc + 1) * 128], ident_b[:])
            tt.inc(last, 1)
        TE(_)

        def _(e, oc=oc):
            tt.wait(e)
            vv.inc(e.tensor_copy(
                out=outwT_w[:, :, oc * 128:(oc + 1) * 128],
                in_=xp4[:].rearrange("p (a q) -> p a q", q=128)), 1)
        VE(_)

    ph4a.close()

    w2s_r = w2qT_h.rearrange("(a p) q -> p a q", p=128)

    ph5b = contextlib.ExitStack()
    w2all = ph5b.enter_context(nc.sbuf_tensor("w2all", [128, 256 * 128], BF16))
    hqc = ph5b.enter_context(nc.sbuf_tensor("hqc", [128, 32 * 512], BF16))
    w2all_w = w2all[:].rearrange("p (a q) -> p a q", q=128)

    def _(e):
        e.wait_ge(vv.h, vv.n)
        e.wait_ge(pe.h, pe.n)
        sp.wait(e)
        for g in range(4):
            ld.inc(e.dma_start(out=w2all_w[:, g * 64:(g + 1) * 64, :],
                               in_=w2s_r[:, g * 64:(g + 1) * 64, :]))
    SY(_)

    hm1 = {}
    hm2 = {}
    for tp in range(4):
        for hf in range(2):
            def _(e, tp=tp, hf=hf):
                if tp == 0:
                    sp.wait(e)
                else:
                    mk = hm1 if hf == 0 else hm2
                    e.wait_ge(pe.h, mk[(tp - 1, 7)])
                for hc in range(hf * 16, hf * 16 + 16):
                    ld.inc(e.dma_start(
                        out=hqc[:, hc * 512:(hc + 1) * 512],
                        in_=hq_h[hc * 128:(hc + 1) * 128,
                                 tp * 512:(tp + 1) * 512]))
            SY(_)
            if hf == 0:
                hq_ld1 = ld.n
            else:
                hq_ld2 = ld.n

        for oc in range(8):
            pp = m2p[oc % 2]

            def _(e, tp=tp, oc=oc, pp=pp, l1=hq_ld1, l2=hq_ld2):
                e.wait_ge(ld.h, l1)
                e.wait_ge(vv.h, vv.n)
                last = None
                for hc in range(16):
                    last = e.matmul(
                        pp[:], w2all[:, (oc * 32 + hc) * 128:
                                     (oc * 32 + hc + 1) * 128],
                        hqc[:, hc * 512:(hc + 1) * 512],
                        start=(hc == 0), stop=False)
                pe.inc(last, 1)
                hm1[(tp, oc)] = pe.n
                e.wait_ge(ld.h, l2)
                for hc in range(16, 32):
                    last = e.matmul(
                        pp[:], w2all[:, (oc * 32 + hc) * 128:
                                     (oc * 32 + hc + 1) * 128],
                        hqc[:, hc * 512:(hc + 1) * 512],
                        start=False, stop=(hc == 31))
                pe.inc(last, 1)
                hm2[(tp, oc)] = pe.n
            TE(_)

            def _(e, tp=tp, oc=oc, pp=pp):
                pe.wait(e)
                dst = outacc[:, oc * TH + tp * 512:oc * TH + (tp + 1) * 512]
                vv.inc(e.tensor_scalar(
                    out=dst, in0=pp[:], scalar1=spp[:, C_S2T:C_S2T + 1],
                    scalar2=biassum[:, oc:oc + 1],
                    op0=ALU.mult, op1=ALU.add), 1)
            VE(_)

    ph5b.close()

    ph4b = contextlib.ExitStack()
    rhsc = ph4b.enter_context(nc.sbuf_tensor("rhsc", [128, 8 * TH], BF16))
    gthr = [ph4b.enter_context(nc.sbuf_tensor(f"gthr{i}", [128, S], BF16))
            for i in range(2)]
    xin6 = [ph4b.enter_context(nc.sbuf_tensor(f"xin6{i}", [128, DM], F32))
            for i in range(2)]
    ybuf = [ph4b.enter_context(nc.sbuf_tensor(f"ybuf{i}", [128, DM], F32))
            for i in range(2)]

    for hc in range(8):
        def _(e, hc=hc):
            e.wait_ge(gg.h, ag_mk)
            if hc >= 2:
                e.wait_ge(vv.h, vv.n)
            else:
                e.wait_ge(pe.h, pe.n)
            ld.inc(e.dma_start(out=gthr[hc % 2][:],
                               in_=am_out[hc * 128:(hc + 1) * 128, :]))
        SY(_)

        def _(e, hc=hc):
            ld.wait(e)
            gt = gthr[hc % 2]
            e.tensor_scalar(out=gt[:, TH:S], in0=gt[:, TH:S],
                            scalar1=spp[:, C_BE:C_BE + 1], scalar2=None,
                            op0=ALU.mult)
            vv.inc(e.scalar_tensor_tensor(
                out=rhsc[:, hc * TH:(hc + 1) * TH], in0=gt[:, 0:TH],
                scalar=spp[:, C_AL:C_AL + 1], in1=gt[:, TH:S],
                op0=ALU.mult, op1=ALU.add), 1)
        VE(_)

    for tp in range(4):
        for oc in range(8):
            pp = m2p[oc % 2]

            def _(e, oc=oc, tp=tp, pp=pp):
                e.wait_ge(vv.h, vv.n)
                last = None
                for hc in range(8):
                    last = e.matmul(
                        pp[:],
                        outwT[:, hc * 1024 + oc * 128:hc * 1024 + (oc + 1) * 128],
                        rhsc[:, hc * TH + tp * 512:hc * TH + (tp + 1) * 512],
                        start=(hc == 0), stop=(hc == 7))
                pe.inc(last, 1)
            TE(_)

            def _(e, oc=oc, tp=tp, pp=pp):
                pe.wait(e)
                dst = outacc[:, oc * TH + tp * 512:oc * TH + (tp + 1) * 512]
                vv.inc(e.scalar_tensor_tensor(
                    out=dst, in0=pp[:], scalar=1.0,
                    in1=dst, op0=ALU.mult, op1=ALU.add), 1)
            VE(_)

        for tc in range(tp * 4, tp * 4 + 4):
            def _(e, tc=tc):
                e.wait_ge(vv.h, vv.n)
                ld.inc(e.dma_start(out=xin6[tc % 2][:],
                                   in_=xh[tc * 128:(tc + 1) * 128, :]))
            SY(_)

            def _(e, tc=tc):
                vv.wait(e)
                last = None
                for oc in range(8):
                    last = e.transpose(
                        fps[:, oc * 128:(oc + 1) * 128],
                        outacc[:, oc * TH + tc * 128:oc * TH + (tc + 1) * 128],
                        ident_f[:])
                tt.inc(last, 1)
            TE(_)

            def _(e, tc=tc):
                tt.wait(e)
                ld.wait(e)
                e.wait_ge(yy.h, yy.n)
                vv.inc(e.scalar_tensor_tensor(out=ybuf[tc % 2][:], in0=fps[:],
                                              scalar=1.0, in1=xin6[tc % 2][:],
                                              op0=ALU.mult, op1=ALU.add), 1)
            VE(_)

            def _(e, tc=tc):
                vv.wait(e)
                yy.inc(e.dma_start(out=y[tc * 128:(tc + 1) * 128, :],
                                   in_=ybuf[tc % 2][:]))
            SY(_)

    ph4b.close()
    ph46.close()

    def _(g):
        for s in sems.values():
            g.wait_ge(s.h, s.n)
    GP(_)

    st.close()
    return nc


def kernel(L0_attn_x, L0_attn_qkv_w, L0_attn_qkv_b, L0_attn_out_w, L0_attn_out_b,
           L0_ffn_W1, L0_ffn_b1, L0_ffn_W2, L0_ffn_b2):
    x = np.ascontiguousarray(np.asarray(L0_attn_x, dtype=np.float32))
    qkv_w = np.ascontiguousarray(np.asarray(L0_attn_qkv_w, dtype=np.float32))
    qkv_b = np.asarray(L0_attn_qkv_b, dtype=np.float32)
    out_w = np.ascontiguousarray(np.asarray(L0_attn_out_w, dtype=np.float32))
    out_b = np.asarray(L0_attn_out_b, dtype=np.float32)
    W1 = np.ascontiguousarray(np.asarray(L0_ffn_W1, dtype=np.float32))
    b1 = np.asarray(L0_ffn_b1, dtype=np.float32)
    W2 = np.ascontiguousarray(np.asarray(L0_ffn_W2, dtype=np.float32))
    b2 = np.asarray(L0_ffn_b2, dtype=np.float32)

    nc = build_program()
    eye_b = np.eye(128, dtype=BF)
    eye_f = np.eye(128, dtype=np.float32)
    o2 = np.zeros((128, 2), np.float32)
    o2[0:64, 0] = 1.0
    o2[64:128, 1] = 1.0
    b1h = np.ascontiguousarray(b1.reshape(32, 128).T)
    b2h = np.ascontiguousarray(b2.reshape(8, 128).T)
    obh = np.ascontiguousarray(out_b.reshape(8, 128).T)
    in_maps = []
    for c in range(8):
        b, half = c // 2, c % 2
        qw = np.ascontiguousarray(np.concatenate(
            [qkv_w[s * DM + half * 512: s * DM + half * 512 + 512, :]
             for s in range(3)], axis=0))
        qb = np.stack([qkv_b[s * DM + half * 512 + p * 128:
                             s * DM + half * 512 + (p + 1) * 128]
                       for s in range(3) for p in range(4)], axis=1)
        qbv = qkv_b[2 * DM + half * 512: 2 * DM + half * 512 + 512].reshape(1, 512)
        hs = np.array([[1.0, 0.0]] if half == 0 else [[0.0, 1.0]], np.float32)
        in_maps.append({
            "xb": np.ascontiguousarray(x[b].reshape(S, DM)),
            "xh": np.ascontiguousarray(
                x[b, half * TH:(half + 1) * TH].reshape(TH, DM)),
            "qkvw": qw,
            "qkvb": np.ascontiguousarray(qb.astype(np.float32)),
            "qkvbv": np.ascontiguousarray(qbv.astype(np.float32)),
            "outw": out_w,
            "outb": obh,
            "w1": W1, "b1": b1h, "w2": W2, "b2": b2h,
            "idbf": eye_b, "idf32": eye_f,
            "ones2": o2, "ones2t": np.ascontiguousarray(o2.T),
            "hsel": hs,
        })
    res = run_bass_kernel_spmd(nc, in_maps, core_ids=list(range(8)))
    out = np.empty((4, S, DM), dtype=np.float32)
    for c in range(8):
        b, half = c // 2, c % 2
        out[b, half * TH:(half + 1) * TH] = res.results[c]["y"]
    return out


# version 21
# speedup vs baseline: 1.0403x; 1.0257x over previous
import contextlib
import numpy as np
import ml_dtypes
import concourse.bass as bass
import concourse.mybir as mybir
from concourse.bass_utils import run_bass_kernel_spmd

F32 = mybir.dt.float32
BF16 = mybir.dt.bfloat16
AF = mybir.ActivationFunctionType
ALU = mybir.AluOpType
AX = mybir.AxisListType.X
BF = ml_dtypes.bfloat16

S = 4096
DM = 1024
FH = 4096
TH = 2048
EPS = 1e-7
MAGIC = 12582912.0
KLN = float(1.0 + np.pi / 2.0)
C_SW1, C_SW2, C_ISW1, C_ISW2, C_SX, C_ISX, C_LOX, C_HIX = range(8)
C_S1T, C_SH, C_ISH, C_LOH, C_HIH, C_S2T, C_AL, C_BE = range(8, 16)


class Sem:
    def __init__(self, stack, nc, name):
        self.h = stack.enter_context(nc.semaphore(name))
        self.n = 0

    def inc(self, instr, amt=16):
        instr.then_inc(self.h, amt)
        self.n += amt
        return instr

    def wait(self, eng, n=None):
        eng.wait_ge(self.h, self.n if n is None else n)


def build_program(debug=()):
    nc = bass.Bass("TRN2", target_bir_lowering=False)
    st = contextlib.ExitStack()

    def dk(name):
        return "ExternalOutput" if name in debug else "Internal"

    xb = nc.dram_tensor("xb", [S, DM], F32, kind="ExternalInput").ap()
    xh = nc.dram_tensor("xh", [TH, DM], F32, kind="ExternalInput").ap()
    qkvw = nc.dram_tensor("qkvw", [3 * 512, DM], F32, kind="ExternalInput").ap()
    qkvb = nc.dram_tensor("qkvb", [128, 12], F32, kind="ExternalInput").ap()
    qkvbv = nc.dram_tensor("qkvbv", [1, 512], F32, kind="ExternalInput").ap()
    outw = nc.dram_tensor("outw", [DM, DM], F32, kind="ExternalInput").ap()
    outb = nc.dram_tensor("outb", [128, 8], F32, kind="ExternalInput").ap()
    w1 = nc.dram_tensor("w1", [FH, DM], F32, kind="ExternalInput").ap()
    b1 = nc.dram_tensor("b1", [128, 32], F32, kind="ExternalInput").ap()
    w2 = nc.dram_tensor("w2", [DM, FH], F32, kind="ExternalInput").ap()
    b2 = nc.dram_tensor("b2", [128, 8], F32, kind="ExternalInput").ap()
    idbf = nc.dram_tensor("idbf", [128, 128], BF16, kind="ExternalInput").ap()
    idf32 = nc.dram_tensor("idf32", [128, 128], F32, kind="ExternalInput").ap()
    ones2 = nc.dram_tensor("ones2", [128, 2], F32, kind="ExternalInput").ap()
    ones2t = nc.dram_tensor("ones2t", [2, 128], F32, kind="ExternalInput").ap()
    hsel = nc.dram_tensor("hsel", [1, 2], F32, kind="ExternalInput").ap()
    y = nc.dram_tensor("y", [TH, DM], F32, kind="ExternalOutput").ap()

    xnT_h = nc.dram_tensor("xnT_h", [DM, S], BF16, kind=dk("xnT_h")).ap()
    h_h = nc.dram_tensor("h_h", [FH, TH], F32, kind=dk("h_h")).ap()
    hq_h = nc.dram_tensor("hq_h", [FH, TH], BF16, kind=dk("hq_h")).ap()
    w2qT_h = nc.dram_tensor("w2qT_h", [8 * FH, 128], BF16, kind=dk("w2qT_h")).ap()
    am_in = nc.dram_tensor("am_in", [512, S], BF16).ap()
    am_out = nc.dram_tensor("am_out", [DM, S], BF16).ap()
    amdbg = (nc.dram_tensor("amdbg", [512, S], BF16, kind="ExternalOutput").ap()
             if "amdbg" in debug else None)
    qkdbg = (nc.dram_tensor("qkdbg", [256, S], BF16, kind="ExternalOutput").ap()
             if "qkdbg" in debug else None)
    stdbg = (nc.dram_tensor("stdbg", [512, S], BF16, kind="ExternalOutput").ap()
             if "qkdbg" in debug else None)
    csdbg = (nc.dram_tensor("csdbg", [128, 16], F32, kind="ExternalOutput").ap()
             if "qkdbg" in debug else None)
    vxdbg = (nc.dram_tensor("vxdbg", [128, 4 * 68], BF16, kind="ExternalOutput").ap()
             if "qkdbg" in debug else None)
    nmdbg = (nc.dram_tensor("nmdbg", [65, S], F32, kind="ExternalOutput").ap()
             if "qkdbg" in debug else None)
    cc1i = nc.dram_tensor("cc1i", [1, 8], F32).ap()
    cc1o = nc.dram_tensor("cc1o", [1, 8], F32).ap()
    cc2i = nc.dram_tensor("cc2i", [1, 8], F32).ap()
    cc2o = nc.dram_tensor("cc2o", [1, 8], F32).ap()
    scdbg = nc.dram_tensor("scdbg", [3, 16], F32, kind=dk("scdbg")).ap()

    def sbuf(name, shape, dtype):
        return st.enter_context(nc.sbuf_tensor(name, shape, dtype))

    ident_b = sbuf("ident_b", [128, 128], BF16)
    ident_f = sbuf("ident_f", [128, 128], F32)
    ones2_s = sbuf("ones2_s", [128, 2], F32)
    ones2t_s = sbuf("ones2t_s", [2, 128], F32)
    onesrow = sbuf("onesrow", [1, 128], F32)
    onesrow_b = sbuf("onesrow_b", [1, 128], BF16)
    onesr64 = sbuf("onesr64", [65, 64], F32)
    bias1 = sbuf("bias1", [128, 32], F32)
    bias2 = sbuf("bias2", [128, 8], F32)
    biaso = sbuf("biaso", [128, 8], F32)
    biasqkv = sbuf("biasqkv", [128, 12], F32)
    bv32 = sbuf("bv32", [1, 512], F32)
    bvrow = sbuf("bvrow", [1, 512], BF16)
    spp = sbuf("spp", [128, 16], F32)
    scal = sbuf("scal", [1, 16], F32)
    ccs = sbuf("ccs", [1, 8], F32)
    ccr = sbuf("ccr", [1, 8], F32)
    stats = sbuf("stats", [128, 80], F32)
    wstat = sbuf("wstat", [128, 40], F32)
    sc1a = sbuf("sc1a", [128, 1], F32)
    sc1b = sbuf("sc1b", [128, 1], F32)
    fsc = sbuf("fsc", [1, 2], F32)
    ctile = sbuf("ctile", [128, 3], F32)
    nc.const_aps.register(F32, 1.0, ctile[:, 0:1])
    nc.const_aps.register(F32, KLN, ctile[:, 1:2])
    nc.const_aps.register(F32, 13.862943611198906, ctile[:, 2:3])

    sems = {}
    for nm in ("ld", "sp", "vv", "ss", "tt", "gg", "c1", "c2", "c3", "yy", "pe"):
        sems[nm] = Sem(st, nc, nm)
    ld, sp, vv, ss, tt, gg = (sems[k] for k in ("ld", "sp", "vv", "ss", "tt", "gg"))
    c1, c2, c3, yy, pe = (sems[k] for k in ("c1", "c2", "c3", "yy", "pe"))

    blk = st.enter_context(nc.Block())

    def SY(f): blk.sync(f)
    def VE(f): blk.vector(f)
    def SC(f): blk.scalar(f)
    def TE(f): blk.tensor(f)
    def GP(f): blk.gpsimd(f)

    def _(g):
        g.memset(onesrow[:], 1.0)
        g.memset(onesrow_b[:], 1.0)
        g.memset(onesr64[:], 1.0)
        g.memset(ctile[:, 0:1], 1.0)
        g.memset(ctile[:, 1:2], KLN)
        g.memset(ctile[:, 2:3], 13.862943611198906)
        g.memset(fsc[:], 0.0)
        g.memset(scal[:], 0.0)
        gg.inc(g.memset(ccs[:], 0.0), 1)
    GP(_)

    def _(e):
        for dst, src in ((ident_b, idbf), (ident_f, idf32), (ones2_s, ones2),
                         (ones2t_s, ones2t), (bias1, b1), (bias2, b2),
                         (biaso, outb), (biasqkv, qkvb), (bv32, qkvbv)):
            ld.inc(e.dma_start(out=dst[:], in_=src[:]))
        ld.inc(e.dma_start(out=scal[0:1, C_AL:C_BE + 1], in_=hsel[:]))
    SY(_)

    def _(e):
        ld.wait(e)
        gg.wait(e)
        vv.inc(e.tensor_copy(out=bvrow[:], in_=bv32[:]), 1)
    VE(_)

    ph01 = contextlib.ExitStack()

    def sb01(name, shape, dtype):
        return ph01.enter_context(nc.sbuf_tensor(name, shape, dtype))

    def ps01(name, shape, dtype=F32):
        return ph01.enter_context(nc.psum_tensor(name, shape, dtype))

    xin = [sb01(f"xin{i}", [128, DM], F32) for i in range(4)]
    xnf = sb01("xnf", [128, DM], F32)
    sqt = sb01("sqt", [128, DM], F32)
    xnb = sb01("xnb", [128, DM], BF16)
    stg = sb01("stg", [128, DM], BF16)
    xpose = ps01("xpose", [128, 1024], BF16)
    pmm1 = ps01("pmm1", [1, 128], F32)
    pmm2 = ps01("pmm2", [1, 128], F32)
    pbc = ps01("pbc", [128, 16], F32)

    xnT_w = xnT_h.rearrange("(a p) t -> p a t", p=128)

    ln_tiny_mk = [0]

    def emit_ln_stats_pair(tc0, src_ap, marks):
        def _(e):
            if len(marks) >= 3:
                e.wait_ge(vv.h, marks[-3])
            elif len(marks) == 0:
                e.wait_ge(vv.h, vv.n)
            for tc in (tc0, tc0 + 1):
                ld.inc(e.dma_start(out=xin[tc % 4][:],
                                   in_=src_ap[tc * 128:(tc + 1) * 128, :]))
        SY(_)

        def _(e):
            ld.wait(e)
            e.wait_ge(vv.h, ln_tiny_mk[0])
            for i, tc in enumerate((tc0, tc0 + 1)):
                c0 = 72 * i
                e.activation(out=sqt[:], in_=xin[tc % 4][:], func=AF.Copy,
                             accum_out=stats[:, c0:c0 + 1])
                e.activation(out=sqt[:], in_=xin[tc % 4][:], func=AF.Square,
                             accum_out=stats[:, c0 + 1:c0 + 2])
            ss.inc(e.copy(out=fsc[0:1, 0:1], in_=fsc[0:1, 1:2]), 1)
        SC(_)
        stats_mk = ss.n

        def _(e, stats_mk=stats_mk):
            e.wait_ge(ss.h, stats_mk)

            def step(ins):
                vv.inc(ins, 1)
                vv.wait(e)
            for i in range(2):
                c0 = 72 * i
                e.tensor_scalar(out=stats[:, c0 + 2:c0 + 3],
                                in0=stats[:, c0:c0 + 1],
                                scalar1=1.0 / DM, scalar2=None, op0=ALU.mult)
                step(e.tensor_scalar(out=stats[:, c0 + 3:c0 + 4],
                                     in0=stats[:, c0 + 1:c0 + 2],
                                     scalar1=1.0 / DM, scalar2=1e-5,
                                     op0=ALU.mult, op1=ALU.add))
                step(e.scalar_tensor_tensor(out=stats[:, c0 + 4:c0 + 5],
                                            in0=stats[:, c0 + 2:c0 + 3],
                                            scalar=0.0,
                                            in1=stats[:, c0 + 2:c0 + 3],
                                            op0=ALU.add, op1=ALU.mult))
                step(e.tensor_sub(stats[:, c0 + 4:c0 + 5],
                                  stats[:, c0 + 3:c0 + 4],
                                  stats[:, c0 + 4:c0 + 5]))
            vv.inc(e.tensor_copy(out=fsc[0:1, 0:1], in_=fsc[0:1, 1:2]), 1)
        VE(_)
        ln_tiny_mk[0] = vv.n

        def _(e):
            vv.wait(e)
            e.activation(out=stats[:, 5:6], in_=stats[:, 4:5], func=AF.Ln)
            ss.inc(e.activation(out=stats[:, 77:78], in_=stats[:, 76:77],
                                func=AF.Ln), 1)
            ss.wait(e)
            e.activation(out=stats[:, 6:7], in_=stats[:, 5:6],
                         func=AF.Exp, scale=-0.5)
            e.activation(out=stats[:, 78:79], in_=stats[:, 77:78],
                         func=AF.Exp, scale=-0.5)
            ss.inc(e.copy(out=fsc[0:1, 0:1], in_=fsc[0:1, 1:2]), 1)
        SC(_)

    def emit_ln_chunk(tc, src_ap, do_spill, marks):
        c0 = 72 * (tc % 2)

        def _(e, c0=c0):
            ss.wait(e)
            xt = xin[tc % 4]
            e.tensor_scalar(out=xnf[:], in0=xt[:],
                            scalar1=stats[:, c0 + 2:c0 + 3],
                            scalar2=stats[:, c0 + 6:c0 + 7], op0=ALU.subtract,
                            op1=ALU.mult)
            if do_spill:
                e.tensor_reduce(out=stats[:, 8 + tc:9 + tc], in_=xnf[:],
                                axis=AX, op=ALU.max)
                e.tensor_reduce(out=stats[:, 40 + tc:41 + tc], in_=xnf[:],
                                axis=AX, op=ALU.min)
                e.wait_ge(sp.h, sp.n)
                e.wait_ge(tt.h, tt.n)
                e.tensor_copy(out=xnb[:], in_=xnf[:])
            vv.inc(e.tensor_copy(out=sqt[:, 0:1], in_=xnf[:, 0:1]), 1)
        VE(_)
        marks.append(vv.n)

        if do_spill:
            def _(e):
                vv.wait(e)
                last = None
                for dc in range(8):
                    last = e.transpose(xpose[:, dc * 128:(dc + 1) * 128],
                                       xnb[:, dc * 128:(dc + 1) * 128],
                                       ident_b[:])
                tt.inc(last, 1)
            TE(_)

            def _(e):
                tt.wait(e)
                vv.inc(e.tensor_copy(out=stg[:], in_=xpose[:]), 1)
            VE(_)
            marks[-1] = vv.n

            def _(e, tc=tc):
                vv.wait(e)
                sp.inc(e.dma_start(
                    out=xnT_w[:, :, tc * 128:(tc + 1) * 128],
                    in_=stg[:].rearrange("p (a q) -> p a q", q=128)))
            SY(_)

    ln_marks0 = []
    for tc0 in range(0, 32, 2):
        emit_ln_stats_pair(tc0, xb, ln_marks0)
        emit_ln_chunk(tc0, xb, do_spill=True, marks=ln_marks0)
        emit_ln_chunk(tc0 + 1, xb, do_spill=True, marks=ln_marks0)

    wld = [sb01(f"wld{i}", [128, FH], F32) for i in range(2)]
    w_marks = []

    def emit_w_absmax(gi, src_ap, cols):
        def _(g):
            if len(w_marks) >= 2:
                g.wait_ge(vv.h, w_marks[-2])
            gg.inc(g.dma_start(out=wld[gi % 2][0:128, 0:cols], in_=src_ap))
        GP(_)
        wld_mk = gg.n

        def _(e, wld_mk=wld_mk):
            e.wait_ge(gg.h, wld_mk)
            vv.inc(e.tensor_reduce(out=wstat[:, gi:gi + 1],
                                   in_=wld[gi % 2][0:128, 0:cols], axis=AX,
                                   op=ALU.max, apply_absolute_value=True), 1)
        VE(_)
        w_marks.append(vv.n)

    for ci in range(32):
        emit_w_absmax(ci, w1[ci * 128:(ci + 1) * 128, :], DM)
    for ci in range(8):
        emit_w_absmax(32 + ci, w2[ci * 128:(ci + 1) * 128, :], FH)

    def _(e):
        e.tensor_reduce(out=sc1a[:], in_=stats[:, 8:40], axis=AX, op=ALU.max)
        vv.inc(e.tensor_reduce(out=sc1b[:], in_=stats[:, 40:72], axis=AX,
                               op=ALU.min), 1)
    VE(_)

    def _(e):
        vv.wait(e)
        e.transpose(pmm1[:], sc1a[:], ident_f[:])
        tt.inc(e.transpose(pmm2[:], sc1b[:], ident_f[:]), 1)
    TE(_)

    def _(e):
        tt.wait(e)
        e.tensor_reduce(out=ccs[0:1, 1:2], in_=pmm1[:], axis=AX, op=ALU.max)
        vv.inc(e.tensor_reduce(out=ccs[0:1, 2:3], in_=pmm2[:], axis=AX,
                               op=ALU.min), 1)
        vv.wait(e)
        e.tensor_scalar(out=ccs[0:1, 0:1], in0=ccs[0:1, 2:3],
                        scalar1=-1.0, scalar2=None, op0=ALU.mult)
        vv.inc(e.tensor_copy(out=fsc[0:1, 0:1], in_=fsc[0:1, 1:2]), 1)
    VE(_)

    def _(g):
        vv.wait(g)
        gg.inc(g.dma_start(out=cc1i[:, :], in_=ccs[0:1, :]))
        gg.wait(g)
        c1.inc(g.collective_compute(
            "AllReduce", ALU.max, replica_groups=[list(range(8))],
            ins=[cc1i.opt()], outs=[cc1o.opt()]), 1)
        g.wait_ge(c1.h, 1)
        gg.inc(g.dma_start(out=ccr[0:1, :], in_=cc1o[:, :]))
    GP(_)

    def _(e):
        e.wait_ge(tt.h, tt.n)
        e.tensor_reduce(out=sc1a[:], in_=wstat[:, 0:32], axis=AX, op=ALU.max)
        vv.inc(e.tensor_reduce(out=sc1b[:], in_=wstat[:, 32:40], axis=AX,
                               op=ALU.max), 1)
    VE(_)

    def _(e):
        vv.wait(e)
        e.transpose(pmm1[:], sc1a[:], ident_f[:])
        tt.inc(e.transpose(pmm2[:], sc1b[:], ident_f[:]), 1)
    TE(_)

    def _(e):
        tt.wait(e)
        e.tensor_reduce(out=scal[0:1, C_SW1:C_SW1 + 1], in_=pmm1[:],
                        axis=AX, op=ALU.max)
        vv.inc(e.tensor_reduce(out=scal[0:1, C_SW2:C_SW2 + 1], in_=pmm2[:],
                               axis=AX, op=ALU.max), 1)
        vv.wait(e)
        e.tensor_scalar(out=scal[0:1, C_SW1:C_SW2 + 1],
                        in0=scal[0:1, C_SW1:C_SW2 + 1],
                        scalar1=1.0 / 127.0, scalar2=1e-8,
                        op0=ALU.mult, op1=ALU.max)
        vv.inc(e.tensor_copy(out=fsc[0:1, 0:1], in_=fsc[0:1, 1:2]), 1)
    VE(_)

    def _(e):
        vv.wait(e)
        ss.inc(e.activation(out=scal[0:1, C_ISW1:C_ISW2 + 1],
                            in_=scal[0:1, C_SW1:C_SW2 + 1], func=AF.Ln), 1)
        ss.wait(e)
        e.activation(out=scal[0:1, C_ISW1:C_ISW2 + 1],
                     in_=scal[0:1, C_ISW1:C_ISW2 + 1],
                     func=AF.Exp, scale=-1.0)
        ss.inc(e.copy(out=fsc[0:1, 0:1], in_=fsc[0:1, 1:2]), 1)
    SC(_)

    def bcast_spp(pbc_t):
        def _(e):
            ss.wait(e)
            vv.wait(e)
            ld.wait(e)
            tt.inc(e.matmul(pbc_t[:], onesrow[:], scal[:],
                            start=True, stop=True), 1)
        TE(_)

        def _(e):
            tt.wait(e)
            e.tensor_copy(out=spp[:], in_=pbc_t[:])
            vv.inc(e.tensor_copy(out=fsc[0:1, 0:1], in_=fsc[0:1, 1:2]), 1)
        VE(_)

    bcast_spp(pbc)

    wqb = sb01("wqb", [128, FH], BF16)
    w1qT = sb01("w1qT", [128, 8 * FH], BF16)
    w2st = sb01("w2st", [128, 4096], BF16)
    w1qT_w = w1qT[:].rearrange("p (a h) -> p a h", a=8)
    w2qT_w = w2qT_h.rearrange("(a p) q -> p a q", p=128)

    def emit_w_quant(gi, src_ap, cols, iswcol, is_w2, ci):
        def _(e):
            if len(w_marks) >= 2:
                e.wait_ge(vv.h, w_marks[-2])
            ld.inc(e.dma_start(out=wld[gi % 2][0:128, 0:cols], in_=src_ap))
        SY(_)

        def _(e):
            ld.wait(e)
            e.wait_ge(tt.h, tt.n)
            wt = wld[gi % 2][0:128, 0:cols]
            e.tensor_scalar(out=wt, in0=wt, scalar1=spp[:, iswcol:iswcol + 1],
                            scalar2=MAGIC, op0=ALU.mult, op1=ALU.add)
            e.tensor_scalar(out=wt, in0=wt, scalar1=MAGIC, scalar2=-127.0,
                            op0=ALU.subtract, op1=ALU.max)
            vv.inc(e.tensor_scalar(out=wqb[0:128, 0:cols], in0=wt,
                                   scalar1=127.0, scalar2=None, op0=ALU.min), 1)
        VE(_)
        w_marks.append(vv.n)

        for g0 in range(0, cols // 128, 8):
            def _(e, g0=g0):
                vv.wait(e)
                last = None
                for k in range(8):
                    last = e.transpose(xpose[:, k * 128:(k + 1) * 128],
                                       wqb[:, (g0 + k) * 128:(g0 + k + 1) * 128],
                                       ident_b[:])
                tt.inc(last, 1)
            TE(_)

            if not is_w2:
                def _(e, ci=ci):
                    tt.wait(e)
                    vv.inc(e.tensor_copy(
                        out=w1qT_w[:, :, ci * 128:(ci + 1) * 128],
                        in_=xpose[:].rearrange("p (a q) -> p a q", q=128)), 1)
                VE(_)
            else:
                def _(e, g0=g0):
                    tt.wait(e)
                    if g0 == 0:
                        e.wait_ge(sp.h, sp.n)
                    vv.inc(e.tensor_copy(out=w2st[:, g0 * 128:(g0 + 8) * 128],
                                         in_=xpose[:]), 1)
                VE(_)
        if is_w2:
            def _(e, ci=ci):
                vv.wait(e)
                sp.inc(e.dma_start(
                    out=w2qT_w[:, ci * 32:(ci + 1) * 32, :],
                    in_=w2st[:].rearrange("p (a q) -> p a q", q=128)))
            SY(_)

    for ci in range(32):
        emit_w_quant(32 + 8 + ci, w1[ci * 128:(ci + 1) * 128, :], DM,
                     C_ISW1, False, ci)
    for ci in range(8):
        emit_w_quant(32 + 8 + 32 + ci, w2[ci * 128:(ci + 1) * 128, :], FH,
                     C_ISW2, True, ci)

    def _(e):
        gg.wait(e)
        vv.inc(e.tensor_add(scal[0:1, C_SX:C_SX + 1], ccr[0:1, 0:1],
                            ccr[0:1, 1:2]), 1)
        vv.wait(e)
        vv.inc(e.tensor_scalar(out=scal[0:1, C_SX:C_SX + 1],
                               in0=scal[0:1, C_SX:C_SX + 1],
                               scalar1=1.0 / 255.0, scalar2=1e-8,
                               op0=ALU.mult, op1=ALU.max), 1)
    VE(_)

    def _(e):
        vv.wait(e)
        ss.inc(e.activation(out=scal[0:1, C_ISX:C_ISX + 1],
                            in_=scal[0:1, C_SX:C_SX + 1], func=AF.Ln), 1)
        ss.wait(e)
        e.activation(out=scal[0:1, C_ISX:C_ISX + 1],
                     in_=scal[0:1, C_ISX:C_ISX + 1],
                     func=AF.Exp, scale=-1.0)
        ss.inc(e.copy(out=fsc[0:1, 0:1], in_=fsc[0:1, 1:2]), 1)
    SC(_)

    def emit_zx(lo_col, hi_col, s_col, isx_col, stot_col, sw_col):
        def _(e):
            ss.wait(e)

            def step(ins):
                vv.inc(ins, 1)
                vv.wait(e)
            step(e.tensor_mul(scal[0:1, lo_col:lo_col + 1], ccr[0:1, 0:1],
                              scal[0:1, isx_col:isx_col + 1]))
            step(e.tensor_scalar(out=scal[0:1, lo_col:lo_col + 1],
                                 in0=scal[0:1, lo_col:lo_col + 1],
                                 scalar1=MAGIC, scalar2=MAGIC,
                                 op0=ALU.add, op1=ALU.subtract))
            step(e.tensor_scalar(out=scal[0:1, lo_col:lo_col + 1],
                                 in0=scal[0:1, lo_col:lo_col + 1],
                                 scalar1=-128.0, scalar2=-128.0,
                                 op0=ALU.add, op1=ALU.max))
            step(e.tensor_scalar(out=scal[0:1, lo_col:lo_col + 1],
                                 in0=scal[0:1, lo_col:lo_col + 1],
                                 scalar1=127.0, scalar2=None, op0=ALU.min))
            step(e.tensor_scalar(out=scal[0:1, hi_col:hi_col + 1],
                                 in0=scal[0:1, lo_col:lo_col + 1],
                                 scalar1=-1.0, scalar2=127.0,
                                 op0=ALU.mult, op1=ALU.add))
            step(e.tensor_scalar(out=scal[0:1, lo_col:lo_col + 1],
                                 in0=scal[0:1, lo_col:lo_col + 1],
                                 scalar1=-1.0, scalar2=-128.0,
                                 op0=ALU.mult, op1=ALU.add))
            e.tensor_mul(scal[0:1, stot_col:stot_col + 1],
                         scal[0:1, s_col:s_col + 1],
                         scal[0:1, sw_col:sw_col + 1])
            vv.inc(e.tensor_copy(out=fsc[0:1, 0:1], in_=fsc[0:1, 1:2]), 1)
        VE(_)

    emit_zx(C_LOX, C_HIX, C_SX, C_ISX, C_S1T, C_SW1)
    bcast_spp(pbc)

    def _(e):
        vv.wait(e)
        sp.inc(e.dma_start(out=scdbg[0:1, :], in_=scal[0:1, :]))
        sp.inc(e.dma_start(out=scdbg[1:2, 0:8], in_=ccs[0:1, :]))
        sp.inc(e.dma_start(out=scdbg[2:3, 0:8], in_=ccr[0:1, :]))
    SY(_)

    xqT = sb01("xqT", [128, 8 * TH], BF16)
    xqb = sb01("xqb", [128, DM], BF16)
    xqT_w = xqT[:].rearrange("p (a t) -> p a t", a=8)
    ln_marks1 = []
    for tc in range(16):
        if tc % 2 == 0:
            emit_ln_stats_pair(tc, xh, ln_marks1)
        emit_ln_chunk(tc, xh, do_spill=False, marks=ln_marks1)

        def _(e, tc=tc):
            e.wait_ge(tt.h, tt.n)
            e.tensor_scalar(out=xnf[:], in0=xnf[:],
                            scalar1=spp[:, C_ISX:C_ISX + 1], scalar2=MAGIC,
                            op0=ALU.mult, op1=ALU.add)
            e.tensor_scalar(out=xnf[:], in0=xnf[:], scalar1=MAGIC,
                            scalar2=spp[:, C_LOX:C_LOX + 1],
                            op0=ALU.subtract, op1=ALU.max)
            vv.inc(e.tensor_scalar(out=xqb[:], in0=xnf[:],
                                   scalar1=spp[:, C_HIX:C_HIX + 1],
                                   scalar2=None, op0=ALU.min), 1)
        VE(_)

        def _(e, tc=tc):
            vv.wait(e)
            last = None
            for dc in range(8):
                last = e.transpose(xpose[:, dc * 128:(dc + 1) * 128],
                                   xqb[:, dc * 128:(dc + 1) * 128], ident_b[:])
            tt.inc(last, 1)
        TE(_)

        def _(e, tc=tc):
            tt.wait(e)
            vv.inc(e.tensor_copy(
                out=xqT_w[:, :, tc * 128:(tc + 1) * 128],
                in_=xpose[:].rearrange("p (a q) -> p a q", q=128)), 1)
        VE(_)
        ln_marks1[-1] = vv.n

    hst = sb01("hst", [128, TH], F32)
    m1p = [ps01(f"m1p{i}", [128, 512], F32) for i in range(4)]
    mm1_ready = vv.n
    gelu_mk = {}

    for hc in range(32):
        for tp in range(4):
            g = hc * 4 + tp
            pp = m1p[g % 4]

            def _(e, hc=hc, tp=tp, pp=pp, g=g):
                e.wait_ge(vv.h, mm1_ready)
                e.wait_ge(ss.h, gelu_mk.get(g - 4, 0))
                last = None
                for dc in range(8):
                    last = e.matmul(
                        pp[:],
                        w1qT[:, dc * FH + hc * 128:dc * FH + (hc + 1) * 128],
                        xqT[:, dc * TH + tp * 512:dc * TH + (tp + 1) * 512],
                        start=(dc == 0), stop=(dc == 7))
                pe.inc(last, 1)
            TE(_)

            def _(e, hc=hc, tp=tp, pp=pp, g=g):
                pe.wait(e)
                if tp == 0:
                    e.wait_ge(sp.h, sp.n)
                    e.wait_ge(vv.h, vv.n)
                ss.inc(e.activation(out=hst[:, tp * 512:(tp + 1) * 512],
                                    in_=pp[:], func=AF.Gelu,
                                    scale=spp[:, C_S1T:C_S1T + 1],
                                    bias=bias1[:, hc:hc + 1]), 1)
                gelu_mk[g] = ss.n
            SC(_)

        def _(e, hc=hc):
            ss.wait(e)
            e.tensor_reduce(out=stats[:, 8 + hc:9 + hc], in_=hst[:], axis=AX,
                            op=ALU.max)
            vv.inc(e.tensor_reduce(out=stats[:, 40 + hc:41 + hc], in_=hst[:],
                                   axis=AX, op=ALU.min), 1)
        VE(_)

        def _(e, hc=hc):
            vv.wait(e)
            sp.inc(e.dma_start(out=h_h[hc * 128:(hc + 1) * 128, :], in_=hst[:]))
        SY(_)

    def _(e):
        e.tensor_reduce(out=sc1a[:], in_=stats[:, 8:40], axis=AX, op=ALU.max)
        vv.inc(e.tensor_reduce(out=sc1b[:], in_=stats[:, 40:72], axis=AX,
                               op=ALU.min), 1)
    VE(_)

    def _(e):
        vv.wait(e)
        e.transpose(pmm1[:], sc1a[:], ident_f[:])
        tt.inc(e.transpose(pmm2[:], sc1b[:], ident_f[:]), 1)
    TE(_)

    def _(e):
        tt.wait(e)
        e.tensor_reduce(out=ccs[0:1, 1:2], in_=pmm1[:], axis=AX, op=ALU.max)
        vv.inc(e.tensor_reduce(out=ccs[0:1, 2:3], in_=pmm2[:], axis=AX,
                               op=ALU.min), 1)
        vv.wait(e)
        e.tensor_scalar(out=ccs[0:1, 0:1], in0=ccs[0:1, 2:3],
                        scalar1=-1.0, scalar2=None, op0=ALU.mult)
        vv.inc(e.tensor_copy(out=fsc[0:1, 0:1], in_=fsc[0:1, 1:2]), 1)
    VE(_)

    def _(g):
        vv.wait(g)
        gg.inc(g.dma_start(out=cc2i[:, :], in_=ccs[0:1, :]))
        gg.wait(g)
        c2.inc(g.collective_compute(
            "AllReduce", ALU.max, replica_groups=[list(range(8))],
            ins=[cc2i.opt()], outs=[cc2o.opt()]), 1)
        g.wait_ge(c2.h, 1)
        gg.inc(g.dma_start(out=ccr[0:1, :], in_=cc2o[:, :]))
    GP(_)
    ccr2_mk = gg.n
    h_done_mk = sp.n

    ph01.close()

    ph2 = contextlib.ExitStack()

    def sb2(name, shape, dtype):
        return ph2.enter_context(nc.sbuf_tensor(name, shape, dtype))

    scoreT = sb2("scoreT", [128, 4 * S], BF16)
    qhat = sb2("qhat", [128, S], BF16)
    khat = sb2("khat", [128, S], BF16)
    vtok = sb2("vtok", [128, 32 * 128], BF16)
    vext = sb2("vext", [128, 4 * 68], BF16)
    numer = sb2("numer", [65, S], F32)
    colss = sb2("colss", [128, 8], F32)
    coltl = sb2("coltl", [128, 4], F32)
    rnq = sb2("rnq", [2, TH], F32)
    atst = [sb2(f"atst{i}", [64, 512], BF16) for i in range(2)]

    hstq = [sb2(f"hstq{i}", [128, 1024], F32) for i in range(2)]
    hqtq = sb2("hqtq", [128, 1024], BF16)
    hq_lmk = {}
    hq_qmk = {}
    hq_smk = {}
    hq_state = [0, 0]

    def emit_hq_load(j):
        hc, t2 = j // 2, j % 2

        def _(e, j=j, hc=hc, t2=t2):
            e.wait_ge(sp.h, h_done_mk)
            if j >= 2:
                e.wait_ge(vv.h, hq_qmk[j - 2])
            ld.inc(e.dma_start(out=hstq[j % 2][:],
                               in_=h_h[hc * 128:(hc + 1) * 128,
                                       t2 * 1024:(t2 + 1) * 1024]))
            hq_lmk[j] = ld.n
        SY(_)

    def emit_hq_quant(j):
        def _(e, j=j):
            e.wait_ge(ld.h, hq_lmk[j])
            e.wait_ge(sp.h, hq_smk.get(j - 1, 0))
            hs = hstq[j % 2]
            e.tensor_scalar(out=hs[:], in0=hs[:],
                            scalar1=spp[:, C_ISH:C_ISH + 1], scalar2=MAGIC,
                            op0=ALU.mult, op1=ALU.add)
            e.tensor_scalar(out=hs[:], in0=hs[:], scalar1=MAGIC,
                            scalar2=spp[:, C_LOH:C_LOH + 1],
                            op0=ALU.subtract, op1=ALU.max)
            vv.inc(e.tensor_scalar(out=hqtq[:], in0=hs[:],
                                   scalar1=spp[:, C_HIH:C_HIH + 1],
                                   scalar2=None, op0=ALU.min), 1)
            hq_qmk[j] = vv.n
        VE(_)

    def emit_hq_spill(j):
        hc, t2 = j // 2, j % 2

        def _(e, j=j, hc=hc, t2=t2):
            e.wait_ge(vv.h, hq_qmk[j])
            sp.inc(e.dma_start(out=hq_h[hc * 128:(hc + 1) * 128,
                                        t2 * 1024:(t2 + 1) * 1024],
                               in_=hqtq[:]))
            hq_smk[j] = sp.n
        SY(_)

    def emit_hq_step():
        j, g = hq_state
        if g % 3 == 0 and j < 64:
            emit_hq_load(j)
        elif g % 3 == 2 and j < 64:
            emit_hq_quant(j)
            emit_hq_spill(j)
            hq_state[0] = j + 1
        hq_state[1] = g + 1

    for p in range(4):
        pstart_vv, pstart_ss = vv.n, ss.n
        prj = contextlib.ExitStack()
        xnp = prj.enter_context(nc.sbuf_tensor(f"xnp{p}", [128, 8 * TH], BF16))
        wqkT = prj.enter_context(nc.sbuf_tensor(f"wqkT{p}", [128, 3 * 1024], BF16))
        qf = prj.enter_context(nc.sbuf_tensor(f"qf{p}", [128, TH], F32))
        kf = prj.enter_context(nc.sbuf_tensor(f"kf{p}", [128, TH], F32))
        sqp = prj.enter_context(nc.sbuf_tensor(f"sqp{p}", [128, 512], F32))
        wl32 = prj.enter_context(nc.sbuf_tensor(f"wl32{p}", [128, DM], F32))
        wlb = prj.enter_context(nc.sbuf_tensor(f"wlb{p}", [128, DM], BF16))
        xp2 = prj.enter_context(nc.psum_tensor(f"xp2{p}", [128, 1024], BF16))
        qp = [prj.enter_context(nc.psum_tensor(f"qp{p}{i}", [128, 512], F32))
              for i in range(2)]
        npp = prj.enter_context(nc.psum_tensor(f"npp{p}", [2, 512], F32))
        bcp = prj.enter_context(nc.psum_tensor(f"bcp{p}", [128, 512], F32))
        vp = [prj.enter_context(nc.psum_tensor(f"vp{p}{i}", [128, 512], F32))
              for i in range(2)]

        if p == 1:
            pbch = prj.enter_context(nc.psum_tensor("pbch", [128, 16], F32))

            def _(e):
                e.wait_ge(gg.h, ccr2_mk)
                vv.inc(e.tensor_add(scal[0:1, C_SH:C_SH + 1], ccr[0:1, 0:1],
                                    ccr[0:1, 1:2]), 1)
                vv.wait(e)
                vv.inc(e.tensor_scalar(out=scal[0:1, C_SH:C_SH + 1],
                                       in0=scal[0:1, C_SH:C_SH + 1],
                                       scalar1=1.0 / 255.0, scalar2=1e-8,
                                       op0=ALU.mult, op1=ALU.max), 1)
            VE(_)

            def _(e):
                vv.wait(e)
                ss.inc(e.activation(out=scal[0:1, C_ISH:C_ISH + 1],
                                    in_=scal[0:1, C_SH:C_SH + 1],
                                    func=AF.Ln), 1)
                ss.wait(e)
                e.activation(out=scal[0:1, C_ISH:C_ISH + 1],
                             in_=scal[0:1, C_ISH:C_ISH + 1],
                             func=AF.Exp, scale=-1.0)
                ss.inc(e.copy(out=fsc[0:1, 0:1], in_=fsc[0:1, 1:2]), 1)
            SC(_)

            emit_zx(C_LOH, C_HIH, C_SH, C_ISH, C_S2T, C_SW2)
            bcast_spp(pbch)

        for sec in range(3):
            def _(e, sec=sec, p=p):
                e.wait_ge(vv.h, vv.n)
                e.wait_ge(ss.h, ss.n)
                e.wait_ge(gg.h, gg.n)
                e.wait_ge(pe.h, pe.n)
                ld.inc(e.dma_start(
                    out=wl32[:],
                    in_=qkvw[sec * 512 + p * 128: sec * 512 + (p + 1) * 128, :]))
            SY(_)

            def _(e, sec=sec):
                ld.wait(e)
                e.wait_ge(tt.h, tt.n)
                vv.inc(e.tensor_copy(out=wlb[:], in_=wl32[:]), 1)
            VE(_)

            def _(e, sec=sec):
                vv.wait(e)
                last = None
                for dc in range(8):
                    last = e.transpose(xp2[:, dc * 128:(dc + 1) * 128],
                                       wlb[:, dc * 128:(dc + 1) * 128],
                                       ident_b[:])
                tt.inc(last, 1)
            TE(_)

            def _(e, sec=sec):
                tt.wait(e)
                vv.inc(e.tensor_copy(out=wqkT[:, sec * 1024:(sec + 1) * 1024],
                                     in_=xp2[:]), 1)
            VE(_)

        wq_mk = vv.n
        qp_mk = [pstart_vv, pstart_vv]
        vp_mk = [pstart_vv, pstart_vv]
        xnp_gmk = [0, 0]

        def emit_norm1(srcf):
            for tp in range(4):
                def _(e, tp=tp, srcf=srcf):
                    e.wait_ge(tt.h, tt.n)
                    qs = srcf[:, tp * 512:(tp + 1) * 512]
                    vv.inc(e.scalar_tensor_tensor(out=sqp[:], in0=qs,
                                                  scalar=0.0, in1=qs,
                                                  op0=ALU.add,
                                                  op1=ALU.mult), 1)
                VE(_)

                def _(e, tp=tp):
                    vv.wait(e)
                    tt.inc(e.matmul(npp[:], ones2_s[:], sqp[:],
                                    start=True, stop=True), 1)
                TE(_)

                def _(e, tp=tp):
                    tt.wait(e)
                    vv.inc(e.tensor_copy(
                        out=rnq[:, tp * 512:(tp + 1) * 512],
                        in_=npp[:]), 1)
                VE(_)

        def emit_norm2(srcf, dstt, t0c):
            def _(e):
                vv.inc(e.tensor_scalar(
                    out=rnq[:], in0=rnq[:],
                    scalar1=1e-24, scalar2=None, op0=ALU.max), 1)
            VE(_)

            def _(e):
                vv.wait(e)
                ss.inc(e.activation(out=rnq[:], in_=rnq[:], func=AF.Ln), 1)
                ss.wait(e)
                e.activation(out=rnq[:], in_=rnq[:], func=AF.Exp,
                             scale=-0.5)
                ss.inc(e.copy(out=fsc[0:1, 0:1], in_=fsc[0:1, 1:2]), 1)
            SC(_)

            for tp in range(4):
                def _(e, tp=tp):
                    ss.wait(e)
                    e.wait_ge(vv.h, vv.n)
                    tt.inc(e.matmul(
                        bcp[:], ones2t_s[:],
                        rnq[:, tp * 512:(tp + 1) * 512],
                        start=True, stop=True), 1)
                TE(_)

                def _(e, tp=tp, dstt=dstt, srcf=srcf, t0c=t0c):
                    tt.wait(e)
                    vv.inc(e.tensor_mul(
                        dstt[:, t0c + tp * 512:t0c + (tp + 1) * 512],
                        srcf[:, tp * 512:(tp + 1) * 512], bcp[:]), 1)
                VE(_)

        for th in range(2):
            t0c = th * TH
            for dc in range(4):
                def _(e, dc=dc, th=th):
                    sp.wait(e)
                    if dc == 0 and th == 1:
                        e.wait_ge(pe.h, pe.n)
                    ld.inc(e.dma_start(
                        out=xnp[:, dc * TH:(dc + 1) * TH],
                        in_=xnT_h[dc * 128:(dc + 1) * 128,
                                  th * TH:(th + 1) * TH]))
                SY(_)
            xnp_lmk = ld.n

            def _(g, th=th, pen=pe.n, pvv=pstart_vv, pss=pstart_ss):
                g.wait_ge(vv.h, pvv)
                g.wait_ge(ss.h, pss)
                g.wait_ge(pe.h, pen)
                for dc in range(4, 8):
                    gg.inc(g.dma_start(
                        out=xnp[:, dc * TH:(dc + 1) * TH],
                        in_=xnT_h[dc * 128:(dc + 1) * 128,
                                  th * TH:(th + 1) * TH]))
            GP(_)
            xnp_gmk[th] = gg.n

            for sec, dstf in ((0, qf), (1, kf)):
                for tp in range(4):
                    pp = qp[tp % 2]

                    def _(e, sec=sec, tp=tp, pp=pp, th=th,
                          pmk=qp_mk[tp % 2], xl=xnp_lmk):
                        e.wait_ge(ld.h, xl)
                        e.wait_ge(gg.h, xnp_gmk[th])
                        e.wait_ge(vv.h, max(pmk, wq_mk))
                        last = None
                        for dc in range(8):
                            last = e.matmul(
                                pp[:],
                                wqkT[:, sec * 1024 + dc * 128:
                                     sec * 1024 + (dc + 1) * 128],
                                xnp[:, dc * TH + tp * 512:dc * TH + (tp + 1) * 512],
                                start=(dc == 0), stop=(dc == 7))
                        pe.inc(last, 1)
                    TE(_)

                    def _(e, sec=sec, tp=tp, pp=pp, p=p, dstf=dstf):
                        pe.wait(e)
                        vv.inc(e.tensor_scalar(
                            out=dstf[:, tp * 512:(tp + 1) * 512], in0=pp[:],
                            scalar1=biasqkv[:, sec * 4 + p:sec * 4 + p + 1],
                            scalar2=None, op0=ALU.add), 1)
                    VE(_)
                    qp_mk[tp % 2] = vv.n

            emit_norm1(qf)

            for g4 in range(4):
                pp = vp[g4 % 2]

                def _(e, g4=g4, pp=pp, p=p, th=th,
                      pmk=vp_mk[g4 % 2], xl=xnp_lmk):
                    e.wait_ge(vv.h, pmk)
                    e.wait_ge(ld.h, xl)
                    e.wait_ge(gg.h, xnp_gmk[th])
                    last = None
                    for k in range(4):
                        tcl = g4 * 4 + k
                        for dc in range(8):
                            e.matmul(pp[:, k * 128:(k + 1) * 128],
                                     xnp[:, dc * TH + tcl * 128:
                                         dc * TH + (tcl + 1) * 128],
                                     wqkT[:, 2 * 1024 + dc * 128:
                                          2 * 1024 + (dc + 1) * 128],
                                     start=(dc == 0), stop=False)
                        last = e.matmul(pp[:, k * 128:(k + 1) * 128],
                                        onesrow_b[:],
                                        bvrow[0:1, p * 128:(p + 1) * 128],
                                        start=False, stop=True)
                    pe.inc(last, 1)
                TE(_)

                def _(e, g4=g4, pp=pp, th=th):
                    pe.wait(e)
                    vv.inc(e.tensor_copy(
                        out=vtok[:, th * 2048 + g4 * 512:
                                 th * 2048 + (g4 + 1) * 512],
                        in_=pp[:]), 1)
                VE(_)
                vp_mk[g4 % 2] = vv.n

            emit_norm2(qf, qhat, t0c)
            emit_norm1(kf)
            emit_norm2(kf, khat, t0c)

        prj.close()

        if qkdbg is not None and p == 0:
            def _(e):
                vv.wait(e)
                sp.inc(e.dma_start(out=qkdbg[0:128, :], in_=qhat[:]))
                sp.inc(e.dma_start(out=qkdbg[128:256, :], in_=khat[:]))
            SY(_)

        stp = contextlib.ExitStack()
        chA = [stp.enter_context(nc.sbuf_tensor(f"chA{p}{s}", [128, 4096], F32))
               for s in range(2)]
        chB = [stp.enter_context(nc.sbuf_tensor(f"chB{p}{s}", [128, 4096], F32))
               for s in range(2)]
        chC = [stp.enter_context(nc.sbuf_tensor(f"chC{p}{s}", [128, 4096], F32))
               for s in range(2)]
        cosp = [stp.enter_context(nc.psum_tensor(f"cosp{p}{s}", [128, 1024], F32))
                for s in range(2)]
        pvp = stp.enter_context(nc.psum_tensor(f"pvp{p}", [65, 2048], F32))

        clampmk = {}
        mmk = {}
        act1mk = {}
        xmk = {}
        act2mk = {}
        vextmk = {}
        evacmk = {}
        pvmk = {}
        lastpv = pe.n
        pvp_free = vv.n
        lastcolread = vv.n
        lastbcast = tt.n
        lastdenom = ss.n
        proj_ready = vv.n

        NCH = 64

        def emit_cos(n):
            stx_, j4_ = (n % 32) // 4, n % 4
            q0_ = 64 * (n // 32)
            s = n % 2
            for r in range(4):
                def _(e, n=n, r=r, stx=stx_, j4=j4_, q0=q0_):
                    prev = (n, r - 2) if r >= 2 else (n - 1, r + 2)
                    e.wait_ge(vv.h, max(clampmk.get(prev, 0), proj_ready))
                    last = None
                    for h2 in range(2):
                        i0_ = r * 1024 + h2 * 512
                        last = e.matmul(
                            cosp[r % 2][:, h2 * 512:(h2 + 1) * 512],
                            khat[q0:q0 + 64,
                                 stx * 512 + j4 * 128:stx * 512 + (j4 + 1) * 128],
                            qhat[q0:q0 + 64, i0_:i0_ + 512],
                            start=True, stop=True)
                    pe.inc(last, 1)
                    pvmk[("cos", n, r)] = pe.n
                TE(_)

                def _(e, n=n, r=r, s=s):
                    e.wait_ge(pe.h, pvmk[("cos", n, r)])
                    e.wait_ge(gg.h, xmk.get(n - 2, 0))
                    vv.inc(e.tensor_scalar(
                        out=chA[s][:, r * 1024:(r + 1) * 1024],
                        in0=cosp[r % 2][:],
                        scalar1=-(1.0 - EPS), scalar2=(1.0 - EPS),
                        op0=ALU.max, op1=ALU.min), 1)
                    clampmk[(n, r)] = vv.n
                VE(_)

        def emit_m(n):
            s = n % 2

            def _(g, n=n, s=s):
                g.wait_ge(vv.h, clampmk[(n, 3)])
                gg.inc(g.tensor_tensor(out=chB[s][:], in0=chA[s][:],
                                       in1=chA[s][:], op=ALU.mult), 1)
                mmk[n] = gg.n
            GP(_)

        def emit_act1(n):
            s = n % 2

            def _(e, n=n, s=s):
                e.wait_ge(gg.h, mmk[n])
                e.activation(out=chB[s][:], in_=chB[s][:], func=AF.Ln,
                             scale=-1.0, bias=1.0)
                ss.inc(e.activation(out=chB[s][:], in_=chB[s][:],
                                    func=AF.Sigmoid, scale=-0.5), 1)
                act1mk[n] = ss.n
            SC(_)

        def emit_x(n):
            s = n % 2

            def _(g, n=n, s=s):
                g.wait_ge(ss.h, max(act1mk[n], act2mk.get(n - 2, 0)))
                gg.inc(g.tensor_tensor(out=chC[s][:], in0=chA[s][:],
                                       in1=chB[s][:], op=ALU.mult), 1)
                xmk[n] = gg.n
            GP(_)

        def emit_act2(n):
            s = n % 2
            j4_ = n % 4

            def _(e, n=n, s=s, j4=j4_, lpv=None):
                e.wait_ge(gg.h, xmk[n])
                e.wait_ge(vv.h, lastcolread)
                e.wait_ge(pe.h, lastpv)
                e.activation(out=chC[s][:], in_=chC[s][:], func=AF.Arctan)
                e.activation(out=chC[s][:], in_=chC[s][:], func=AF.Ln,
                             scale=-2.0, bias=KLN)
                e.activation(out=scoreT[:, j4 * S:(j4 + 1) * S],
                             in_=chC[s][:], func=AF.Exp, scale=-65.0,
                             accum_out=colss[:, j4:j4 + 1])
                ss.inc(e.copy(out=fsc[0:1, 0:1], in_=fsc[0:1, 1:2]), 1)
                act2mk[n] = ss.n
            SC(_)

        def emit_tail(st_g):
            nonlocal_vars = None
            stx_ = st_g % 8
            q0_ = 64 * (st_g // 8)

            def _(e, st_g=st_g):
                e.wait_ge(ss.h, act2mk[st_g * 4 + 3])
                vv.inc(e.tensor_scalar(out=colss[:, 4:8], in0=colss[:, 0:4],
                                       scalar1=1e-25, scalar2=None,
                                       op0=ALU.max), 1)
            VE(_)
            ret_lastcolread = vv.n

            def _(e, st_g=st_g):
                vv.wait(e)
                ss.inc(e.activation(out=colss[:, 4:8], in_=colss[:, 4:8],
                                    func=AF.Ln, scale=float(2.0 ** 40)), 1)
                ss.wait(e)
                e.activation(out=coltl[:], in_=colss[:, 4:8],
                             func=AF.Exp, scale=-0.5,
                             bias=13.862943611198906)
                ss.inc(e.copy(out=fsc[0:1, 0:1], in_=fsc[0:1, 1:2]), 1)
            SC(_)
            coltl_mk = ss.n

            def _(e, stx=stx_, q0=q0_, coltl_mk=coltl_mk):
                e.wait_ge(ss.h, coltl_mk)
                e.wait_ge(pe.h, lastpv)
                for j4 in range(4):
                    e.tensor_copy(out=vext[:, j4 * 68 + 64:j4 * 68 + 65],
                                  in_=coltl[:, j4:j4 + 1])
                for j4 in range(4):
                    tc = stx * 4 + j4
                    e.tensor_scalar(
                        out=vext[:, j4 * 68:j4 * 68 + 64],
                        in0=vtok[:, tc * 128 + q0:tc * 128 + q0 + 64],
                        scalar1=coltl[:, j4:j4 + 1], scalar2=None,
                        op0=ALU.mult)
                vv.inc(e.tensor_copy(out=fsc[0:1, 0:1], in_=fsc[0:1, 1:2]), 1)
            VE(_)
            vextmk[st_g] = vv.n
            return ret_lastcolread

        def emit_pv(st_g):
            nonlocal pvp_free
            stx_ = st_g % 8
            for ipc in range(2):
                def _(e, ipc=ipc, st_g=st_g, pf=pvp_free):
                    e.wait_ge(vv.h, max(vextmk[st_g], pf))
                    e.wait_ge(ss.h, act2mk[st_g * 4 + 3])
                    last = None
                    for j4 in range(4):
                        for n5 in range(4):
                            c0 = j4 * S + ipc * 2048 + n5 * 512
                            last = e.matmul(
                                pvp[:, n5 * 512:(n5 + 1) * 512],
                                vext[:, j4 * 68:j4 * 68 + 65],
                                scoreT[:, c0:c0 + 512],
                                start=(j4 == 0), stop=(j4 == 3))
                    pe.inc(last, 1)
                    pvmk[(st_g, ipc)] = pe.n
                TE(_)

                def _(e, ipc=ipc, st_g=st_g, stx=stx_, lb=lastbcast, ld_=lastdenom):
                    e.wait_ge(pe.h, pvmk[(st_g, ipc)])
                    if stx == 0:
                        e.wait_ge(tt.h, lb)
                        e.wait_ge(ss.h, ld_)
                    dst = numer[:, ipc * 2048:(ipc + 1) * 2048]
                    if stx == 0:
                        ins = e.tensor_copy(out=dst, in_=pvp[:])
                    else:
                        ins = e.tensor_add(dst, dst, pvp[:])
                    vv.inc(ins, 1)
                    evacmk[(st_g, ipc)] = vv.n
                VE(_)
                pvp_free = vv.n

        def emit_epilogue(hh, p):
            nonlocal pvp_free, lastdenom, lastbcast

            def _(e, hh=hh):
                e.wait_ge(vv.h, max(evacmk[(hh * 8 + 7, 0)],
                                    evacmk[(hh * 8 + 7, 1)]))
                ss.inc(e.activation(out=numer[64:65, :], in_=numer[64:65, :],
                                    func=AF.Ln), 1)
                ss.wait(e)
                e.activation(out=numer[64:65, :], in_=numer[64:65, :],
                             func=AF.Exp, scale=-1.0)
                ss.inc(e.copy(out=fsc[0:1, 0:1], in_=fsc[0:1, 1:2]), 1)
            SC(_)
            lastdenom = ss.n

            for ip in range(8):
                def _(e, ip=ip, pf=pvp_free, hh=hh):
                    e.wait_ge(ss.h, lastdenom)
                    e.wait_ge(vv.h, max(pf, evacmk[(hh * 8 + 7, 0)],
                                        evacmk[(hh * 8 + 7, 1)]))
                    tt.inc(e.matmul(pvp[0:64, 0:512], onesr64[64:65, :],
                                    numer[64:65, ip * 512:(ip + 1) * 512],
                                    start=True, stop=True), 1)
                TE(_)
                bc_mk = tt.n

                def _(e, ip=ip, bc_mk=bc_mk):
                    e.wait_ge(tt.h, bc_mk)
                    if ip >= 2:
                        e.wait_ge(sp.h, sp.n)
                    e.tensor_mul(atst[ip % 2][:],
                                 numer[0:64, ip * 512:(ip + 1) * 512],
                                 pvp[0:64, 0:512])
                    vv.inc(e.tensor_copy(out=fsc[0:1, 0:1],
                                         in_=fsc[0:1, 1:2]), 1)
                VE(_)
                pvp_free = vv.n

                def _(e, ip=ip, p=p, hh=hh):
                    vv.wait(e)
                    r0 = (p * 2 + hh) * 64
                    sp.inc(e.dma_start(
                        out=am_in[r0:r0 + 64, ip * 512:(ip + 1) * 512],
                        in_=atst[ip % 2][:]))
                    if amdbg is not None:
                        sp.inc(e.dma_start(
                            out=amdbg[r0:r0 + 64, ip * 512:(ip + 1) * 512],
                            in_=atst[ip % 2][:]))
                SY(_)
            lastbcast = tt.n

        emit_cos(0); emit_m(0)
        emit_cos(1); emit_m(1)
        emit_act1(0)
        for n in range(NCH):
            if n + 1 < NCH:
                emit_act1(n + 1)
            emit_x(n)
            emit_act2(n)
            if n + 2 < NCH:
                emit_cos(n + 2)
                emit_m(n + 2)
            if n % 4 == 3:
                st_g = n // 4
                lastcolread = emit_tail(st_g)
                emit_pv(st_g)
                lastpv = pe.n
                if st_g % 8 == 7:
                    emit_epilogue(st_g // 8, p)
            if p >= 1:
                emit_hq_step()

        stp.close()

    ph2.close()

    def _(g):
        sp.wait(g)
        c3.inc(g.collective_compute(
            "AllGather", ALU.bypass,
            replica_groups=[[0, 1], [2, 3], [4, 5], [6, 7]],
            ins=[am_in.opt()], outs=[am_out.opt()]), 1)
        g.wait_ge(c3.h, 1)
        gg.inc(g.engine_nop(), 1)
    GP(_)
    ag_mk = gg.n

    ph46 = contextlib.ExitStack()

    def sb46(name, shape, dtype):
        return ph46.enter_context(nc.sbuf_tensor(name, shape, dtype))

    def ps46(name, shape, dtype=F32):
        return ph46.enter_context(nc.psum_tensor(name, shape, dtype))

    outacc = sb46("outacc", [128, 8 * TH], F32)
    outwT = sb46("outwT", [128, 8 * 1024], BF16)
    biassum = sb46("biassum", [128, 8], F32)
    fps = ps46("fps", [128, 1024], F32)
    m2p = [ps46(f"m2p{i}", [128, 512]) for i in range(2)]
    outwT_w = outwT[:].rearrange("p (a o) -> p a o", a=8)

    def _(e):
        e.wait_ge(sp.h, sp.n)
        vv.inc(e.tensor_add(biassum[:], biaso[:], bias2[:]), 1)
        vv.wait(e)
        vv.inc(e.tensor_copy(out=fsc[0:1, 0:1], in_=fsc[0:1, 1:2]), 1)
    VE(_)

    ph4a = contextlib.ExitStack()
    wl4 = ph4a.enter_context(nc.sbuf_tensor("wl4", [128, DM], F32))
    wlb4 = ph4a.enter_context(nc.sbuf_tensor("wlb4", [128, DM], BF16))
    xp4 = ph4a.enter_context(nc.psum_tensor("xp4", [128, 1024], BF16))

    for oc in range(8):
        def _(e, oc=oc):
            e.wait_ge(vv.h, vv.n)
            if oc == 0:
                e.wait_ge(sp.h, sp.n)
            ld.inc(e.dma_start(out=wl4[:], in_=outw[oc * 128:(oc + 1) * 128, :]))
        SY(_)

        def _(e, oc=oc):
            ld.wait(e)
            e.wait_ge(tt.h, tt.n)
            vv.inc(e.tensor_copy(out=wlb4[:], in_=wl4[:]), 1)
        VE(_)

        def _(e, oc=oc):
            vv.wait(e)
            last = None
            for dc in range(8):
                last = e.transpose(xp4[:, dc * 128:(dc + 1) * 128],
                                   wlb4[:, dc * 128:(dc + 1) * 128], ident_b[:])
            tt.inc(last, 1)
        TE(_)

        def _(e, oc=oc):
            tt.wait(e)
            vv.inc(e.tensor_copy(
                out=outwT_w[:, :, oc * 128:(oc + 1) * 128],
                in_=xp4[:].rearrange("p (a q) -> p a q", q=128)), 1)
        VE(_)

    ph4a.close()

    w2s_r = w2qT_h.rearrange("(a p) q -> p a q", p=128)

    ph5b = contextlib.ExitStack()
    w2all = ph5b.enter_context(nc.sbuf_tensor("w2all", [128, 256 * 128], BF16))
    hqc = ph5b.enter_context(nc.sbuf_tensor("hqc", [128, 32 * 512], BF16))
    w2all_w = w2all[:].rearrange("p (a q) -> p a q", q=128)

    def _(e):
        e.wait_ge(vv.h, vv.n)
        e.wait_ge(pe.h, pe.n)
        sp.wait(e)
        for g in range(4):
            ld.inc(e.dma_start(out=w2all_w[:, g * 64:(g + 1) * 64, :],
                               in_=w2s_r[:, g * 64:(g + 1) * 64, :]))
    SY(_)

    hm1 = {}
    hm2 = {}
    for tp in range(4):
        for hf in range(2):
            def _(e, tp=tp, hf=hf):
                if tp == 0:
                    sp.wait(e)
                else:
                    mk = hm1 if hf == 0 else hm2
                    e.wait_ge(pe.h, mk[(tp - 1, 7)])
                for hc in range(hf * 16, hf * 16 + 16):
                    ld.inc(e.dma_start(
                        out=hqc[:, hc * 512:(hc + 1) * 512],
                        in_=hq_h[hc * 128:(hc + 1) * 128,
                                 tp * 512:(tp + 1) * 512]))
            SY(_)
            if hf == 0:
                hq_ld1 = ld.n
            else:
                hq_ld2 = ld.n

        for oc in range(8):
            pp = m2p[oc % 2]

            def _(e, tp=tp, oc=oc, pp=pp, l1=hq_ld1, l2=hq_ld2):
                e.wait_ge(ld.h, l1)
                e.wait_ge(vv.h, vv.n)
                last = None
                for hc in range(16):
                    last = e.matmul(
                        pp[:], w2all[:, (oc * 32 + hc) * 128:
                                     (oc * 32 + hc + 1) * 128],
                        hqc[:, hc * 512:(hc + 1) * 512],
                        start=(hc == 0), stop=False)
                pe.inc(last, 1)
                hm1[(tp, oc)] = pe.n
                e.wait_ge(ld.h, l2)
                for hc in range(16, 32):
                    last = e.matmul(
                        pp[:], w2all[:, (oc * 32 + hc) * 128:
                                     (oc * 32 + hc + 1) * 128],
                        hqc[:, hc * 512:(hc + 1) * 512],
                        start=False, stop=(hc == 31))
                pe.inc(last, 1)
                hm2[(tp, oc)] = pe.n
            TE(_)

            def _(e, tp=tp, oc=oc, pp=pp):
                pe.wait(e)
                dst = outacc[:, oc * TH + tp * 512:oc * TH + (tp + 1) * 512]
                vv.inc(e.tensor_scalar(
                    out=dst, in0=pp[:], scalar1=spp[:, C_S2T:C_S2T + 1],
                    scalar2=biassum[:, oc:oc + 1],
                    op0=ALU.mult, op1=ALU.add), 1)
            VE(_)

    ph5b.close()

    ph4b = contextlib.ExitStack()
    rhsc = ph4b.enter_context(nc.sbuf_tensor("rhsc", [128, 8 * TH], BF16))
    gthr = [ph4b.enter_context(nc.sbuf_tensor(f"gthr{i}", [128, S], BF16))
            for i in range(2)]
    xin6 = [ph4b.enter_context(nc.sbuf_tensor(f"xin6{i}", [128, DM], F32))
            for i in range(2)]
    ybuf = [ph4b.enter_context(nc.sbuf_tensor(f"ybuf{i}", [128, DM], F32))
            for i in range(2)]

    for hc in range(8):
        def _(e, hc=hc):
            e.wait_ge(gg.h, ag_mk)
            if hc >= 2:
                e.wait_ge(vv.h, vv.n)
            else:
                e.wait_ge(pe.h, pe.n)
            ld.inc(e.dma_start(out=gthr[hc % 2][:],
                               in_=am_out[hc * 128:(hc + 1) * 128, :]))
        SY(_)

        def _(e, hc=hc):
            ld.wait(e)
            gt = gthr[hc % 2]
            e.tensor_scalar(out=gt[:, TH:S], in0=gt[:, TH:S],
                            scalar1=spp[:, C_BE:C_BE + 1], scalar2=None,
                            op0=ALU.mult)
            vv.inc(e.scalar_tensor_tensor(
                out=rhsc[:, hc * TH:(hc + 1) * TH], in0=gt[:, 0:TH],
                scalar=spp[:, C_AL:C_AL + 1], in1=gt[:, TH:S],
                op0=ALU.mult, op1=ALU.add), 1)
        VE(_)

    for tp in range(4):
        for oc in range(8):
            pp = m2p[oc % 2]

            def _(e, oc=oc, tp=tp, pp=pp):
                e.wait_ge(vv.h, vv.n)
                last = None
                for hc in range(8):
                    last = e.matmul(
                        pp[:],
                        outwT[:, hc * 1024 + oc * 128:hc * 1024 + (oc + 1) * 128],
                        rhsc[:, hc * TH + tp * 512:hc * TH + (tp + 1) * 512],
                        start=(hc == 0), stop=(hc == 7))
                pe.inc(last, 1)
            TE(_)

            def _(e, oc=oc, tp=tp, pp=pp):
                pe.wait(e)
                dst = outacc[:, oc * TH + tp * 512:oc * TH + (tp + 1) * 512]
                vv.inc(e.scalar_tensor_tensor(
                    out=dst, in0=pp[:], scalar=1.0,
                    in1=dst, op0=ALU.mult, op1=ALU.add), 1)
            VE(_)

        for tc in range(tp * 4, tp * 4 + 4):
            def _(e, tc=tc):
                e.wait_ge(vv.h, vv.n)
                ld.inc(e.dma_start(out=xin6[tc % 2][:],
                                   in_=xh[tc * 128:(tc + 1) * 128, :]))
            SY(_)

            def _(e, tc=tc):
                vv.wait(e)
                last = None
                for oc in range(8):
                    last = e.transpose(
                        fps[:, oc * 128:(oc + 1) * 128],
                        outacc[:, oc * TH + tc * 128:oc * TH + (tc + 1) * 128],
                        ident_f[:])
                tt.inc(last, 1)
            TE(_)

            def _(e, tc=tc):
                tt.wait(e)
                ld.wait(e)
                e.wait_ge(yy.h, yy.n)
                vv.inc(e.scalar_tensor_tensor(out=ybuf[tc % 2][:], in0=fps[:],
                                              scalar=1.0, in1=xin6[tc % 2][:],
                                              op0=ALU.mult, op1=ALU.add), 1)
            VE(_)

            def _(e, tc=tc):
                vv.wait(e)
                yy.inc(e.dma_start(out=y[tc * 128:(tc + 1) * 128, :],
                                   in_=ybuf[tc % 2][:]))
            SY(_)

    ph4b.close()
    ph46.close()

    def _(g):
        for s in sems.values():
            g.wait_ge(s.h, s.n)
    GP(_)

    st.close()
    return nc


def kernel(L0_attn_x, L0_attn_qkv_w, L0_attn_qkv_b, L0_attn_out_w, L0_attn_out_b,
           L0_ffn_W1, L0_ffn_b1, L0_ffn_W2, L0_ffn_b2):
    x = np.ascontiguousarray(np.asarray(L0_attn_x, dtype=np.float32))
    qkv_w = np.ascontiguousarray(np.asarray(L0_attn_qkv_w, dtype=np.float32))
    qkv_b = np.asarray(L0_attn_qkv_b, dtype=np.float32)
    out_w = np.ascontiguousarray(np.asarray(L0_attn_out_w, dtype=np.float32))
    out_b = np.asarray(L0_attn_out_b, dtype=np.float32)
    W1 = np.ascontiguousarray(np.asarray(L0_ffn_W1, dtype=np.float32))
    b1 = np.asarray(L0_ffn_b1, dtype=np.float32)
    W2 = np.ascontiguousarray(np.asarray(L0_ffn_W2, dtype=np.float32))
    b2 = np.asarray(L0_ffn_b2, dtype=np.float32)

    nc = build_program()
    eye_b = np.eye(128, dtype=BF)
    eye_f = np.eye(128, dtype=np.float32)
    o2 = np.zeros((128, 2), np.float32)
    o2[0:64, 0] = 1.0
    o2[64:128, 1] = 1.0
    b1h = np.ascontiguousarray(b1.reshape(32, 128).T)
    b2h = np.ascontiguousarray(b2.reshape(8, 128).T)
    obh = np.ascontiguousarray(out_b.reshape(8, 128).T)
    in_maps = []
    for c in range(8):
        b, half = c // 2, c % 2
        qw = np.ascontiguousarray(np.concatenate(
            [qkv_w[s * DM + half * 512: s * DM + half * 512 + 512, :]
             for s in range(3)], axis=0))
        qb = np.stack([qkv_b[s * DM + half * 512 + p * 128:
                             s * DM + half * 512 + (p + 1) * 128]
                       for s in range(3) for p in range(4)], axis=1)
        qbv = qkv_b[2 * DM + half * 512: 2 * DM + half * 512 + 512].reshape(1, 512)
        hs = np.array([[1.0, 0.0]] if half == 0 else [[0.0, 1.0]], np.float32)
        in_maps.append({
            "xb": np.ascontiguousarray(x[b].reshape(S, DM)),
            "xh": np.ascontiguousarray(
                x[b, half * TH:(half + 1) * TH].reshape(TH, DM)),
            "qkvw": qw,
            "qkvb": np.ascontiguousarray(qb.astype(np.float32)),
            "qkvbv": np.ascontiguousarray(qbv.astype(np.float32)),
            "outw": out_w,
            "outb": obh,
            "w1": W1, "b1": b1h, "w2": W2, "b2": b2h,
            "idbf": eye_b, "idf32": eye_f,
            "ones2": o2, "ones2t": np.ascontiguousarray(o2.T),
            "hsel": hs,
        })
    res = run_bass_kernel_spmd(nc, in_maps, core_ids=list(range(8)))
    out = np.empty((4, S, DM), dtype=np.float32)
    for c in range(8):
        b, half = c // 2, c % 2
        out[b, half * TH:(half + 1) * TH] = res.results[c]["y"]
    return out


# version 22 (final)
# speedup vs baseline: 1.0552x; 1.0143x over previous
import contextlib
import numpy as np
import ml_dtypes
import concourse.bass as bass
import concourse.mybir as mybir
from concourse.bass_utils import run_bass_kernel_spmd

F32 = mybir.dt.float32
BF16 = mybir.dt.bfloat16
AF = mybir.ActivationFunctionType
ALU = mybir.AluOpType
AX = mybir.AxisListType.X
BF = ml_dtypes.bfloat16

S = 4096
DM = 1024
FH = 4096
TH = 2048
EPS = 1e-7
MAGIC = 12582912.0
KLN = float(1.0 + np.pi / 2.0)
C_SW1, C_SW2, C_ISW1, C_ISW2, C_SX, C_ISX, C_LOX, C_HIX = range(8)
C_S1T, C_SH, C_ISH, C_LOH, C_HIH, C_S2T, C_AL, C_BE = range(8, 16)


class Sem:
    def __init__(self, stack, nc, name):
        self.h = stack.enter_context(nc.semaphore(name))
        self.n = 0

    def inc(self, instr, amt=16):
        instr.then_inc(self.h, amt)
        self.n += amt
        return instr

    def wait(self, eng, n=None):
        eng.wait_ge(self.h, self.n if n is None else n)


def build_program(debug=()):
    nc = bass.Bass("TRN2", target_bir_lowering=False)
    st = contextlib.ExitStack()

    def dk(name):
        return "ExternalOutput" if name in debug else "Internal"

    xb = nc.dram_tensor("xb", [S, DM], F32, kind="ExternalInput").ap()
    xh = nc.dram_tensor("xh", [TH, DM], F32, kind="ExternalInput").ap()
    qkvw = nc.dram_tensor("qkvw", [3 * 512, DM], F32, kind="ExternalInput").ap()
    qkvb = nc.dram_tensor("qkvb", [128, 12], F32, kind="ExternalInput").ap()
    qkvbv = nc.dram_tensor("qkvbv", [1, 512], F32, kind="ExternalInput").ap()
    outw = nc.dram_tensor("outw", [DM, DM], F32, kind="ExternalInput").ap()
    outb = nc.dram_tensor("outb", [128, 8], F32, kind="ExternalInput").ap()
    w1 = nc.dram_tensor("w1", [FH, DM], F32, kind="ExternalInput").ap()
    b1 = nc.dram_tensor("b1", [128, 32], F32, kind="ExternalInput").ap()
    w2 = nc.dram_tensor("w2", [DM, FH], F32, kind="ExternalInput").ap()
    b2 = nc.dram_tensor("b2", [128, 8], F32, kind="ExternalInput").ap()
    idbf = nc.dram_tensor("idbf", [128, 128], BF16, kind="ExternalInput").ap()
    idf32 = nc.dram_tensor("idf32", [128, 128], F32, kind="ExternalInput").ap()
    ones2 = nc.dram_tensor("ones2", [128, 2], F32, kind="ExternalInput").ap()
    ones2t = nc.dram_tensor("ones2t", [2, 128], F32, kind="ExternalInput").ap()
    hsel = nc.dram_tensor("hsel", [1, 2], F32, kind="ExternalInput").ap()
    y = nc.dram_tensor("y", [TH, DM], F32, kind="ExternalOutput").ap()

    xnT_h = nc.dram_tensor("xnT_h", [DM, S], BF16, kind=dk("xnT_h")).ap()
    h_h = nc.dram_tensor("h_h", [FH, TH], F32, kind=dk("h_h")).ap()
    hq_h = nc.dram_tensor("hq_h", [FH, TH], BF16, kind=dk("hq_h")).ap()
    w2qT_h = nc.dram_tensor("w2qT_h", [8 * FH, 128], BF16, kind=dk("w2qT_h")).ap()
    am_in = nc.dram_tensor("am_in", [512, S], BF16).ap()
    am_out = nc.dram_tensor("am_out", [DM, S], BF16).ap()
    amdbg = (nc.dram_tensor("amdbg", [512, S], BF16, kind="ExternalOutput").ap()
             if "amdbg" in debug else None)
    qkdbg = (nc.dram_tensor("qkdbg", [256, S], BF16, kind="ExternalOutput").ap()
             if "qkdbg" in debug else None)
    stdbg = (nc.dram_tensor("stdbg", [512, S], BF16, kind="ExternalOutput").ap()
             if "qkdbg" in debug else None)
    csdbg = (nc.dram_tensor("csdbg", [128, 16], F32, kind="ExternalOutput").ap()
             if "qkdbg" in debug else None)
    vxdbg = (nc.dram_tensor("vxdbg", [128, 4 * 68], BF16, kind="ExternalOutput").ap()
             if "qkdbg" in debug else None)
    nmdbg = (nc.dram_tensor("nmdbg", [65, S], F32, kind="ExternalOutput").ap()
             if "qkdbg" in debug else None)
    cc1i = nc.dram_tensor("cc1i", [1, 8], F32).ap()
    cc1o = nc.dram_tensor("cc1o", [1, 8], F32).ap()
    cc2i = nc.dram_tensor("cc2i", [1, 8], F32).ap()
    cc2o = nc.dram_tensor("cc2o", [1, 8], F32).ap()
    scdbg = nc.dram_tensor("scdbg", [3, 16], F32, kind=dk("scdbg")).ap()

    def sbuf(name, shape, dtype):
        return st.enter_context(nc.sbuf_tensor(name, shape, dtype))

    ident_b = sbuf("ident_b", [128, 128], BF16)
    ident_f = sbuf("ident_f", [128, 128], F32)
    ones2_s = sbuf("ones2_s", [128, 2], F32)
    ones2t_s = sbuf("ones2t_s", [2, 128], F32)
    onesrow = sbuf("onesrow", [1, 128], F32)
    onesrow_b = sbuf("onesrow_b", [1, 128], BF16)
    onesr64 = sbuf("onesr64", [65, 64], F32)
    bias1 = sbuf("bias1", [128, 32], F32)
    bias2 = sbuf("bias2", [128, 8], F32)
    biaso = sbuf("biaso", [128, 8], F32)
    biasqkv = sbuf("biasqkv", [128, 12], F32)
    bv32 = sbuf("bv32", [1, 512], F32)
    bvrow = sbuf("bvrow", [1, 512], BF16)
    spp = sbuf("spp", [128, 16], F32)
    scal = sbuf("scal", [1, 16], F32)
    ccs = sbuf("ccs", [1, 8], F32)
    ccr = sbuf("ccr", [1, 8], F32)
    stats = sbuf("stats", [128, 80], F32)
    wstat = sbuf("wstat", [128, 40], F32)
    sc1a = sbuf("sc1a", [128, 1], F32)
    sc1b = sbuf("sc1b", [128, 1], F32)
    fsc = sbuf("fsc", [1, 2], F32)
    ctile = sbuf("ctile", [128, 3], F32)
    nc.const_aps.register(F32, 1.0, ctile[:, 0:1])
    nc.const_aps.register(F32, KLN, ctile[:, 1:2])
    nc.const_aps.register(F32, 13.862943611198906, ctile[:, 2:3])

    sems = {}
    for nm in ("ld", "sp", "vv", "ss", "tt", "gg", "c1", "c2", "c3", "yy", "pe"):
        sems[nm] = Sem(st, nc, nm)
    ld, sp, vv, ss, tt, gg = (sems[k] for k in ("ld", "sp", "vv", "ss", "tt", "gg"))
    c1, c2, c3, yy, pe = (sems[k] for k in ("c1", "c2", "c3", "yy", "pe"))

    blk = st.enter_context(nc.Block())

    def SY(f): blk.sync(f)
    def VE(f): blk.vector(f)
    def SC(f): blk.scalar(f)
    def TE(f): blk.tensor(f)
    def GP(f): blk.gpsimd(f)

    def _(g):
        g.memset(onesrow[:], 1.0)
        g.memset(onesrow_b[:], 1.0)
        g.memset(onesr64[:], 1.0)
        g.memset(ctile[:, 0:1], 1.0)
        g.memset(ctile[:, 1:2], KLN)
        g.memset(ctile[:, 2:3], 13.862943611198906)
        g.memset(fsc[:], 0.0)
        g.memset(scal[:], 0.0)
        gg.inc(g.memset(ccs[:], 0.0), 1)
    GP(_)

    def _(e):
        for dst, src in ((ident_b, idbf), (ident_f, idf32), (ones2_s, ones2),
                         (ones2t_s, ones2t), (bias1, b1), (bias2, b2),
                         (biaso, outb), (biasqkv, qkvb), (bv32, qkvbv)):
            ld.inc(e.dma_start(out=dst[:], in_=src[:]))
        ld.inc(e.dma_start(out=scal[0:1, C_AL:C_BE + 1], in_=hsel[:]))
    SY(_)

    def _(e):
        ld.wait(e)
        gg.wait(e)
        vv.inc(e.tensor_copy(out=bvrow[:], in_=bv32[:]), 1)
    VE(_)

    ph01 = contextlib.ExitStack()

    def sb01(name, shape, dtype):
        return ph01.enter_context(nc.sbuf_tensor(name, shape, dtype))

    def ps01(name, shape, dtype=F32):
        return ph01.enter_context(nc.psum_tensor(name, shape, dtype))

    xin = [sb01(f"xin{i}", [128, DM], F32) for i in range(4)]
    xnf = sb01("xnf", [128, DM], F32)
    sqt = sb01("sqt", [128, DM], F32)
    xnb = sb01("xnb", [128, DM], BF16)
    stg = sb01("stg", [128, DM], BF16)
    xpose = ps01("xpose", [128, 1024], BF16)
    pmm1 = ps01("pmm1", [1, 128], F32)
    pmm2 = ps01("pmm2", [1, 128], F32)
    pbc = ps01("pbc", [128, 16], F32)

    xnT_w = xnT_h.rearrange("(a p) t -> p a t", p=128)

    ln_tiny_mk = [0]

    def emit_ln_stats_pair(tc0, src_ap, marks):
        def _(e):
            if len(marks) >= 3:
                e.wait_ge(vv.h, marks[-3])
            elif len(marks) == 0:
                e.wait_ge(vv.h, vv.n)
            for tc in (tc0, tc0 + 1):
                ld.inc(e.dma_start(out=xin[tc % 4][:],
                                   in_=src_ap[tc * 128:(tc + 1) * 128, :]))
        SY(_)

        def _(e):
            ld.wait(e)
            e.wait_ge(vv.h, ln_tiny_mk[0])
            for i, tc in enumerate((tc0, tc0 + 1)):
                c0 = 72 * i
                e.activation(out=sqt[:], in_=xin[tc % 4][:], func=AF.Copy,
                             accum_out=stats[:, c0:c0 + 1])
                e.activation(out=sqt[:], in_=xin[tc % 4][:], func=AF.Square,
                             accum_out=stats[:, c0 + 1:c0 + 2])
            ss.inc(e.copy(out=fsc[0:1, 0:1], in_=fsc[0:1, 1:2]), 1)
        SC(_)
        stats_mk = ss.n

        def _(e, stats_mk=stats_mk):
            e.wait_ge(ss.h, stats_mk)

            def step(ins):
                vv.inc(ins, 1)
                vv.wait(e)
            for i in range(2):
                c0 = 72 * i
                e.tensor_scalar(out=stats[:, c0 + 2:c0 + 3],
                                in0=stats[:, c0:c0 + 1],
                                scalar1=1.0 / DM, scalar2=None, op0=ALU.mult)
                step(e.tensor_scalar(out=stats[:, c0 + 3:c0 + 4],
                                     in0=stats[:, c0 + 1:c0 + 2],
                                     scalar1=1.0 / DM, scalar2=1e-5,
                                     op0=ALU.mult, op1=ALU.add))
                step(e.scalar_tensor_tensor(out=stats[:, c0 + 4:c0 + 5],
                                            in0=stats[:, c0 + 2:c0 + 3],
                                            scalar=0.0,
                                            in1=stats[:, c0 + 2:c0 + 3],
                                            op0=ALU.add, op1=ALU.mult))
                step(e.tensor_sub(stats[:, c0 + 4:c0 + 5],
                                  stats[:, c0 + 3:c0 + 4],
                                  stats[:, c0 + 4:c0 + 5]))
            vv.inc(e.tensor_copy(out=fsc[0:1, 0:1], in_=fsc[0:1, 1:2]), 1)
        VE(_)
        ln_tiny_mk[0] = vv.n

        def _(e):
            vv.wait(e)
            e.activation(out=stats[:, 5:6], in_=stats[:, 4:5], func=AF.Ln)
            ss.inc(e.activation(out=stats[:, 77:78], in_=stats[:, 76:77],
                                func=AF.Ln), 1)
            ss.wait(e)
            e.activation(out=stats[:, 6:7], in_=stats[:, 5:6],
                         func=AF.Exp, scale=-0.5)
            e.activation(out=stats[:, 78:79], in_=stats[:, 77:78],
                         func=AF.Exp, scale=-0.5)
            ss.inc(e.copy(out=fsc[0:1, 0:1], in_=fsc[0:1, 1:2]), 1)
        SC(_)

    def emit_ln_chunk(tc, src_ap, do_spill, marks):
        c0 = 72 * (tc % 2)

        def _(e, c0=c0):
            ss.wait(e)
            xt = xin[tc % 4]
            e.tensor_scalar(out=xnf[:], in0=xt[:],
                            scalar1=stats[:, c0 + 2:c0 + 3],
                            scalar2=stats[:, c0 + 6:c0 + 7], op0=ALU.subtract,
                            op1=ALU.mult)
            if do_spill:
                e.tensor_reduce(out=stats[:, 8 + tc:9 + tc], in_=xnf[:],
                                axis=AX, op=ALU.max)
                e.tensor_reduce(out=stats[:, 40 + tc:41 + tc], in_=xnf[:],
                                axis=AX, op=ALU.min)
                e.wait_ge(sp.h, sp.n)
                e.wait_ge(tt.h, tt.n)
                e.tensor_copy(out=xnb[:], in_=xnf[:])
            vv.inc(e.tensor_copy(out=sqt[:, 0:1], in_=xnf[:, 0:1]), 1)
        VE(_)
        marks.append(vv.n)

        if do_spill:
            def _(e):
                vv.wait(e)
                last = None
                for dc in range(8):
                    last = e.transpose(xpose[:, dc * 128:(dc + 1) * 128],
                                       xnb[:, dc * 128:(dc + 1) * 128],
                                       ident_b[:])
                tt.inc(last, 1)
            TE(_)

            def _(e):
                tt.wait(e)
                vv.inc(e.tensor_copy(out=stg[:], in_=xpose[:]), 1)
            VE(_)
            marks[-1] = vv.n

            def _(e, tc=tc):
                vv.wait(e)
                sp.inc(e.dma_start(
                    out=xnT_w[:, :, tc * 128:(tc + 1) * 128],
                    in_=stg[:].rearrange("p (a q) -> p a q", q=128)))
            SY(_)

    ln_marks0 = []
    for tc0 in range(0, 32, 2):
        emit_ln_stats_pair(tc0, xb, ln_marks0)
        emit_ln_chunk(tc0, xb, do_spill=True, marks=ln_marks0)
        emit_ln_chunk(tc0 + 1, xb, do_spill=True, marks=ln_marks0)

    wld = [sb01(f"wld{i}", [128, FH], F32) for i in range(2)]
    w_marks = []

    def emit_w_absmax(gi, src_ap, cols):
        def _(g):
            if len(w_marks) >= 2:
                g.wait_ge(vv.h, w_marks[-2])
            gg.inc(g.dma_start(out=wld[gi % 2][0:128, 0:cols], in_=src_ap))
        GP(_)
        wld_mk = gg.n

        def _(e, wld_mk=wld_mk):
            e.wait_ge(gg.h, wld_mk)
            vv.inc(e.tensor_reduce(out=wstat[:, gi:gi + 1],
                                   in_=wld[gi % 2][0:128, 0:cols], axis=AX,
                                   op=ALU.max, apply_absolute_value=True), 1)
        VE(_)
        w_marks.append(vv.n)

    for ci in range(32):
        emit_w_absmax(ci, w1[ci * 128:(ci + 1) * 128, :], DM)
    for ci in range(8):
        emit_w_absmax(32 + ci, w2[ci * 128:(ci + 1) * 128, :], FH)

    def _(e):
        e.tensor_reduce(out=sc1a[:], in_=stats[:, 8:40], axis=AX, op=ALU.max)
        vv.inc(e.tensor_reduce(out=sc1b[:], in_=stats[:, 40:72], axis=AX,
                               op=ALU.min), 1)
    VE(_)

    def _(e):
        vv.wait(e)
        e.transpose(pmm1[:], sc1a[:], ident_f[:])
        tt.inc(e.transpose(pmm2[:], sc1b[:], ident_f[:]), 1)
    TE(_)

    def _(e):
        tt.wait(e)
        e.tensor_reduce(out=ccs[0:1, 1:2], in_=pmm1[:], axis=AX, op=ALU.max)
        vv.inc(e.tensor_reduce(out=ccs[0:1, 2:3], in_=pmm2[:], axis=AX,
                               op=ALU.min), 1)
        vv.wait(e)
        e.tensor_scalar(out=ccs[0:1, 0:1], in0=ccs[0:1, 2:3],
                        scalar1=-1.0, scalar2=None, op0=ALU.mult)
        vv.inc(e.tensor_copy(out=fsc[0:1, 0:1], in_=fsc[0:1, 1:2]), 1)
    VE(_)

    def _(g):
        vv.wait(g)
        gg.inc(g.dma_start(out=cc1i[:, :], in_=ccs[0:1, :]))
        gg.wait(g)
        c1.inc(g.collective_compute(
            "AllReduce", ALU.max, replica_groups=[list(range(8))],
            ins=[cc1i.opt()], outs=[cc1o.opt()]), 1)
        g.wait_ge(c1.h, 1)
        gg.inc(g.dma_start(out=ccr[0:1, :], in_=cc1o[:, :]))
    GP(_)

    def _(e):
        e.wait_ge(tt.h, tt.n)
        e.tensor_reduce(out=sc1a[:], in_=wstat[:, 0:32], axis=AX, op=ALU.max)
        vv.inc(e.tensor_reduce(out=sc1b[:], in_=wstat[:, 32:40], axis=AX,
                               op=ALU.max), 1)
    VE(_)

    def _(e):
        vv.wait(e)
        e.transpose(pmm1[:], sc1a[:], ident_f[:])
        tt.inc(e.transpose(pmm2[:], sc1b[:], ident_f[:]), 1)
    TE(_)

    def _(e):
        tt.wait(e)
        e.tensor_reduce(out=scal[0:1, C_SW1:C_SW1 + 1], in_=pmm1[:],
                        axis=AX, op=ALU.max)
        vv.inc(e.tensor_reduce(out=scal[0:1, C_SW2:C_SW2 + 1], in_=pmm2[:],
                               axis=AX, op=ALU.max), 1)
        vv.wait(e)
        e.tensor_scalar(out=scal[0:1, C_SW1:C_SW2 + 1],
                        in0=scal[0:1, C_SW1:C_SW2 + 1],
                        scalar1=1.0 / 127.0, scalar2=1e-8,
                        op0=ALU.mult, op1=ALU.max)
        vv.inc(e.tensor_copy(out=fsc[0:1, 0:1], in_=fsc[0:1, 1:2]), 1)
    VE(_)

    def _(e):
        vv.wait(e)
        ss.inc(e.activation(out=scal[0:1, C_ISW1:C_ISW2 + 1],
                            in_=scal[0:1, C_SW1:C_SW2 + 1], func=AF.Ln), 1)
        ss.wait(e)
        e.activation(out=scal[0:1, C_ISW1:C_ISW2 + 1],
                     in_=scal[0:1, C_ISW1:C_ISW2 + 1],
                     func=AF.Exp, scale=-1.0)
        ss.inc(e.copy(out=fsc[0:1, 0:1], in_=fsc[0:1, 1:2]), 1)
    SC(_)

    def bcast_spp(pbc_t):
        def _(e):
            ss.wait(e)
            vv.wait(e)
            ld.wait(e)
            tt.inc(e.matmul(pbc_t[:], onesrow[:], scal[:],
                            start=True, stop=True), 1)
        TE(_)

        def _(e):
            tt.wait(e)
            e.tensor_copy(out=spp[:], in_=pbc_t[:])
            vv.inc(e.tensor_copy(out=fsc[0:1, 0:1], in_=fsc[0:1, 1:2]), 1)
        VE(_)

    bcast_spp(pbc)

    wqb = [sb01(f"wqb{i}", [128, FH], BF16) for i in range(2)]
    wq_ttm = {}
    w1qT = sb01("w1qT", [128, 8 * FH], BF16)
    w2st = sb01("w2st", [128, 4096], BF16)
    w1qT_w = w1qT[:].rearrange("p (a h) -> p a h", a=8)
    w2qT_w = w2qT_h.rearrange("(a p) q -> p a q", p=128)

    def emit_w_quant(gi, src_ap, cols, iswcol, is_w2, ci):
        def _(e):
            if len(w_marks) >= 2:
                e.wait_ge(vv.h, w_marks[-2])
            ld.inc(e.dma_start(out=wld[gi % 2][0:128, 0:cols], in_=src_ap))
        SY(_)

        def _(e):
            ld.wait(e)
            e.wait_ge(tt.h, wq_ttm.get(gi - 2, 0))
            wt = wld[gi % 2][0:128, 0:cols]
            e.tensor_scalar(out=wt, in0=wt, scalar1=spp[:, iswcol:iswcol + 1],
                            scalar2=MAGIC, op0=ALU.mult, op1=ALU.add)
            e.tensor_scalar(out=wt, in0=wt, scalar1=MAGIC, scalar2=-127.0,
                            op0=ALU.subtract, op1=ALU.max)
            vv.inc(e.tensor_scalar(out=wqb[gi % 2][0:128, 0:cols], in0=wt,
                                   scalar1=127.0, scalar2=None, op0=ALU.min), 1)
        VE(_)
        w_marks.append(vv.n)

        for g0 in range(0, cols // 128, 8):
            def _(e, g0=g0):
                vv.wait(e)
                last = None
                for k in range(8):
                    last = e.transpose(
                        xpose[:, k * 128:(k + 1) * 128],
                        wqb[gi % 2][:, (g0 + k) * 128:(g0 + k + 1) * 128],
                        ident_b[:])
                tt.inc(last, 1)
            TE(_)

            if not is_w2:
                def _(e, ci=ci):
                    tt.wait(e)
                    vv.inc(e.tensor_copy(
                        out=w1qT_w[:, :, ci * 128:(ci + 1) * 128],
                        in_=xpose[:].rearrange("p (a q) -> p a q", q=128)), 1)
                VE(_)
            else:
                def _(e, g0=g0):
                    tt.wait(e)
                    if g0 == 0:
                        e.wait_ge(sp.h, sp.n)
                    vv.inc(e.tensor_copy(out=w2st[:, g0 * 128:(g0 + 8) * 128],
                                         in_=xpose[:]), 1)
                VE(_)
        wq_ttm[gi] = tt.n
        if is_w2:
            def _(e, ci=ci):
                vv.wait(e)
                sp.inc(e.dma_start(
                    out=w2qT_w[:, ci * 32:(ci + 1) * 32, :],
                    in_=w2st[:].rearrange("p (a q) -> p a q", q=128)))
            SY(_)

    for ci in range(32):
        emit_w_quant(32 + 8 + ci, w1[ci * 128:(ci + 1) * 128, :], DM,
                     C_ISW1, False, ci)
    for ci in range(8):
        emit_w_quant(32 + 8 + 32 + ci, w2[ci * 128:(ci + 1) * 128, :], FH,
                     C_ISW2, True, ci)

    def _(e):
        gg.wait(e)
        vv.inc(e.tensor_add(scal[0:1, C_SX:C_SX + 1], ccr[0:1, 0:1],
                            ccr[0:1, 1:2]), 1)
        vv.wait(e)
        vv.inc(e.tensor_scalar(out=scal[0:1, C_SX:C_SX + 1],
                               in0=scal[0:1, C_SX:C_SX + 1],
                               scalar1=1.0 / 255.0, scalar2=1e-8,
                               op0=ALU.mult, op1=ALU.max), 1)
    VE(_)

    def _(e):
        vv.wait(e)
        ss.inc(e.activation(out=scal[0:1, C_ISX:C_ISX + 1],
                            in_=scal[0:1, C_SX:C_SX + 1], func=AF.Ln), 1)
        ss.wait(e)
        e.activation(out=scal[0:1, C_ISX:C_ISX + 1],
                     in_=scal[0:1, C_ISX:C_ISX + 1],
                     func=AF.Exp, scale=-1.0)
        ss.inc(e.copy(out=fsc[0:1, 0:1], in_=fsc[0:1, 1:2]), 1)
    SC(_)

    def emit_zx(lo_col, hi_col, s_col, isx_col, stot_col, sw_col):
        def _(e):
            ss.wait(e)

            def step(ins):
                vv.inc(ins, 1)
                vv.wait(e)
            step(e.tensor_mul(scal[0:1, lo_col:lo_col + 1], ccr[0:1, 0:1],
                              scal[0:1, isx_col:isx_col + 1]))
            step(e.tensor_scalar(out=scal[0:1, lo_col:lo_col + 1],
                                 in0=scal[0:1, lo_col:lo_col + 1],
                                 scalar1=MAGIC, scalar2=MAGIC,
                                 op0=ALU.add, op1=ALU.subtract))
            step(e.tensor_scalar(out=scal[0:1, lo_col:lo_col + 1],
                                 in0=scal[0:1, lo_col:lo_col + 1],
                                 scalar1=-128.0, scalar2=-128.0,
                                 op0=ALU.add, op1=ALU.max))
            step(e.tensor_scalar(out=scal[0:1, lo_col:lo_col + 1],
                                 in0=scal[0:1, lo_col:lo_col + 1],
                                 scalar1=127.0, scalar2=None, op0=ALU.min))
            step(e.tensor_scalar(out=scal[0:1, hi_col:hi_col + 1],
                                 in0=scal[0:1, lo_col:lo_col + 1],
                                 scalar1=-1.0, scalar2=127.0,
                                 op0=ALU.mult, op1=ALU.add))
            step(e.tensor_scalar(out=scal[0:1, lo_col:lo_col + 1],
                                 in0=scal[0:1, lo_col:lo_col + 1],
                                 scalar1=-1.0, scalar2=-128.0,
                                 op0=ALU.mult, op1=ALU.add))
            e.tensor_mul(scal[0:1, stot_col:stot_col + 1],
                         scal[0:1, s_col:s_col + 1],
                         scal[0:1, sw_col:sw_col + 1])
            vv.inc(e.tensor_copy(out=fsc[0:1, 0:1], in_=fsc[0:1, 1:2]), 1)
        VE(_)

    emit_zx(C_LOX, C_HIX, C_SX, C_ISX, C_S1T, C_SW1)
    bcast_spp(pbc)

    def _(e):
        vv.wait(e)
        sp.inc(e.dma_start(out=scdbg[0:1, :], in_=scal[0:1, :]))
        sp.inc(e.dma_start(out=scdbg[1:2, 0:8], in_=ccs[0:1, :]))
        sp.inc(e.dma_start(out=scdbg[2:3, 0:8], in_=ccr[0:1, :]))
    SY(_)

    xqT = sb01("xqT", [128, 8 * TH], BF16)
    xqb = [sb01(f"xqb{i}", [128, DM], BF16) for i in range(2)]
    xq_tt = {}
    xqT_w = xqT[:].rearrange("p (a t) -> p a t", a=8)
    ln_marks1 = []
    for tc in range(16):
        if tc % 2 == 0:
            emit_ln_stats_pair(tc, xh, ln_marks1)
        emit_ln_chunk(tc, xh, do_spill=False, marks=ln_marks1)

        def _(e, tc=tc):
            e.wait_ge(tt.h, xq_tt.get(tc - 2, 0))
            e.tensor_scalar(out=xnf[:], in0=xnf[:],
                            scalar1=spp[:, C_ISX:C_ISX + 1], scalar2=MAGIC,
                            op0=ALU.mult, op1=ALU.add)
            e.tensor_scalar(out=xnf[:], in0=xnf[:], scalar1=MAGIC,
                            scalar2=spp[:, C_LOX:C_LOX + 1],
                            op0=ALU.subtract, op1=ALU.max)
            vv.inc(e.tensor_scalar(out=xqb[tc % 2][:], in0=xnf[:],
                                   scalar1=spp[:, C_HIX:C_HIX + 1],
                                   scalar2=None, op0=ALU.min), 1)
        VE(_)

        def _(e, tc=tc):
            vv.wait(e)
            e.wait_ge(vv.h, vv.n)
            last = None
            for dc in range(8):
                last = e.transpose(xpose[:, dc * 128:(dc + 1) * 128],
                                   xqb[tc % 2][:, dc * 128:(dc + 1) * 128],
                                   ident_b[:])
            tt.inc(last, 1)
            xq_tt[tc] = tt.n
        TE(_)

        def _(e, tc=tc):
            tt.wait(e)
            vv.inc(e.tensor_copy(
                out=xqT_w[:, :, tc * 128:(tc + 1) * 128],
                in_=xpose[:].rearrange("p (a q) -> p a q", q=128)), 1)
        VE(_)
        ln_marks1[-1] = vv.n

    hst = sb01("hst", [128, TH], F32)
    m1p = [ps01(f"m1p{i}", [128, 512], F32) for i in range(4)]
    mm1_ready = vv.n
    gelu_mk = {}

    for hc in range(32):
        for tp in range(4):
            g = hc * 4 + tp
            pp = m1p[g % 4]

            def _(e, hc=hc, tp=tp, pp=pp, g=g):
                e.wait_ge(vv.h, mm1_ready)
                e.wait_ge(ss.h, gelu_mk.get(g - 4, 0))
                last = None
                for dc in range(8):
                    last = e.matmul(
                        pp[:],
                        w1qT[:, dc * FH + hc * 128:dc * FH + (hc + 1) * 128],
                        xqT[:, dc * TH + tp * 512:dc * TH + (tp + 1) * 512],
                        start=(dc == 0), stop=(dc == 7))
                pe.inc(last, 1)
            TE(_)

            def _(e, hc=hc, tp=tp, pp=pp, g=g):
                pe.wait(e)
                if tp == 0:
                    e.wait_ge(sp.h, sp.n)
                    e.wait_ge(vv.h, vv.n)
                ss.inc(e.activation(out=hst[:, tp * 512:(tp + 1) * 512],
                                    in_=pp[:], func=AF.Gelu,
                                    scale=spp[:, C_S1T:C_S1T + 1],
                                    bias=bias1[:, hc:hc + 1]), 1)
                gelu_mk[g] = ss.n
            SC(_)

        def _(e, hc=hc):
            ss.wait(e)
            e.tensor_reduce(out=stats[:, 8 + hc:9 + hc], in_=hst[:], axis=AX,
                            op=ALU.max)
            vv.inc(e.tensor_reduce(out=stats[:, 40 + hc:41 + hc], in_=hst[:],
                                   axis=AX, op=ALU.min), 1)
        VE(_)

        def _(e, hc=hc):
            vv.wait(e)
            sp.inc(e.dma_start(out=h_h[hc * 128:(hc + 1) * 128, :], in_=hst[:]))
        SY(_)

    def _(e):
        e.tensor_reduce(out=sc1a[:], in_=stats[:, 8:40], axis=AX, op=ALU.max)
        vv.inc(e.tensor_reduce(out=sc1b[:], in_=stats[:, 40:72], axis=AX,
                               op=ALU.min), 1)
    VE(_)

    def _(e):
        vv.wait(e)
        e.transpose(pmm1[:], sc1a[:], ident_f[:])
        tt.inc(e.transpose(pmm2[:], sc1b[:], ident_f[:]), 1)
    TE(_)

    def _(e):
        tt.wait(e)
        e.tensor_reduce(out=ccs[0:1, 1:2], in_=pmm1[:], axis=AX, op=ALU.max)
        vv.inc(e.tensor_reduce(out=ccs[0:1, 2:3], in_=pmm2[:], axis=AX,
                               op=ALU.min), 1)
        vv.wait(e)
        e.tensor_scalar(out=ccs[0:1, 0:1], in0=ccs[0:1, 2:3],
                        scalar1=-1.0, scalar2=None, op0=ALU.mult)
        vv.inc(e.tensor_copy(out=fsc[0:1, 0:1], in_=fsc[0:1, 1:2]), 1)
    VE(_)

    def _(g):
        vv.wait(g)
        gg.inc(g.dma_start(out=cc2i[:, :], in_=ccs[0:1, :]))
        gg.wait(g)
        c2.inc(g.collective_compute(
            "AllReduce", ALU.max, replica_groups=[list(range(8))],
            ins=[cc2i.opt()], outs=[cc2o.opt()]), 1)
        g.wait_ge(c2.h, 1)
        gg.inc(g.dma_start(out=ccr[0:1, :], in_=cc2o[:, :]))
    GP(_)
    ccr2_mk = gg.n
    h_done_mk = sp.n

    ph01.close()

    ph2 = contextlib.ExitStack()

    def sb2(name, shape, dtype):
        return ph2.enter_context(nc.sbuf_tensor(name, shape, dtype))

    scoreT = sb2("scoreT", [128, 4 * S], BF16)
    qhat = sb2("qhat", [128, S], BF16)
    khat = sb2("khat", [128, S], BF16)
    vtok = sb2("vtok", [128, 32 * 128], BF16)
    vext = sb2("vext", [128, 4 * 68], BF16)
    numer = sb2("numer", [65, S], F32)
    colss = sb2("colss", [128, 8], F32)
    coltl = sb2("coltl", [128, 4], F32)
    rnq = sb2("rnq", [2, TH], F32)
    atst = [sb2(f"atst{i}", [64, 512], BF16) for i in range(2)]

    hstq = [sb2(f"hstq{i}", [128, 1024], F32) for i in range(2)]
    hqtq = sb2("hqtq", [128, 1024], BF16)
    hq_lmk = {}
    hq_qmk = {}
    hq_smk = {}
    hq_state = [0, 0]

    def emit_hq_load(j):
        hc, t2 = j // 2, j % 2

        def _(e, j=j, hc=hc, t2=t2):
            e.wait_ge(sp.h, h_done_mk)
            if j >= 2:
                e.wait_ge(vv.h, hq_qmk[j - 2])
            ld.inc(e.dma_start(out=hstq[j % 2][:],
                               in_=h_h[hc * 128:(hc + 1) * 128,
                                       t2 * 1024:(t2 + 1) * 1024]))
            hq_lmk[j] = ld.n
        SY(_)

    def emit_hq_quant(j):
        def _(e, j=j):
            e.wait_ge(ld.h, hq_lmk[j])
            e.wait_ge(sp.h, hq_smk.get(j - 1, 0))
            hs = hstq[j % 2]
            e.tensor_scalar(out=hs[:], in0=hs[:],
                            scalar1=spp[:, C_ISH:C_ISH + 1], scalar2=MAGIC,
                            op0=ALU.mult, op1=ALU.add)
            e.tensor_scalar(out=hs[:], in0=hs[:], scalar1=MAGIC,
                            scalar2=spp[:, C_LOH:C_LOH + 1],
                            op0=ALU.subtract, op1=ALU.max)
            vv.inc(e.tensor_scalar(out=hqtq[:], in0=hs[:],
                                   scalar1=spp[:, C_HIH:C_HIH + 1],
                                   scalar2=None, op0=ALU.min), 1)
            hq_qmk[j] = vv.n
        VE(_)

    def emit_hq_spill(j):
        hc, t2 = j // 2, j % 2

        def _(e, j=j, hc=hc, t2=t2):
            e.wait_ge(vv.h, hq_qmk[j])
            sp.inc(e.dma_start(out=hq_h[hc * 128:(hc + 1) * 128,
                                        t2 * 1024:(t2 + 1) * 1024],
                               in_=hqtq[:]))
            hq_smk[j] = sp.n
        SY(_)

    def emit_hq_step():
        j, g = hq_state
        if g % 3 == 0 and j < 64:
            emit_hq_load(j)
        elif g % 3 == 2 and j < 64:
            emit_hq_quant(j)
            emit_hq_spill(j)
            hq_state[0] = j + 1
        hq_state[1] = g + 1

    for p in range(4):
        pstart_vv, pstart_ss = vv.n, ss.n
        prj = contextlib.ExitStack()
        xnp = prj.enter_context(nc.sbuf_tensor(f"xnp{p}", [128, 8 * TH], BF16))
        wqkT = prj.enter_context(nc.sbuf_tensor(f"wqkT{p}", [128, 3 * 1024], BF16))
        qf = prj.enter_context(nc.sbuf_tensor(f"qf{p}", [128, TH], F32))
        kf = prj.enter_context(nc.sbuf_tensor(f"kf{p}", [128, TH], F32))
        sqp = prj.enter_context(nc.sbuf_tensor(f"sqp{p}", [128, 512], F32))
        wl32 = prj.enter_context(nc.sbuf_tensor(f"wl32{p}", [128, DM], F32))
        wlb = prj.enter_context(nc.sbuf_tensor(f"wlb{p}", [128, DM], BF16))
        xp2 = prj.enter_context(nc.psum_tensor(f"xp2{p}", [128, 1024], BF16))
        qp = [prj.enter_context(nc.psum_tensor(f"qp{p}{i}", [128, 512], F32))
              for i in range(2)]
        npp = prj.enter_context(nc.psum_tensor(f"npp{p}", [2, 512], F32))
        bcp = prj.enter_context(nc.psum_tensor(f"bcp{p}", [128, 512], F32))
        vp = [prj.enter_context(nc.psum_tensor(f"vp{p}{i}", [128, 512], F32))
              for i in range(2)]

        if p == 1:
            pbch = prj.enter_context(nc.psum_tensor("pbch", [128, 16], F32))

            def _(e):
                e.wait_ge(gg.h, ccr2_mk)
                vv.inc(e.tensor_add(scal[0:1, C_SH:C_SH + 1], ccr[0:1, 0:1],
                                    ccr[0:1, 1:2]), 1)
                vv.wait(e)
                vv.inc(e.tensor_scalar(out=scal[0:1, C_SH:C_SH + 1],
                                       in0=scal[0:1, C_SH:C_SH + 1],
                                       scalar1=1.0 / 255.0, scalar2=1e-8,
                                       op0=ALU.mult, op1=ALU.max), 1)
            VE(_)

            def _(e):
                vv.wait(e)
                ss.inc(e.activation(out=scal[0:1, C_ISH:C_ISH + 1],
                                    in_=scal[0:1, C_SH:C_SH + 1],
                                    func=AF.Ln), 1)
                ss.wait(e)
                e.activation(out=scal[0:1, C_ISH:C_ISH + 1],
                             in_=scal[0:1, C_ISH:C_ISH + 1],
                             func=AF.Exp, scale=-1.0)
                ss.inc(e.copy(out=fsc[0:1, 0:1], in_=fsc[0:1, 1:2]), 1)
            SC(_)

            emit_zx(C_LOH, C_HIH, C_SH, C_ISH, C_S2T, C_SW2)
            bcast_spp(pbch)

        for sec in range(3):
            def _(e, sec=sec, p=p):
                e.wait_ge(vv.h, vv.n)
                e.wait_ge(ss.h, ss.n)
                e.wait_ge(gg.h, gg.n)
                e.wait_ge(pe.h, pe.n)
                ld.inc(e.dma_start(
                    out=wl32[:],
                    in_=qkvw[sec * 512 + p * 128: sec * 512 + (p + 1) * 128, :]))
            SY(_)

            def _(e, sec=sec):
                ld.wait(e)
                e.wait_ge(tt.h, tt.n)
                vv.inc(e.tensor_copy(out=wlb[:], in_=wl32[:]), 1)
            VE(_)

            def _(e, sec=sec):
                vv.wait(e)
                last = None
                for dc in range(8):
                    last = e.transpose(xp2[:, dc * 128:(dc + 1) * 128],
                                       wlb[:, dc * 128:(dc + 1) * 128],
                                       ident_b[:])
                tt.inc(last, 1)
            TE(_)

            def _(e, sec=sec):
                tt.wait(e)
                vv.inc(e.tensor_copy(out=wqkT[:, sec * 1024:(sec + 1) * 1024],
                                     in_=xp2[:]), 1)
            VE(_)

        wq_mk = vv.n
        qp_mk = [pstart_vv, pstart_vv]
        vp_mk = [pstart_vv, pstart_vv]
        xnp_gmk = [0, 0]

        def emit_norm1(srcf):
            for tp in range(4):
                def _(e, tp=tp, srcf=srcf):
                    e.wait_ge(tt.h, tt.n)
                    qs = srcf[:, tp * 512:(tp + 1) * 512]
                    vv.inc(e.scalar_tensor_tensor(out=sqp[:], in0=qs,
                                                  scalar=0.0, in1=qs,
                                                  op0=ALU.add,
                                                  op1=ALU.mult), 1)
                VE(_)

                def _(e, tp=tp):
                    vv.wait(e)
                    tt.inc(e.matmul(npp[:], ones2_s[:], sqp[:],
                                    start=True, stop=True), 1)
                TE(_)

                def _(e, tp=tp):
                    tt.wait(e)
                    vv.inc(e.tensor_copy(
                        out=rnq[:, tp * 512:(tp + 1) * 512],
                        in_=npp[:]), 1)
                VE(_)

        def emit_norm2(srcf, dstt, t0c):
            def _(e):
                vv.inc(e.tensor_scalar(
                    out=rnq[:], in0=rnq[:],
                    scalar1=1e-24, scalar2=None, op0=ALU.max), 1)
            VE(_)

            def _(e):
                vv.wait(e)
                ss.inc(e.activation(out=rnq[:], in_=rnq[:], func=AF.Ln), 1)
                ss.wait(e)
                e.activation(out=rnq[:], in_=rnq[:], func=AF.Exp,
                             scale=-0.5)
                ss.inc(e.copy(out=fsc[0:1, 0:1], in_=fsc[0:1, 1:2]), 1)
            SC(_)

            for tp in range(4):
                def _(e, tp=tp):
                    ss.wait(e)
                    e.wait_ge(vv.h, vv.n)
                    tt.inc(e.matmul(
                        bcp[:], ones2t_s[:],
                        rnq[:, tp * 512:(tp + 1) * 512],
                        start=True, stop=True), 1)
                TE(_)

                def _(e, tp=tp, dstt=dstt, srcf=srcf, t0c=t0c):
                    tt.wait(e)
                    vv.inc(e.tensor_mul(
                        dstt[:, t0c + tp * 512:t0c + (tp + 1) * 512],
                        srcf[:, tp * 512:(tp + 1) * 512], bcp[:]), 1)
                VE(_)

        for th in range(2):
            t0c = th * TH
            for dc in range(4):
                def _(e, dc=dc, th=th):
                    sp.wait(e)
                    if dc == 0 and th == 1:
                        e.wait_ge(pe.h, pe.n)
                    ld.inc(e.dma_start(
                        out=xnp[:, dc * TH:(dc + 1) * TH],
                        in_=xnT_h[dc * 128:(dc + 1) * 128,
                                  th * TH:(th + 1) * TH]))
                SY(_)
            xnp_lmk = ld.n

            def _(g, th=th, pen=pe.n, pvv=pstart_vv, pss=pstart_ss):
                g.wait_ge(vv.h, pvv)
                g.wait_ge(ss.h, pss)
                g.wait_ge(pe.h, pen)
                for dc in range(4, 8):
                    gg.inc(g.dma_start(
                        out=xnp[:, dc * TH:(dc + 1) * TH],
                        in_=xnT_h[dc * 128:(dc + 1) * 128,
                                  th * TH:(th + 1) * TH]))
            GP(_)
            xnp_gmk[th] = gg.n

            for sec, dstf in ((0, qf), (1, kf)):
                for tp in range(4):
                    pp = qp[tp % 2]

                    def _(e, sec=sec, tp=tp, pp=pp, th=th,
                          pmk=qp_mk[tp % 2], xl=xnp_lmk):
                        e.wait_ge(ld.h, xl)
                        e.wait_ge(gg.h, xnp_gmk[th])
                        e.wait_ge(vv.h, max(pmk, wq_mk))
                        last = None
                        for dc in range(8):
                            last = e.matmul(
                                pp[:],
                                wqkT[:, sec * 1024 + dc * 128:
                                     sec * 1024 + (dc + 1) * 128],
                                xnp[:, dc * TH + tp * 512:dc * TH + (tp + 1) * 512],
                                start=(dc == 0), stop=(dc == 7))
                        pe.inc(last, 1)
                    TE(_)

                    def _(e, sec=sec, tp=tp, pp=pp, p=p, dstf=dstf):
                        pe.wait(e)
                        vv.inc(e.tensor_scalar(
                            out=dstf[:, tp * 512:(tp + 1) * 512], in0=pp[:],
                            scalar1=biasqkv[:, sec * 4 + p:sec * 4 + p + 1],
                            scalar2=None, op0=ALU.add), 1)
                    VE(_)
                    qp_mk[tp % 2] = vv.n

            emit_norm1(qf)

            for g4 in range(4):
                pp = vp[g4 % 2]

                def _(e, g4=g4, pp=pp, p=p, th=th,
                      pmk=vp_mk[g4 % 2], xl=xnp_lmk):
                    e.wait_ge(vv.h, pmk)
                    e.wait_ge(ld.h, xl)
                    e.wait_ge(gg.h, xnp_gmk[th])
                    last = None
                    for k in range(4):
                        tcl = g4 * 4 + k
                        for dc in range(8):
                            e.matmul(pp[:, k * 128:(k + 1) * 128],
                                     xnp[:, dc * TH + tcl * 128:
                                         dc * TH + (tcl + 1) * 128],
                                     wqkT[:, 2 * 1024 + dc * 128:
                                          2 * 1024 + (dc + 1) * 128],
                                     start=(dc == 0), stop=False)
                        last = e.matmul(pp[:, k * 128:(k + 1) * 128],
                                        onesrow_b[:],
                                        bvrow[0:1, p * 128:(p + 1) * 128],
                                        start=False, stop=True)
                    pe.inc(last, 1)
                TE(_)

                def _(e, g4=g4, pp=pp, th=th):
                    pe.wait(e)
                    vv.inc(e.tensor_copy(
                        out=vtok[:, th * 2048 + g4 * 512:
                                 th * 2048 + (g4 + 1) * 512],
                        in_=pp[:]), 1)
                VE(_)
                vp_mk[g4 % 2] = vv.n

            emit_norm2(qf, qhat, t0c)
            emit_norm1(kf)
            emit_norm2(kf, khat, t0c)

        prj.close()

        if qkdbg is not None and p == 0:
            def _(e):
                vv.wait(e)
                sp.inc(e.dma_start(out=qkdbg[0:128, :], in_=qhat[:]))
                sp.inc(e.dma_start(out=qkdbg[128:256, :], in_=khat[:]))
            SY(_)

        stp = contextlib.ExitStack()
        chA = [stp.enter_context(nc.sbuf_tensor(f"chA{p}{s}", [128, 4096], F32))
               for s in range(2)]
        chB = [stp.enter_context(nc.sbuf_tensor(f"chB{p}{s}", [128, 4096], F32))
               for s in range(2)]
        chC = [stp.enter_context(nc.sbuf_tensor(f"chC{p}{s}", [128, 4096], F32))
               for s in range(2)]
        cosp = [stp.enter_context(nc.psum_tensor(f"cosp{p}{s}", [128, 1024], F32))
                for s in range(2)]
        pvp = stp.enter_context(nc.psum_tensor(f"pvp{p}", [65, 2048], F32))

        clampmk = {}
        mmk = {}
        act1mk = {}
        xmk = {}
        act2mk = {}
        vextmk = {}
        evacmk = {}
        pvmk = {}
        lastpv = pe.n
        pvp_free = vv.n
        lastcolread = vv.n
        lastbcast = tt.n
        lastdenom = ss.n
        proj_ready = vv.n

        NCH = 64

        def emit_cos(n):
            stx_, j4_ = (n % 32) // 4, n % 4
            q0_ = 64 * (n // 32)
            s = n % 2
            for r in range(4):
                def _(e, n=n, r=r, stx=stx_, j4=j4_, q0=q0_):
                    prev = (n, r - 2) if r >= 2 else (n - 1, r + 2)
                    e.wait_ge(vv.h, max(clampmk.get(prev, 0), proj_ready))
                    last = None
                    for h2 in range(2):
                        i0_ = r * 1024 + h2 * 512
                        last = e.matmul(
                            cosp[r % 2][:, h2 * 512:(h2 + 1) * 512],
                            khat[q0:q0 + 64,
                                 stx * 512 + j4 * 128:stx * 512 + (j4 + 1) * 128],
                            qhat[q0:q0 + 64, i0_:i0_ + 512],
                            start=True, stop=True)
                    pe.inc(last, 1)
                    pvmk[("cos", n, r)] = pe.n
                TE(_)

                def _(e, n=n, r=r, s=s):
                    e.wait_ge(pe.h, pvmk[("cos", n, r)])
                    e.wait_ge(gg.h, xmk.get(n - 2, 0))
                    vv.inc(e.tensor_scalar(
                        out=chA[s][:, r * 1024:(r + 1) * 1024],
                        in0=cosp[r % 2][:],
                        scalar1=-(1.0 - EPS), scalar2=(1.0 - EPS),
                        op0=ALU.max, op1=ALU.min), 1)
                    clampmk[(n, r)] = vv.n
                VE(_)

        def emit_m(n):
            s = n % 2

            def _(g, n=n, s=s):
                g.wait_ge(vv.h, clampmk[(n, 3)])
                gg.inc(g.tensor_tensor(out=chB[s][:], in0=chA[s][:],
                                       in1=chA[s][:], op=ALU.mult), 1)
                mmk[n] = gg.n
            GP(_)

        def emit_act1(n):
            s = n % 2

            def _(e, n=n, s=s):
                e.wait_ge(gg.h, mmk[n])
                e.activation(out=chB[s][:], in_=chB[s][:], func=AF.Ln,
                             scale=-1.0, bias=1.0)
                ss.inc(e.activation(out=chB[s][:], in_=chB[s][:],
                                    func=AF.Sigmoid, scale=-0.5), 1)
                act1mk[n] = ss.n
            SC(_)

        def emit_x(n):
            s = n % 2

            def _(g, n=n, s=s):
                g.wait_ge(ss.h, max(act1mk[n], act2mk.get(n - 2, 0)))
                gg.inc(g.tensor_tensor(out=chC[s][:], in0=chA[s][:],
                                       in1=chB[s][:], op=ALU.mult), 1)
                xmk[n] = gg.n
            GP(_)

        def emit_act2(n):
            s = n % 2
            j4_ = n % 4

            def _(e, n=n, s=s, j4=j4_, lpv=None):
                e.wait_ge(gg.h, xmk[n])
                e.wait_ge(vv.h, lastcolread)
                e.wait_ge(pe.h, lastpv)
                e.activation(out=chC[s][:], in_=chC[s][:], func=AF.Arctan)
                e.activation(out=chC[s][:], in_=chC[s][:], func=AF.Ln,
                             scale=-2.0, bias=KLN)
                e.activation(out=scoreT[:, j4 * S:(j4 + 1) * S],
                             in_=chC[s][:], func=AF.Exp, scale=-65.0,
                             accum_out=colss[:, j4:j4 + 1])
                ss.inc(e.copy(out=fsc[0:1, 0:1], in_=fsc[0:1, 1:2]), 1)
                act2mk[n] = ss.n
            SC(_)

        def emit_tail(st_g):
            nonlocal_vars = None
            stx_ = st_g % 8
            q0_ = 64 * (st_g // 8)

            def _(e, st_g=st_g):
                e.wait_ge(ss.h, act2mk[st_g * 4 + 3])
                vv.inc(e.tensor_scalar(out=colss[:, 4:8], in0=colss[:, 0:4],
                                       scalar1=1e-25, scalar2=None,
                                       op0=ALU.max), 1)
            VE(_)
            ret_lastcolread = vv.n

            def _(e, st_g=st_g):
                vv.wait(e)
                ss.inc(e.activation(out=colss[:, 4:8], in_=colss[:, 4:8],
                                    func=AF.Ln, scale=float(2.0 ** 40)), 1)
                ss.wait(e)
                e.activation(out=coltl[:], in_=colss[:, 4:8],
                             func=AF.Exp, scale=-0.5,
                             bias=13.862943611198906)
                ss.inc(e.copy(out=fsc[0:1, 0:1], in_=fsc[0:1, 1:2]), 1)
            SC(_)
            coltl_mk = ss.n

            def _(e, stx=stx_, q0=q0_, coltl_mk=coltl_mk):
                e.wait_ge(ss.h, coltl_mk)
                e.wait_ge(pe.h, lastpv)
                for j4 in range(4):
                    e.tensor_copy(out=vext[:, j4 * 68 + 64:j4 * 68 + 65],
                                  in_=coltl[:, j4:j4 + 1])
                for j4 in range(4):
                    tc = stx * 4 + j4
                    e.tensor_scalar(
                        out=vext[:, j4 * 68:j4 * 68 + 64],
                        in0=vtok[:, tc * 128 + q0:tc * 128 + q0 + 64],
                        scalar1=coltl[:, j4:j4 + 1], scalar2=None,
                        op0=ALU.mult)
                vv.inc(e.tensor_copy(out=fsc[0:1, 0:1], in_=fsc[0:1, 1:2]), 1)
            VE(_)
            vextmk[st_g] = vv.n
            return ret_lastcolread

        def emit_pv(st_g):
            nonlocal pvp_free
            stx_ = st_g % 8
            for ipc in range(2):
                def _(e, ipc=ipc, st_g=st_g, pf=pvp_free):
                    e.wait_ge(vv.h, max(vextmk[st_g], pf))
                    e.wait_ge(ss.h, act2mk[st_g * 4 + 3])
                    last = None
                    for j4 in range(4):
                        for n5 in range(4):
                            c0 = j4 * S + ipc * 2048 + n5 * 512
                            last = e.matmul(
                                pvp[:, n5 * 512:(n5 + 1) * 512],
                                vext[:, j4 * 68:j4 * 68 + 65],
                                scoreT[:, c0:c0 + 512],
                                start=(j4 == 0), stop=(j4 == 3))
                    pe.inc(last, 1)
                    pvmk[(st_g, ipc)] = pe.n
                TE(_)

                def _(e, ipc=ipc, st_g=st_g, stx=stx_, lb=lastbcast, ld_=lastdenom):
                    e.wait_ge(pe.h, pvmk[(st_g, ipc)])
                    if stx == 0:
                        e.wait_ge(tt.h, lb)
                        e.wait_ge(ss.h, ld_)
                    dst = numer[:, ipc * 2048:(ipc + 1) * 2048]
                    if stx == 0:
                        ins = e.tensor_copy(out=dst, in_=pvp[:])
                    else:
                        ins = e.tensor_add(dst, dst, pvp[:])
                    vv.inc(ins, 1)
                    evacmk[(st_g, ipc)] = vv.n
                VE(_)
                pvp_free = vv.n

        def emit_epilogue(hh, p):
            nonlocal pvp_free, lastdenom, lastbcast

            def _(e, hh=hh):
                e.wait_ge(vv.h, max(evacmk[(hh * 8 + 7, 0)],
                                    evacmk[(hh * 8 + 7, 1)]))
                ss.inc(e.activation(out=numer[64:65, :], in_=numer[64:65, :],
                                    func=AF.Ln), 1)
                ss.wait(e)
                e.activation(out=numer[64:65, :], in_=numer[64:65, :],
                             func=AF.Exp, scale=-1.0)
                ss.inc(e.copy(out=fsc[0:1, 0:1], in_=fsc[0:1, 1:2]), 1)
            SC(_)
            lastdenom = ss.n

            for ip in range(8):
                def _(e, ip=ip, pf=pvp_free, hh=hh):
                    e.wait_ge(ss.h, lastdenom)
                    e.wait_ge(vv.h, max(pf, evacmk[(hh * 8 + 7, 0)],
                                        evacmk[(hh * 8 + 7, 1)]))
                    tt.inc(e.matmul(pvp[0:64, 0:512], onesr64[64:65, :],
                                    numer[64:65, ip * 512:(ip + 1) * 512],
                                    start=True, stop=True), 1)
                TE(_)
                bc_mk = tt.n

                def _(e, ip=ip, bc_mk=bc_mk):
                    e.wait_ge(tt.h, bc_mk)
                    if ip >= 2:
                        e.wait_ge(sp.h, sp.n)
                    e.tensor_mul(atst[ip % 2][:],
                                 numer[0:64, ip * 512:(ip + 1) * 512],
                                 pvp[0:64, 0:512])
                    vv.inc(e.tensor_copy(out=fsc[0:1, 0:1],
                                         in_=fsc[0:1, 1:2]), 1)
                VE(_)
                pvp_free = vv.n

                def _(e, ip=ip, p=p, hh=hh):
                    vv.wait(e)
                    r0 = (p * 2 + hh) * 64
                    sp.inc(e.dma_start(
                        out=am_in[r0:r0 + 64, ip * 512:(ip + 1) * 512],
                        in_=atst[ip % 2][:]))
                    if amdbg is not None:
                        sp.inc(e.dma_start(
                            out=amdbg[r0:r0 + 64, ip * 512:(ip + 1) * 512],
                            in_=atst[ip % 2][:]))
                SY(_)
            lastbcast = tt.n

        emit_cos(0); emit_m(0)
        emit_cos(1); emit_m(1)
        emit_act1(0)
        for n in range(NCH):
            if n + 1 < NCH:
                emit_act1(n + 1)
            emit_x(n)
            emit_act2(n)
            if n + 2 < NCH:
                emit_cos(n + 2)
                emit_m(n + 2)
            if n % 4 == 3:
                st_g = n // 4
                lastcolread = emit_tail(st_g)
                emit_pv(st_g)
                lastpv = pe.n
                if st_g % 8 == 7:
                    emit_epilogue(st_g // 8, p)
            if p >= 1:
                emit_hq_step()

        stp.close()

    ph2.close()

    def _(g):
        sp.wait(g)
        c3.inc(g.collective_compute(
            "AllGather", ALU.bypass,
            replica_groups=[[0, 1], [2, 3], [4, 5], [6, 7]],
            ins=[am_in.opt()], outs=[am_out.opt()]), 1)
        g.wait_ge(c3.h, 1)
        gg.inc(g.engine_nop(), 1)
    GP(_)
    ag_mk = gg.n

    ph46 = contextlib.ExitStack()

    def sb46(name, shape, dtype):
        return ph46.enter_context(nc.sbuf_tensor(name, shape, dtype))

    def ps46(name, shape, dtype=F32):
        return ph46.enter_context(nc.psum_tensor(name, shape, dtype))

    outacc = sb46("outacc", [128, 8 * TH], F32)
    outwT = sb46("outwT", [128, 8 * 1024], BF16)
    biassum = sb46("biassum", [128, 8], F32)
    fps = ps46("fps", [128, 1024], F32)
    m2p = [ps46(f"m2p{i}", [128, 512]) for i in range(2)]
    outwT_w = outwT[:].rearrange("p (a o) -> p a o", a=8)

    def _(e):
        e.wait_ge(sp.h, sp.n)
        vv.inc(e.tensor_add(biassum[:], biaso[:], bias2[:]), 1)
        vv.wait(e)
        vv.inc(e.tensor_copy(out=fsc[0:1, 0:1], in_=fsc[0:1, 1:2]), 1)
    VE(_)

    ph4a = contextlib.ExitStack()
    wl4 = ph4a.enter_context(nc.sbuf_tensor("wl4", [128, DM], F32))
    wlb4 = ph4a.enter_context(nc.sbuf_tensor("wlb4", [128, DM], BF16))
    xp4 = ph4a.enter_context(nc.psum_tensor("xp4", [128, 1024], BF16))

    for oc in range(8):
        def _(e, oc=oc):
            e.wait_ge(vv.h, vv.n)
            if oc == 0:
                e.wait_ge(sp.h, sp.n)
            ld.inc(e.dma_start(out=wl4[:], in_=outw[oc * 128:(oc + 1) * 128, :]))
        SY(_)

        def _(e, oc=oc):
            ld.wait(e)
            e.wait_ge(tt.h, tt.n)
            vv.inc(e.tensor_copy(out=wlb4[:], in_=wl4[:]), 1)
        VE(_)

        def _(e, oc=oc):
            vv.wait(e)
            last = None
            for dc in range(8):
                last = e.transpose(xp4[:, dc * 128:(dc + 1) * 128],
                                   wlb4[:, dc * 128:(dc + 1) * 128], ident_b[:])
            tt.inc(last, 1)
        TE(_)

        def _(e, oc=oc):
            tt.wait(e)
            vv.inc(e.tensor_copy(
                out=outwT_w[:, :, oc * 128:(oc + 1) * 128],
                in_=xp4[:].rearrange("p (a q) -> p a q", q=128)), 1)
        VE(_)

    ph4a.close()

    w2s_r = w2qT_h.rearrange("(a p) q -> p a q", p=128)

    ph5b = contextlib.ExitStack()
    w2all = ph5b.enter_context(nc.sbuf_tensor("w2all", [128, 256 * 128], BF16))
    hqc = ph5b.enter_context(nc.sbuf_tensor("hqc", [128, 32 * 512], BF16))
    w2all_w = w2all[:].rearrange("p (a q) -> p a q", q=128)

    def _(e):
        e.wait_ge(vv.h, vv.n)
        e.wait_ge(pe.h, pe.n)
        sp.wait(e)
        for g in range(4):
            ld.inc(e.dma_start(out=w2all_w[:, g * 64:(g + 1) * 64, :],
                               in_=w2s_r[:, g * 64:(g + 1) * 64, :]))
    SY(_)

    hm1 = {}
    hm2 = {}
    for tp in range(4):
        for hf in range(2):
            def _(e, tp=tp, hf=hf):
                if tp == 0:
                    sp.wait(e)
                else:
                    mk = hm1 if hf == 0 else hm2
                    e.wait_ge(pe.h, mk[(tp - 1, 7)])
                for hc in range(hf * 16, hf * 16 + 16):
                    ld.inc(e.dma_start(
                        out=hqc[:, hc * 512:(hc + 1) * 512],
                        in_=hq_h[hc * 128:(hc + 1) * 128,
                                 tp * 512:(tp + 1) * 512]))
            SY(_)
            if hf == 0:
                hq_ld1 = ld.n
            else:
                hq_ld2 = ld.n

        for oc in range(8):
            g2 = tp * 8 + oc
            pp = m2p[oc % 2]
            if g2 == 0:
                m2evac_mk = {}
                mm2_init = vv.n

            def _(e, tp=tp, oc=oc, pp=pp, l1=hq_ld1, l2=hq_ld2, g2=g2):
                e.wait_ge(ld.h, l1)
                e.wait_ge(vv.h, m2evac_mk.get(g2 - 2, mm2_init))
                last = None
                for hc in range(16):
                    last = e.matmul(
                        pp[:], w2all[:, (oc * 32 + hc) * 128:
                                     (oc * 32 + hc + 1) * 128],
                        hqc[:, hc * 512:(hc + 1) * 512],
                        start=(hc == 0), stop=False)
                pe.inc(last, 1)
                hm1[(tp, oc)] = pe.n
                e.wait_ge(ld.h, l2)
                for hc in range(16, 32):
                    last = e.matmul(
                        pp[:], w2all[:, (oc * 32 + hc) * 128:
                                     (oc * 32 + hc + 1) * 128],
                        hqc[:, hc * 512:(hc + 1) * 512],
                        start=False, stop=(hc == 31))
                pe.inc(last, 1)
                hm2[(tp, oc)] = pe.n
            TE(_)

            def _(e, tp=tp, oc=oc, pp=pp):
                pe.wait(e)
                dst = outacc[:, oc * TH + tp * 512:oc * TH + (tp + 1) * 512]
                vv.inc(e.tensor_scalar(
                    out=dst, in0=pp[:], scalar1=spp[:, C_S2T:C_S2T + 1],
                    scalar2=biassum[:, oc:oc + 1],
                    op0=ALU.mult, op1=ALU.add), 1)
            VE(_)
            m2evac_mk[g2] = vv.n

    ph5b.close()

    ph4b = contextlib.ExitStack()
    rhsc = ph4b.enter_context(nc.sbuf_tensor("rhsc", [128, 8 * TH], BF16))
    gthr = [ph4b.enter_context(nc.sbuf_tensor(f"gthr{i}", [128, S], BF16))
            for i in range(2)]
    xin6 = [ph4b.enter_context(nc.sbuf_tensor(f"xin6{i}", [128, DM], F32))
            for i in range(2)]
    ybuf = [ph4b.enter_context(nc.sbuf_tensor(f"ybuf{i}", [128, DM], F32))
            for i in range(2)]

    for hc in range(8):
        def _(e, hc=hc):
            e.wait_ge(gg.h, ag_mk)
            if hc >= 2:
                e.wait_ge(vv.h, vv.n)
            else:
                e.wait_ge(pe.h, pe.n)
            ld.inc(e.dma_start(out=gthr[hc % 2][:],
                               in_=am_out[hc * 128:(hc + 1) * 128, :]))
        SY(_)

        def _(e, hc=hc):
            ld.wait(e)
            gt = gthr[hc % 2]
            e.tensor_scalar(out=gt[:, TH:S], in0=gt[:, TH:S],
                            scalar1=spp[:, C_BE:C_BE + 1], scalar2=None,
                            op0=ALU.mult)
            vv.inc(e.scalar_tensor_tensor(
                out=rhsc[:, hc * TH:(hc + 1) * TH], in0=gt[:, 0:TH],
                scalar=spp[:, C_AL:C_AL + 1], in1=gt[:, TH:S],
                op0=ALU.mult, op1=ALU.add), 1)
        VE(_)

    combine_all = vv.n
    opevac_mk = {}
    for tp in range(4):
        for oc in range(8):
            g3 = tp * 8 + oc
            pp = m2p[oc % 2]

            def _(e, oc=oc, tp=tp, pp=pp, g3=g3):
                e.wait_ge(vv.h, max(combine_all, opevac_mk.get(g3 - 2, 0)))
                last = None
                for hc in range(8):
                    last = e.matmul(
                        pp[:],
                        outwT[:, hc * 1024 + oc * 128:hc * 1024 + (oc + 1) * 128],
                        rhsc[:, hc * TH + tp * 512:hc * TH + (tp + 1) * 512],
                        start=(hc == 0), stop=(hc == 7))
                pe.inc(last, 1)
            TE(_)

            def _(e, oc=oc, tp=tp, pp=pp, g3=g3):
                pe.wait(e)
                dst = outacc[:, oc * TH + tp * 512:oc * TH + (tp + 1) * 512]
                vv.inc(e.scalar_tensor_tensor(
                    out=dst, in0=pp[:], scalar=1.0,
                    in1=dst, op0=ALU.mult, op1=ALU.add), 1)
            VE(_)
            opevac_mk[g3] = vv.n

        for tc in range(tp * 4, tp * 4 + 4):
            def _(e, tc=tc):
                e.wait_ge(vv.h, vv.n)
                ld.inc(e.dma_start(out=xin6[tc % 2][:],
                                   in_=xh[tc * 128:(tc + 1) * 128, :]))
            SY(_)

            def _(e, tc=tc):
                vv.wait(e)
                last = None
                for oc in range(8):
                    last = e.transpose(
                        fps[:, oc * 128:(oc + 1) * 128],
                        outacc[:, oc * TH + tc * 128:oc * TH + (tc + 1) * 128],
                        ident_f[:])
                tt.inc(last, 1)
            TE(_)

            def _(e, tc=tc):
                tt.wait(e)
                ld.wait(e)
                e.wait_ge(yy.h, yy.n)
                vv.inc(e.scalar_tensor_tensor(out=ybuf[tc % 2][:], in0=fps[:],
                                              scalar=1.0, in1=xin6[tc % 2][:],
                                              op0=ALU.mult, op1=ALU.add), 1)
            VE(_)

            def _(e, tc=tc):
                vv.wait(e)
                yy.inc(e.dma_start(out=y[tc * 128:(tc + 1) * 128, :],
                                   in_=ybuf[tc % 2][:]))
            SY(_)

    ph4b.close()
    ph46.close()

    def _(g):
        for s in sems.values():
            g.wait_ge(s.h, s.n)
    GP(_)

    st.close()
    return nc


def kernel(L0_attn_x, L0_attn_qkv_w, L0_attn_qkv_b, L0_attn_out_w, L0_attn_out_b,
           L0_ffn_W1, L0_ffn_b1, L0_ffn_W2, L0_ffn_b2):
    x = np.ascontiguousarray(np.asarray(L0_attn_x, dtype=np.float32))
    qkv_w = np.ascontiguousarray(np.asarray(L0_attn_qkv_w, dtype=np.float32))
    qkv_b = np.asarray(L0_attn_qkv_b, dtype=np.float32)
    out_w = np.ascontiguousarray(np.asarray(L0_attn_out_w, dtype=np.float32))
    out_b = np.asarray(L0_attn_out_b, dtype=np.float32)
    W1 = np.ascontiguousarray(np.asarray(L0_ffn_W1, dtype=np.float32))
    b1 = np.asarray(L0_ffn_b1, dtype=np.float32)
    W2 = np.ascontiguousarray(np.asarray(L0_ffn_W2, dtype=np.float32))
    b2 = np.asarray(L0_ffn_b2, dtype=np.float32)

    nc = build_program()
    eye_b = np.eye(128, dtype=BF)
    eye_f = np.eye(128, dtype=np.float32)
    o2 = np.zeros((128, 2), np.float32)
    o2[0:64, 0] = 1.0
    o2[64:128, 1] = 1.0
    b1h = np.ascontiguousarray(b1.reshape(32, 128).T)
    b2h = np.ascontiguousarray(b2.reshape(8, 128).T)
    obh = np.ascontiguousarray(out_b.reshape(8, 128).T)
    in_maps = []
    for c in range(8):
        b, half = c // 2, c % 2
        qw = np.ascontiguousarray(np.concatenate(
            [qkv_w[s * DM + half * 512: s * DM + half * 512 + 512, :]
             for s in range(3)], axis=0))
        qb = np.stack([qkv_b[s * DM + half * 512 + p * 128:
                             s * DM + half * 512 + (p + 1) * 128]
                       for s in range(3) for p in range(4)], axis=1)
        qbv = qkv_b[2 * DM + half * 512: 2 * DM + half * 512 + 512].reshape(1, 512)
        hs = np.array([[1.0, 0.0]] if half == 0 else [[0.0, 1.0]], np.float32)
        in_maps.append({
            "xb": np.ascontiguousarray(x[b].reshape(S, DM)),
            "xh": np.ascontiguousarray(
                x[b, half * TH:(half + 1) * TH].reshape(TH, DM)),
            "qkvw": qw,
            "qkvb": np.ascontiguousarray(qb.astype(np.float32)),
            "qkvbv": np.ascontiguousarray(qbv.astype(np.float32)),
            "outw": out_w,
            "outb": obh,
            "w1": W1, "b1": b1h, "w2": W2, "b2": b2h,
            "idbf": eye_b, "idf32": eye_f,
            "ones2": o2, "ones2t": np.ascontiguousarray(o2.T),
            "hsel": hs,
        })
    res = run_bass_kernel_spmd(nc, in_maps, core_ids=list(range(8)))
    out = np.empty((4, S, DM), dtype=np.float32)
    for c in range(8):
        b, half = c // 2, c % 2
        out[b, half * TH:(half + 1) * TH] = res.results[c]["y"]
    return out
